# version 13
# baseline (speedup 1.0000x reference)
import numpy as np
import ml_dtypes

import concourse.bass as bass
import concourse.mybir as mybir
from concourse.bass_utils import run_bass_kernel_spmd

F32 = mybir.dt.float32
BF16 = mybir.dt.bfloat16
I8 = mybir.dt.int8
AF = mybir.ActivationFunctionType
ALU = mybir.AluOpType
AX = mybir.AxisListType
AP = bass.AP

NCORES = 8
D = 2048
H = 16
CK = 128
NTOK = 8192
TLOC = NTOK // NCORES
NCH = TLOC // CK
NKC = D // 128
NG = 2
GH = H // NG
BASEF = 10000.0
LN_EPS = 1e-5
RMS_EPS = 1e-6

LOG_LAM = [-(2.0 ** (-0.5 * (h + 1))) for h in range(H)]
LAMC = [float(np.exp(np.float64(LOG_LAM[h]) * CK)) for h in range(H)]

DEBUG_TAPS = ()
_LAST_RES = None


class Bld:
    ENG = ("tensor", "vector", "scalar", "gpsimd", "sync")

    def __init__(self, nc):
        self.nc = nc
        self.prog = {e: [] for e in self.ENG}
        self.cnt = {k: 0 for k in ("T", "V", "S", "G", "DG", "DS", "DA", "CC")}
        self.sems = {}
        self.waited = {e: {} for e in self.ENG}
        self.gate = []

    def _step(self, eng, sem, inc, waits, fn):
        w2 = []
        wd = self.waited[eng]
        for tok in list(waits) + list(self.gate):
            if tok is None:
                continue
            s, v = tok
            if v <= 0 or wd.get(s, 0) >= v:
                continue
            wd[s] = v
            w2.append((s, v))
        self.cnt[sem] += inc
        token = (sem, self.cnt[sem])

        def run(e):
            for s, v in w2:
                e.wait_ge(self.sems[s], v)
            last = fn(e)
            if last is not None:
                last.then_inc(self.sems[sem], inc)

        self.prog[eng].append(run)
        return token

    def t(self, fn, waits=()):
        return self._step("tensor", "T", 1, waits, fn)

    def v(self, fn, waits=()):
        return self._step("vector", "V", 1, waits, fn)

    def s(self, fn, waits=()):
        return self._step("scalar", "S", 1, waits, fn)

    def g(self, fn, waits=()):
        return self._step("gpsimd", "G", 1, waits, fn)

    def dg(self, fn, waits=()):
        return self._step("gpsimd", "DG", 16, waits, fn)

    def ds(self, fn, waits=()):
        return self._step("sync", "DS", 16, waits, fn)

    def da(self, fn, waits=()):
        return self._step("scalar", "DA", 16, waits, fn)

    def cc(self, fn, waits=()):
        return self._step("gpsimd", "CC", 1, waits, fn)

    def bar(self):
        return [(k, self.cnt[k]) for k in self.cnt]


def build_program():
    nc = bass.Bass("TRN2", target_bir_lowering=False)
    b = Bld(nc)

    def din(name, shape, dt=F32):
        return nc.dram_tensor(name, shape, dt, kind="ExternalInput")

    x_s = din("x_s", [TLOC, D])
    wq = din("wq", [D, D])
    wk = din("wk", [D, D])
    wv = din("wv", [D, D])
    wo = din("wo", [D, D])
    wg1 = din("wg1", [D, 128])
    wg2_i = din("wg2_i", [128, D])
    nw_t = din("nw_t", [128, 16])
    fb = din("fb", [1, D])
    c_t = din("c_t", [128, 16])
    wmod_s = din("wmod_s", [D, 1536])
    ffnw_s = din("ffnw_s", [256, D])
    cosx = din("cosx", [128, NCH * 64])
    sinx = din("sinx", [128, NCH * 64])
    dt_c = din("dt_c", [128, H * 128], BF16)
    dq_c = din("dq_c", [128, H * 128], BF16)
    dk_c = din("dk_c", [128, H])
    coef = din("coef", [128, 7 * H])
    idb = din("idb", [128, 128], BF16)
    idf = din("idf", [128, 128])
    gmask = din("gmask", [1, 1536])

    out_s = nc.dram_tensor("out_s", [TLOC, D], F32, kind="ExternalOutput")

    agA_in = nc.dram_tensor("agA_in", [1, 1792], F32)
    agA_out = nc.dram_tensor("agA_out", [NCORES, 1792], F32)
    agB_in = nc.dram_tensor("agB_in", [D, 256], BF16)
    agB_out = nc.dram_tensor("agB_out", [NCORES * D, 256], BF16)
    agL_in = [nc.dram_tensor(f"agL{g}_in", [128, GH * 128], F32) for g in range(NG)]
    agL_out = [nc.dram_tensor(f"agL{g}_out", [NCORES * 128, GH * 128], F32)
               for g in range(NG)]

    KB = 1024
    off_B = 19 * KB
    off_C = 44 * KB
    off_D = 76 * KB
    off_E = 108 * KB
    off_F = 172 * KB
    off_G = 178 * KB
    off_H = 184 * KB
    LIM = 204 * KB

    SB_BASE = (nc.sbuf_base + 31) // 32 * 32
    assert SB_BASE + LIM <= nc.sbuf_top, (SB_BASE, nc.sbuf_top)

    def sb_at(name, shape, dt, off):
        return nc.alloc_sbuf_tensor_at(name, shape, dt, offset=SB_BASE + off)

    cur = [0]

    def sbA(name, shape, dt):
        t = sb_at(name, shape, dt, cur[0])
        esz = {F32: 4, BF16: 2, I8: 1}[dt]
        cur[0] += (int(np.prod(shape[1:])) * esz + 31) // 32 * 32
        return t

    s_idb = sbA("s_idb", [128, 128], BF16)
    s_idf = sbA("s_idf", [128, 128], F32)
    s_cos = sbA("s_cos", [128, NCH * 64], F32)
    s_sin = sbA("s_sin", [128, NCH * 64], F32)
    s_dt = sbA("s_dt", [128, H * 128], BF16)
    s_dq = sbA("s_dq", [128, H * 128], BF16)
    s_dk = sbA("s_dk", [128, H], F32)
    s_coef = sbA("s_coef", [128, 7 * H], F32)
    s_modT = sbA("s_modT", [128, 96], F32)
    s_ca1 = sbA("s_ca1", [128, 16], F32)
    s_nw = sbA("s_nw", [128, 16], F32)
    s_ct = sbA("s_ct", [128, 16], F32)
    s_silc = sbA("s_silc", [128, 16], F32)
    s_ones = sbA("s_ones", [1, 128], F32)
    s_xsal = sbA("s_xsal", [128, NCH], F32)
    s_sd = sbA("s_sd", [128, 1], F32)
    s_rs = sbA("s_rs", [128, 1], F32)
    s_ms = sbA("s_ms", [128, GH], F32)
    s_rsg = sbA("s_rsg", [128, GH], F32)
    s_am = sbA("s_am", [128, 8], F32)
    s_wsc = sbA("s_wsc", [128, 8], F32)
    s_row = sbA("s_row", [1, 512], F32)
    s_mnat = sbA("s_mnat", [128, 128], F32)
    s_kc0 = sbA("s_kc0", [128, 1], F32)
    s_kc1 = sbA("s_kc1", [128, 1], F32)
    s_kc2 = sbA("s_kc2", [128, 1], F32)
    s_kcw1 = sbA("s_kcw1", [128, 1], F32)
    s_kcw2 = sbA("s_kcw2", [128, 1], F32)
    s_kcx1 = sbA("s_kcx1", [128, 1], F32)
    s_kcx2 = sbA("s_kcx2", [128, 1], F32)
    s_wrm = sbA("s_wrm", [128, 1], F32)
    s_am2 = sbA("s_am2", [128, 1], F32)
    s_rws = sbA("s_rws", [128, 1], F32)
    s_rsg2 = sbA("s_rsg2", [128, GH], F32)
    s_m8 = sbA("s_m8", [128, 8], F32)
    s_m8b = sbA("s_m8b", [128, 8], F32)
    s_z8 = sbA("s_z8", [128, 8], F32)
    s_ca_ = sbA("s_ca_", [128, 1], F32)
    s_cb_ = sbA("s_cb_", [128, 1], F32)
    s_cc_ = sbA("s_cc_", [128, 1], F32)
    s_cd_ = sbA("s_cd_", [128, 1], F32)
    s_ce_ = sbA("s_ce_", [128, 1], F32)
    s_cm_ = sbA("s_cm_", [128, 1], F32)
    s_cv_ = sbA("s_cv_", [128, 1], F32)
    s_qr1 = sbA("s_qr1", [128, 512], BF16)
    assert cur[0] <= off_B, f"A overflow {cur[0]}"

    s_x0 = sb_at("s_x0", [128, D], F32, off_B)
    s_x1 = sb_at("s_x1", [128, D], F32, off_B + 8 * KB)
    s_h0a = sb_at("s_h0a", [128, D], BF16, off_B + 16 * KB)
    s_h0b = sb_at("s_h0b", [128, D], BF16, off_B + 20 * KB)
    s_bc1 = sb_at("s_bc1", [128, 512], F32, off_B + 16 * KB)
    s_bc2 = sb_at("s_bc2", [128, 512], F32, off_B + 18 * KB)
    s_sil0 = sb_at("s_sil0", [128, 512], F32, off_B)
    s_sil1 = sb_at("s_sil1", [128, 512], F32, off_B + 2 * KB)
    s_rt0 = sb_at("s_rt0", [128, 256], F32, off_B + 4 * KB)
    s_rt1 = sb_at("s_rt1", [128, 256], F32, off_B + 5 * KB)
    s_rt2 = sb_at("s_rt2", [128, 256], F32, off_B + 6 * KB)
    s_rt3 = sb_at("s_rt3", [128, 256], F32, off_B + 7 * KB)
    s_qr0 = sb_at("s_qr0", [128, 512], BF16, off_B + 24 * KB)
    s_whA = sb_at("s_whA", [128, NKC * 512], BF16, off_B + 8 * KB)
    s_fw = sb_at("s_fw", [128, D], F32, off_B + 10 * KB)
    s_fi8 = sb_at("s_fi8", [128, D], I8, off_B + 18 * KB)
    s_fbf = sb_at("s_fbf", [128, D], BF16, off_B + 20 * KB)
    s_wqTs = sb_at("s_wqTs", [128, 16 * 256], BF16, off_H + 8 * KB)
    s_pt0 = sb_at("s_pt0", [128, 512], BF16, off_B)
    s_pt1 = sb_at("s_pt1", [128, 512], BF16, off_B + 1 * KB)
    s_qt0 = sb_at("s_qt0", [128, 512], BF16, off_B + 2 * KB)
    s_qt1 = sb_at("s_qt1", [128, 512], BF16, off_B + 3 * KB)
    s_oh0 = sb_at("s_oh0", [128, 1024], F32, off_B + 4 * KB)
    s_oh1 = sb_at("s_oh1", [128, 1024], F32, off_B + 8 * KB)
    s_gg = sb_at("s_gg", [128, 1024], F32, off_B + 12 * KB)
    s_sq = sb_at("s_sq", [128, 1024], F32, off_B + 16 * KB)
    s_on = sb_at("s_on", [128, 1024], BF16, off_B + 20 * KB)
    s_lgj = sb_at("s_lgj", [128, GH * 128], F32, off_B + 16 * KB)
    s_x2 = sb_at("s_x2", [128, D], F32, off_B + 16 * KB)
    s_x2b = sb_at("s_x2b", [128, D], F32, off_C + 16 * KB)
    s_x2c = sb_at("s_x2c", [128, D], F32, off_C + 24 * KB)
    s_rb0 = sb_at("s_rb0", [128, D], F32, off_B)
    s_rb1 = sb_at("s_rb1", [128, D], F32, off_B + 8 * KB)
    s_t1 = sb_at("s_t1", [128, 512], F32, off_B + 16 * KB)

    s_h1T = sb_at("s_h1T", [128, NKC * TLOC], BF16, off_C)
    s_onT2 = sb_at("s_onT2", [128, NCH * 1024], BF16, off_C)
    s_sgg2 = sb_at("s_sgg2", [128, NCH * 1024], BF16, off_C + 16 * KB)
    s_xqTb = sb_at("s_xqTb", [128, NCH * 1024], BF16, off_C)

    s_wh0 = sb_at("s_wh0", [128, NKC * 512], BF16, off_D)
    s_wh1 = sb_at("s_wh1", [128, NKC * 512], BF16, off_D + 16 * KB)
    s_onT1 = sb_at("s_onT1", [128, NCH * 1024], BF16, off_D)
    s_sgg1 = sb_at("s_sgg1", [128, NCH * 1024], BF16, off_D + 16 * KB)
    s_wp0 = sb_at("s_wp0", [128, NKC * 256], BF16, off_D + 16 * KB)
    s_wp1 = sb_at("s_wp1", [128, NKC * 256], BF16, off_D + 24 * KB)
    s_xqTa = sb_at("s_xqTa", [128, NCH * 1024], BF16, off_D)
    s_h2d = sb_at("s_h2d", [128, D], F32, off_D + 16 * KB)
    s_i8d = sb_at("s_i8d", [128, D], I8, off_D + 24 * KB)
    s_bfd = sb_at("s_bfd", [128, D], BF16, off_D + 26 * KB)
    s_bfd1 = sb_at("s_bfd1", [128, D], BF16, off_F)
    s_ob0 = sb_at("s_ob0", [128, D], F32, off_D + 16 * KB)
    s_ob1 = sb_at("s_ob1", [128, D], F32, off_D + 24 * KB)

    s_qT = sb_at("s_qT", [128, GH * TLOC], BF16, off_E)
    s_kT = sb_at("s_kT", [128, GH * TLOC], BF16, off_E + 16 * KB)
    s_k2 = sb_at("s_k2", [128, NCH * GH * 128], BF16, off_E + 32 * KB)
    s_vt = sb_at("s_vt", [128, NCH * GH * 128], BF16, off_E + 48 * KB)
    s_wmod = sb_at("s_wmod", [128, 8 * 1536], F32, off_E + 16 * KB)
    s_woP = sb_at("s_woP", [128, NKC * D], BF16, off_E)
    s_wqT = sb_at("s_wqT", [128, NKC * D], BF16, off_E)

    s_S = sb_at("s_S", [128, GH * 128], F32, off_F)
    s_Sb = sb_at("s_Sb", [128, GH * 128], BF16, off_F + 4 * KB)

    s_g1T = sb_at("s_g1T", [128, TLOC], BF16, off_G)
    s_wg2 = sb_at("s_wg2", [128, D], BF16, off_G + 2 * KB)

    s_modH = sb_at("s_modH", [1, 1536], F32, off_H)
    s_gmask = sb_at("s_gmask", [1, 1536], F32, off_D)
    s_modH2 = sb_at("s_modH2", [1, 1536], F32, off_D + 8 * KB)
    s_mcA = sb_at("s_mcA", [1, 512], F32, off_B)
    s_mcB = sb_at("s_mcB", [1, 512], F32, off_B + 2048)
    s_mcC = sb_at("s_mcC", [1, 512], F32, off_B + 4096)
    s_gab = sb_at("s_gab", [128, D], BF16, off_H)
    s_cb1b = sb_at("s_cb1b", [128, D], BF16, off_H + 4 * KB)
    s_sbb = sb_at("s_sbb", [128, D], BF16, off_H + 8 * KB)
    s_wsgb = sb_at("s_wsgb", [128, D], BF16, off_H + 12 * KB)
    s_bgbb = sb_at("s_bgbb", [128, D], BF16, off_H + 16 * KB)
    assert off_H + 20 * KB <= LIM

    pb = [nc.alloc_psum_tensor(f"pb{i}", [128, 512], F32) for i in range(6)]
    pt = [nc.alloc_psum_tensor(f"pt{i}", [128, 1024], BF16) for i in range(2)]

    for name in ("T", "V", "S", "G", "DG", "DS", "DA", "CC"):
        b.sems[name] = nc.alloc_semaphore(name=name)

    MMK = dict(start=True, stop=True, skip_group_check=True)


    end_taps = []

    def tap(name, t_sb):
        if name.startswith("end_"):
            if name in DEBUG_TAPS:
                end_taps.append((name, t_sb))
            return
        if name not in DEBUG_TAPS:
            return
        shp = [int(x) for x in t_sb.shape]
        dr = nc.dram_tensor("tap_" + name, shp, t_sb.dtype, kind="ExternalOutput")
        b.ds(lambda e, dr=dr, t_sb=t_sb: e.dma_start(dr[:, :], t_sb[:, :]),
             waits=b.bar())


    tk_const = {}
    for nm, t_sb, t_dr in (("idb", s_idb, idb), ("idf", s_idf, idf),
                           ("cos", s_cos, cosx), ("sin", s_sin, sinx),
                           ("dt", s_dt, dt_c), ("dq", s_dq, dq_c),
                           ("dk", s_dk, dk_c), ("coef", s_coef, coef),
                           ("nw", s_nw, nw_t), ("ct", s_ct, c_t)):
        tk_const[nm] = b.ds(lambda e, a=t_sb, c=t_dr: e.dma_start(a[:, :], c[:, :]))
    tk_ones = b.v(lambda e: e.memset(s_ones[:, :], 1.0))
    tk_kc = b.v(lambda e: [e.memset(s_z8[:, :], 0.0),
                           e.memset(s_kcw1[:, :], -1e-8),
                           e.memset(s_kcw2[:, :], 1e-8 / 127.0),
                           e.memset(s_kcx1[:, :], -1e-12),
                           e.memset(s_kcx2[:, :], 1e-12),
                           e.memset(s_kc0[:, :], float(LN_EPS)),
                           e.memset(s_kc1[:, :], float(RMS_EPS)),
                           e.memset(s_kc2[:, :], 1.0 / 128.0)][-1])
    def ld_wm(e, hh):
        srcap = AP(wmod_s, hh * 8 * 128 * 1536,
                   [[1536, 128], [1536 * 128, 8], [1, 1536]])
        dst = AP(s_wmod, 0, [[8 * 1536, 128], [1536, 8], [1, 1536]])
        return e.dma_start(dst, srcap)
    tk_wm0 = b.dg(lambda e: ld_wm(e, 0))
    tk_silc = b.s(lambda e: e.activation(s_silc[:, :], s_ct[:, :], AF.Silu),
                  waits=[tk_const["ct"]])

    tk_warm0 = b.s(lambda e: e.activation(s_wrm[:, :], s_kc0[:, 0:1], AF.Sqrt),
                   waits=[tk_kc, tk_silc])
    h0tok = {}
    prev_sd = None
    prev_rs_read = None
    tk_ev_bank = {}
    tok_h1T_plain = None
    prev_pair_tr = None
    prev_mth = None
    for pair in range(NCH // 2):
        for j in range(2):
            tc = pair * 2 + j
            buf = s_x0 if tc % 2 == 0 else s_x1
            h0 = s_h0a if j == 0 else s_h0b
            tk_x = b.ds(lambda e, tc=tc, buf=buf: e.dma_start(
                buf[:, :], AP(x_s, tc * 128 * D, [[D, 128], [1, D]])),
                waits=[h0tok.get(tc - 2)])
            tk_sqd = b.s(lambda e, buf=buf, h0=h0: e.activation(
                h0[:, :], buf[:, :], AF.Square),
                waits=[tk_x, prev_mth, prev_pair_tr, tk_warm0])
            tk_sum = b.v(lambda e, buf=buf: e.reduce_sum(
                s_ca_[:, :], buf[:, :], axis=AX.X),
                waits=[tk_x, prev_mth, prev_pair_tr])
            tk_sq2 = b.v(lambda e, h0=h0: e.reduce_sum(
                s_cb_[:, :], h0[:, :], axis=AX.X), waits=[tk_sqd])
            def mth(e):
                e.mul(s_cm_[:, :], s_ca_[:, 0:1], 1.0 / 2048.0)
                e.mul(s_cd_[:, :], s_cb_[:, 0:1], 1.0 / 2048.0)
                e.activation(s_ce_[:, :], s_cm_[:, 0:1], AF.Square)
                e.activation(s_cv_[:, :], s_ce_[:, 0:1], AF.Identity,
                             scale=-1.0, bias=s_cd_[:, 0:1])
                return e.activation(s_sd[:, :], s_cv_[:, 0:1], AF.Sqrt,
                                    bias=s_kc0[:, 0:1])
            tk_sd = b.s(mth, waits=[tk_sum, tk_sq2, prev_sd, prev_rs_read,
                                    tk_kc])
            prev_mth = tk_sd
            prev_sd = tk_sd
            tk_rs = b.v(lambda e: e.reciprocal(s_rs[:, :], s_sd[:, :]),
                        waits=[tk_sd])
            h0tok[tc] = b.v(lambda e, buf=buf, h0=h0: e.tensor_scalar(
                h0[:, :], buf[:, :], s_cm_[:, 0:1], s_rs[:, :],
                ALU.subtract, ALU.mult), waits=[tk_rs])
            prev_rs_read = h0tok[tc]
        tca = pair * 2
        for kk in range(NKC):
            bank = pt[kk % 2]
            def tr2(e, kk=kk, bank=bank):
                e.transpose(bank[:, 0:128], s_h0a[:, kk * 128:(kk + 1) * 128],
                            s_idb[:, :])
                return e.transpose(bank[:, 128:256],
                                   s_h0b[:, kk * 128:(kk + 1) * 128], s_idb[:, :])
            tk_tr = b.t(tr2, waits=[h0tok[tca], h0tok[tca + 1], tk_const["idb"],
                                    tk_ev_bank.get(kk % 2)])
            tk_ev = b.s(lambda e, kk=kk, bank=bank, tca=tca: e.activation(
                s_h1T[:, kk * TLOC + tca * 128:kk * TLOC + tca * 128 + 256],
                bank[:, 0:256], AF.Copy), waits=[tk_tr])
            tk_ev_bank[kk % 2] = tk_ev
            tok_h1T_plain = tk_ev
            prev_pair_tr = tk_tr
    bar_P0 = b.bar()

    b.gate = bar_P0
    tk_modmm = []
    for nt in range(3):
        def mmod0(e, nt=nt):
            last = None
            for kl in range(8):
                last = e.matmul(pb[nt][0:1, :], s_silc[:, kl:kl + 1],
                                s_wmod[:, kl * 1536 + nt * 512:
                                       kl * 1536 + (nt + 1) * 512],
                                start=(kl == 0), stop=False)
            return last
        tk_modmm.append(b.t(mmod0, waits=[tk_silc, tk_wm0]))
    tk_wm1 = b.dg(lambda e: ld_wm(e, 1), waits=[tk_modmm[-1]])
    for nt in range(3):
        def mmod1(e, nt=nt):
            last = None
            for kl in range(8):
                last = e.matmul(pb[nt][0:1, :], s_silc[:, 8 + kl:9 + kl],
                                s_wmod[:, kl * 1536 + nt * 512:
                                       kl * 1536 + (nt + 1) * 512],
                                start=False, stop=(kl == 7))
            return last
        tk_modmm[nt] = b.t(mmod1, waits=[tk_wm1])
    tk_gmask = b.ds(lambda e: e.dma_start(s_gmask[:, :], gmask[:, :]))
    tk_modev = []
    for nt in range(3):
        def mev(e, nt=nt):
            ps = pb[nt][0:1, :]
            mh = s_modH[0:1, nt * 512:(nt + 1) * 512]
            mh2 = s_modH2[0:1, nt * 512:(nt + 1) * 512]
            mk = s_gmask[0:1, nt * 512:(nt + 1) * 512]
            e.tensor_scalar(s_mcA[:, :], ps, 0.0, None, ALU.is_ge)
            e.tensor_scalar(s_mcB[:, :], s_mcA[:, :], 2.0, -1.0,
                            ALU.mult, ALU.add)
            e.tensor_scalar(s_mcA[:, :], ps, 0.0, None, ALU.max)
            e.scalar_tensor_tensor(s_mcC[:, :], ps, -1.0, s_mcA[:, :],
                                   ALU.mult, ALU.max)
            e.tensor_scalar(s_mcA[:, :], s_mcC[:, :], 0.25, None, ALU.max)
            e.tensor_tensor(s_mcC[:, :], s_mcB[:, :], s_mcA[:, :], ALU.mult)
            e.tensor_copy(mh, ps)
            e.tensor_tensor(s_mcA[:, :], s_mcC[:, :], mh, ALU.subtract)
            e.tensor_tensor(s_mcB[:, :], s_mcA[:, :], mk, ALU.mult)
            return e.tensor_tensor(mh2, mh, s_mcB[:, :], ALU.add)
        tk_modev.append(b.v(mev, waits=[tk_modmm[nt], tk_gmask]))
    tk_modsave = b.dg(lambda e: e.dma_start(agA_in[0:1, 0:1536], s_modH2[0:1, :]),
                      waits=tk_modev)

    fq_last_v = None
    fq_last_i = None
    tk_ws_done = {}
    for tile in range(2):
        tk_ldf = b.ds(lambda e, tile=tile: e.dma_start(
            s_fw[:, :], AP(ffnw_s, tile * 128 * D, [[D, 128], [1, D]])),
            waits=[fq_last_i])
        tk_a = b.v(lambda e: e.reduce_max(s_am[:, 0:1], s_fw[:, :], axis=AX.X,
                                          apply_absolute_value=True),
                   waits=[tk_ldf])
        tk_w1 = b.s(lambda e: e.activation(s_rws[:, :], s_am[:, 0:1], AF.Relu,
                                           bias=s_kcw1[:, 0:1]),
                    waits=[tk_a, tk_kc])
        tk_w2 = b.s(lambda e, tile=tile: e.activation(
            s_wsc[:, tile:tile + 1], s_rws[:, :], AF.Identity,
            bias=s_kcw2[:, 0:1], scale=1.0 / 127.0), waits=[tk_w1])
        tk_ws = b.v(lambda e, tile=tile: e.reciprocal(
            s_am2[:, 0:1], s_wsc[:, tile:tile + 1]), waits=[tk_w2])
        tk_ws_done[tile] = tk_w2
        tk_q = b.v(lambda e: e.tensor_scalar(
            s_fw[:, :], s_fw[:, :], s_am2[:, 0:1], None, ALU.mult),
            waits=[tk_ws, fq_last_i])
        tk_i = b.v(lambda e: e.tensor_copy(s_fi8[:, :], s_fw[:, :]),
                   waits=[tk_q])
        fq_last_i = tk_i
        tk_bf = b.s(lambda e: e.activation(s_fbf[:, :], s_fi8[:, :], AF.Copy),
                    waits=[tk_i])
        for half in range(2):
            def trf(e, half=half):
                last = None
                for kk in range(8):
                    last = e.transpose(pt[0][:, kk * 128:(kk + 1) * 128],
                                       s_fbf[:, (half * 8 + kk) * 128:
                                             (half * 8 + kk + 1) * 128],
                                       s_idb[:, :])
                return last
            tk_trf = b.t(trf, waits=[tk_bf, tk_const["idb"], fq_last_v])
            def evf(e, tile=tile, half=half):
                dst = AP(s_wqTs, (half * 8) * 256 + tile * 128,
                         [[16 * 256, 128], [256, 8], [1, 128]])
                return e.tensor_copy(dst, AP(pt[0], 0,
                                             [[1024, 128], [128, 8], [1, 128]]))
            fq_last_v = b.v(evf, waits=[tk_trf])
    tk_wqTs_done = fq_last_v

    tk_wssave = [b.dg(lambda e, tile=tile: e.dma_start(
        AP(agA_in, 1536 + tile * 128, [[1, 128], [1, 1]]),
        AP(s_wsc, tile, [[8, 128], [1, 1]])), waits=[tk_ws_done[tile]])
        for tile in range(2)]
    tk_wqsave = b.dg(lambda e: e.dma_start(
        AP(agB_in, 0, [[256, 128], [256 * 128, 16], [1, 256]]),
        AP(s_wqTs, 0, [[16 * 256, 128], [256, 16], [1, 256]])),
        waits=[tk_wqTs_done])
    tk_agA = b.cc(lambda e: e.collective_compute(
        "AllGather", ALU.bypass, replica_groups=[list(range(NCORES))],
        ins=[agA_in.ap().opt()], outs=[agA_out.ap().opt()]),
        waits=[tk_modsave, tk_wqsave] + tk_wssave)
    tk_agB = b.cc(lambda e: e.collective_compute(
        "AllGather", ALU.bypass, replica_groups=[list(range(NCORES))],
        ins=[agB_in.ap().opt()], outs=[agB_out.ap().opt()]), waits=[tk_wqsave])

    tk_mn = [b.ds(lambda e, r=r: e.dma_start(
        AP(s_mnat, r * 12 * 128, [[128, 12], [1, 128]]),
        AP(agA_out, r * 1792, [[128, 12], [1, 128]])), waits=[tk_agA])
        for r in range(NCORES)]
    tk_modT_t = b.t(lambda e: e.transpose(pb[4][:, 0:96],
                                          AP(s_mnat, 0, [[128, 96], [1, 128]]),
                                          s_idf[0:96, 0:96]),
                    waits=tk_mn + [tk_const["idf"], tk_modmm[2]])
    tk_modT = b.v(lambda e: e.tensor_copy(s_modT[:, :], pb[4][:, 0:96]),
                  waits=[tk_modT_t])
    tk_ca1 = b.v(lambda e: e.tensor_scalar(s_ca1[:, :], s_modT[:, 16:32], 1.0,
                                           None, ALU.add), waits=[tk_modT])

    tk_h1m = []
    for k in range(NKC):
        tk_h1m.append(b.v(lambda e, k=k: e.tensor_scalar(
            s_h1T[:, k * TLOC:(k + 1) * TLOC], s_h1T[:, k * TLOC:(k + 1) * TLOC],
            s_ca1[:, k:k + 1], s_modT[:, k:k + 1], ALU.mult, ALU.add),
            waits=[tk_ca1, tok_h1T_plain]))
    tok_h1T = tk_h1m[-1]
    tap("modT", s_modT)
    tap("ca1", s_ca1)
    tap("h1T", s_h1T)



    def qkv_panel(g, which, panel_tok, panelbuf, ncols, col0, prev):
        nheads = ncols // 128
        h0l = col0 // 128
        dstT = s_qT if which == "q" else s_kT
        hist = {}
        rot_hist = {}
        trq_hist = {}
        prev = prev or {"ev": None, "hist": {}, "rot": {}, "trq": {}}
        ev_cell = [prev["ev"]]
        pend = [None]

        def histw(tc):
            return hist[tc] if tc >= 0 else prev["hist"].get(tc + 8)

        def rotw(tc):
            return rot_hist[tc] if tc >= 0 else prev["rot"].get(tc + 8, ())

        def trqw(tc):
            return trq_hist[tc] if tc >= 0 else prev["trq"].get(tc + 8)

        def emit_trq(tc, qr, rot_done):
            def trq(e, qr=qr, nheads=nheads):
                last = None
                for hh in range(nheads):
                    last = e.transpose(pt[1][:, hh * 128:(hh + 1) * 128],
                                       qr[:, hh * 128:(hh + 1) * 128],
                                       s_idb[:, :])
                return last
            tk_trq = b.t(trq, waits=list(rot_done) + [ev_cell[0]])
            trq_hist[tc] = tk_trq
            def evq(e, tc=tc, h0l=h0l, nheads=nheads, dstT=dstT):
                dst = AP(dstT, h0l * TLOC + tc * 128,
                         [[GH * TLOC, 128], [TLOC, nheads], [1, 128]])
                return e.tensor_copy(dst, AP(pt[1], 0, [[1024, 128],
                                                        [128, nheads],
                                                        [1, 128]]))
            ev_cell[0] = b.v(evq, waits=[tk_trq])

        for tc in range(NCH):
            bank = pb[tc % 2]
            def mmf(e, tc=tc, panelbuf=panelbuf, ncols=ncols, bank=bank):
                last = None
                for k in range(NKC):
                    last = e.matmul(
                        bank[:, 0:ncols],
                        s_h1T[:, k * TLOC + tc * 128:k * TLOC + tc * 128 + 128],
                        panelbuf[:, k * ncols:(k + 1) * ncols],
                        start=(k == 0), stop=(k == NKC - 1))
                return last
            tk_mm = b.t(mmf, waits=[panel_tok, tok_h1T, histw(tc - 2)])
            sil = s_sil0 if tc % 2 == 0 else s_sil1
            qr = s_qr0 if tc % 2 == 0 else s_qr1
            if which in ("q", "k"):
                tk_sil = b.s(lambda e, bank=bank, sil=sil, ncols=ncols:
                             e.activation(sil[:, 0:ncols], bank[:, 0:ncols],
                                          AF.Silu),
                             waits=[tk_mm] + list(rotw(tc - 2)))
                hist[tc] = tk_sil
                ca = AP(s_cos, tc * 64, [[NCH * 64, 128], [0, nheads], [1, 64]])
                sa = AP(s_sin, tc * 64, [[NCH * 64, 128], [0, nheads], [1, 64]])
                x1 = AP(sil, 0, [[512, 128], [128, nheads], [1, 64]])
                x2 = AP(sil, 64, [[512, 128], [128, nheads], [1, 64]])
                r1 = AP(qr, 0, [[512, 128], [128, nheads], [1, 64]])
                r2 = AP(qr, 64, [[512, 128], [128, nheads], [1, 64]])
                ta = AP(s_rt0 if tc % 2 == 0 else s_rt1, 0,
                        [[256, 128], [64, nheads], [1, 64]])
                tb = AP(s_rt2 if tc % 2 == 0 else s_rt3, 0,
                        [[256, 128], [64, nheads], [1, 64]])
                tk_r1 = b.v(lambda e, x1=x1, x2=x2, ta=ta, r1=r1, ca=ca, sa=sa: [
                    e.tensor_tensor(ta, x2, sa, ALU.mult),
                    e.tensor_tensor(r1, x1, ca, ALU.mult)][-1],
                    waits=[tk_sil, tk_const["cos"], tk_const["sin"],
                           trqw(tc - 2)])
                tk_r1b = b.v(lambda e, r1=r1, ta=ta: e.tensor_tensor(
                    r1, r1, ta, ALU.subtract), waits=[tk_r1])
                tk_r2 = b.g(lambda e, x1=x1, tb=tb, sa=sa: e.tensor_tensor(
                    tb, x1, sa, ALU.mult),
                    waits=[tk_sil, trqw(tc - 2)])
                tk_r2b = b.g(lambda e, r2=r2, x2=x2, ca=ca, tb=tb: [
                    e.tensor_tensor(r2, x2, ca, ALU.mult),
                    e.tensor_tensor(r2, r2, tb, ALU.add)][-1], waits=[tk_r2])
                rot_done = (tk_r1b, tk_r2b)
                rot_hist[tc] = rot_done
                if which == "k":
                    dkap = AP(s_dk, g * GH + h0l, [[H, 128], [1, nheads], [0, 128]])
                    qrap = AP(qr, 0, [[512, 128], [128, nheads], [1, 128]])
                    k2ap = AP(s_k2, tc * 1024 + h0l * 128,
                              [[NCH * GH * 128, 128], [128, nheads], [1, 128]])
                    b.v(lambda e, k2ap=k2ap, qrap=qrap, dkap=dkap: e.tensor_tensor(
                        k2ap, qrap, dkap, ALU.mult),
                        waits=list(rot_done) + [tk_const["dk"]])
                if pend[0] is not None:
                    emit_trq(*pend[0])
                pend[0] = (tc, qr, rot_done)
            else:
                def evv(e, bank=bank, tc=tc, h0l=h0l, ncols=ncols):
                    return e.copy(
                        s_vt[:, tc * 1024 + h0l * 128:tc * 1024 + h0l * 128 + ncols],
                        bank[:, 0:ncols])
                hist[tc] = b.s(evv, waits=[tk_mm])
                ev_cell[0] = hist[tc]
        if pend[0] is not None:
            emit_trq(*pend[0])
        return {"ev": ev_cell[0], "hist": hist, "rot": rot_hist,
                "trq": trq_hist}

    def emit_group(g, mode, bufs=None):
        ncols = 512 if mode == "a" else 256
        if bufs is None:
            bufs = (s_wh0, s_wh1) if mode == "a" else (s_wp0, s_wp1)
        npan = 1024 // ncols
        order = [("q", wq), ("k", wk), ("v", wv)]
        panlist = [(which, wsrc, p) for which, wsrc in order for p in range(npan)]
        panel_res = {}
        prev = None
        for i, (which, wsrc, p) in enumerate(panlist):
            buf = bufs[i % 2]
            def ldp(e, wsrc=wsrc, p=p, ncols=ncols, buf=buf):
                src = AP(wsrc, g * 1024 + p * ncols,
                         [[D, 128], [D * 128, NKC], [1, ncols]])
                dst = AP(buf, 0, [[NKC * ncols, 128], [ncols, NKC], [1, ncols]])
                return e.dma_start(dst, src)
            pv = panel_res.get(i - 2)
            tk_ld = b.dg(ldp, waits=[pv["hist"].get(NCH - 1)] if pv else [])
            panel_res[i] = qkv_panel(g, which, tk_ld, buf, ncols, p * ncols,
                                     prev)
            prev = panel_res[i]
        last = panel_res[len(panlist) - 1]
        return last["ev"] or last["hist"][NCH - 1]

    def emit_g1():
        tk_ld = b.dg(lambda e: e.dma_start(
            AP(s_wh0, 0, [[NKC * 512, 128], [128, NKC], [1, 128]]),
            AP(wg1, 0, [[128, 128], [128 * 128, NKC], [1, 128]])))
        tk_ld2 = b.dg(lambda e: e.dma_start(s_wg2[:, :], wg2_i[:, :]))
        last = None
        for tt in range(2):
            def mmf(e, tt=tt):
                lastx = None
                for k in range(NKC):
                    lastx = e.matmul(
                        pb[4][:, :], s_wh0[:, k * 128:(k + 1) * 128],
                        s_h1T[:, k * TLOC + tt * 512:k * TLOC + (tt + 1) * 512],
                        start=(k == 0), stop=(k == NKC - 1))
                return lastx
            tk_mm = b.t(mmf, waits=[tk_ld, tok_h1T, last])
            last = b.v(lambda e, tt=tt: e.tensor_copy(
                s_g1T[:, tt * 512:(tt + 1) * 512], pb[4][:, :]), waits=[tk_mm])
        return last, tk_ld2

    def emit_L(g):
        tk_z = b.v(lambda e: e.memset(s_S[:, :], 0.0))
        last_stt = {0: tk_z, 1: tk_z}
        for c in range(NCH):
            for hb in range(2):
                bank = pb[2 + hb]
                def umm(e, c=c, hb=hb, bank=bank):
                    last = None
                    for q in range(4):
                        o = c * 1024 + (hb * 4 + q) * 128
                        last = e.matmul(bank[:, q * 128:(q + 1) * 128],
                                        s_k2[:, o:o + 128], s_vt[:, o:o + 128],
                                        **MMK)
                    return last
                tk_u = b.t(umm, waits=[last_stt[hb]])
                def sttf(e, c=c, hb=hb, bank=bank, g=g):
                    last = None
                    for q in range(4):
                        hl = hb * 4 + q
                        last = e.scalar_tensor_tensor(
                            s_S[:, hl * 128:(hl + 1) * 128],
                            s_S[:, hl * 128:(hl + 1) * 128], LAMC[g * GH + hl],
                            bank[:, q * 128:(q + 1) * 128], ALU.mult, ALU.add)
                    return last
                last_stt[hb] = b.v(sttf, waits=[tk_u])
        tk_save = b.dg(lambda e, g=g: e.dma_start(agL_in[g][:, :], s_S[:, :]),
                       waits=[last_stt[0], last_stt[1]])
        tk_ag = b.cc(lambda e, g=g: e.collective_compute(
            "AllGather", ALU.bypass, replica_groups=[list(range(NCORES))],
            ins=[agL_in[g].ap().opt()], outs=[agL_out[g].ap().opt()]),
            waits=[tk_save])
        last = tk_ag
        for j in range(7):
            tk_ld = b.ds(lambda e, j=j, g=g: e.dma_start(
                s_lgj[:, :], AP(agL_out[g], j * 128 * (GH * 128),
                                [[GH * 128, 128], [1, GH * 128]])), waits=[last])
            def comb(e, j=j, g=g):
                lastx = None
                for hl in range(GH):
                    col = j * H + g * GH + hl
                    if j == 0:
                        lastx = e.tensor_scalar(
                            s_S[:, hl * 128:(hl + 1) * 128],
                            s_lgj[:, hl * 128:(hl + 1) * 128],
                            s_coef[:, col:col + 1], None, ALU.mult)
                    else:
                        lastx = e.scalar_tensor_tensor(
                            s_S[:, hl * 128:(hl + 1) * 128],
                            s_lgj[:, hl * 128:(hl + 1) * 128],
                            s_coef[:, col:col + 1],
                            s_S[:, hl * 128:(hl + 1) * 128], ALU.mult, ALU.add)
                return lastx
            last = b.v(comb, waits=[tk_ld, tk_const["coef"]])
        tk_sb = b.v(lambda e: e.tensor_copy(s_Sb[:, :], s_S[:, :]),
                    waits=[last])
        return tk_sb

    def emit_scan(g, tok_Sb, onT, sgg, tok_g1, tok_wg2):
        tk_gate = {}
        prev = None
        for c in range(NCH):
            for half in range(2):
                bank = pb[4 + ((c * 2 + half) % 2)]
                def gmm(e, c=c, half=half, bank=bank):
                    return e.matmul(
                        bank[:, :], s_g1T[:, c * 128:(c + 1) * 128],
                        s_wg2[:, g * 1024 + half * 512:g * 1024 + (half + 1) * 512],
                        **MMK)
                tk_mm = b.t(gmm, waits=[tok_g1, tok_wg2, prev])
                prev = b.s(lambda e, c=c, half=half, bank=bank, sgg=sgg:
                           e.activation(
                               sgg[:, c * 1024 + half * 512:
                                   c * 1024 + (half + 1) * 512],
                               bank[:, :], AF.Sigmoid), waits=[tk_mm])
                tk_gate[(c, half)] = prev
        prev_sb = tok_Sb
        prev_onT = None
        prev_pt = {0: None, 1: None}
        prev_oev = {0: None, 1: None}
        prev_stt = {0: tk_gate[(NCH - 1, 1)], 1: tk_gate[(NCH - 1, 1)]}
        prev_omm = {0: None, 1: None}
        for c in range(NCH):
            oev = []
            stts = []
            for hb in range(2):
                psc = pb[hb]
                po = pb[2 + hb]
                pu = pb[4 + hb]
                hg0 = g * GH + hb * 4
                def smm(e, c=c, hb=hb, psc=psc):
                    last = None
                    for q in range(4):
                        o = (hb * 4 + q) * TLOC + c * 128
                        last = e.matmul(psc[:, q * 128:(q + 1) * 128],
                                        s_kT[:, o:o + 128], s_qT[:, o:o + 128],
                                        **MMK)
                    return last
                tk_sc = b.t(smm, waits=[prev_pt[hb]])
                pts = s_pt0 if hb == 0 else s_pt1
                qts = s_qt0 if hb == 0 else s_qt1
                tk_pt = b.v(lambda e, psc=psc, pts=pts, hg0=hg0: e.tensor_tensor(
                    pts[:, :], psc[:, :], s_dt[:, hg0 * 128:(hg0 + 4) * 128],
                    ALU.mult), waits=[tk_sc, tk_const["dt"]])
                prev_pt[hb] = tk_pt
                tk_qt = b.g(lambda e, c=c, hb=hb, qts=qts, hg0=hg0:
                            e.tensor_tensor(
                                qts[:, :],
                                AP(s_qT, (hb * 4) * TLOC + c * 128,
                                   [[GH * TLOC, 128], [TLOC, 4], [1, 128]]),
                                AP(s_dq, hg0 * 128,
                                   [[H * 128, 128], [128, 4], [1, 128]]),
                                ALU.mult),
                            waits=[tk_const["dq"], prev_omm[hb]])
                def omm(e, c=c, hb=hb, po=po, pts=pts, qts=qts):
                    last = None
                    for q in range(4):
                        hl = hb * 4 + q
                        o = c * 1024 + hl * 128
                        e.matmul(po[:, q * 128:(q + 1) * 128],
                                 pts[:, q * 128:(q + 1) * 128],
                                 s_vt[:, o:o + 128],
                                 start=True, stop=False, skip_group_check=True)
                        last = e.matmul(po[:, q * 128:(q + 1) * 128],
                                        qts[:, q * 128:(q + 1) * 128],
                                        s_Sb[:, hl * 128:(hl + 1) * 128],
                                        start=False, stop=True,
                                        skip_group_check=True)
                    return last
                tk_o = b.t(omm, waits=[tk_pt, tk_qt, prev_sb, prev_oev[hb]])
                prev_omm[hb] = tk_o
                def umm(e, c=c, hb=hb, pu=pu):
                    last = None
                    for q in range(4):
                        o = c * 1024 + (hb * 4 + q) * 128
                        last = e.matmul(pu[:, q * 128:(q + 1) * 128],
                                        s_k2[:, o:o + 128], s_vt[:, o:o + 128],
                                        **MMK)
                    return last
                tk_u = b.t(umm, waits=[prev_stt[hb]])
                oh = s_oh0 if c % 2 == 0 else s_oh1
                oev.append(b.s(lambda e, po=po, hb=hb, oh=oh: e.copy(
                    oh[:, hb * 512:(hb + 1) * 512], po[:, :]), waits=[tk_o]))
                prev_oev[hb] = oev[-1]
                def sttf(e, c=c, hb=hb, pu=pu, g=g):
                    last = None
                    for q in range(4):
                        hl = hb * 4 + q
                        last = e.scalar_tensor_tensor(
                            s_S[:, hl * 128:(hl + 1) * 128],
                            s_S[:, hl * 128:(hl + 1) * 128], LAMC[g * GH + hl],
                            pu[:, q * 128:(q + 1) * 128], ALU.mult, ALU.add)
                    return last
                stts.append(b.v(sttf, waits=[tk_u, tk_o]))
                prev_stt[hb] = stts[-1]
            prev_sb = b.s(lambda e: e.activation(s_Sb[:, :], s_S[:, :], AF.Copy),
                          waits=stts)
            oh = s_oh0 if c % 2 == 0 else s_oh1
            tk_gg = b.v(lambda e, oh=oh, c=c, sgg=sgg: e.tensor_tensor(
                s_gg[:, :], oh[:, :], sgg[:, c * 1024:(c + 1) * 1024], ALU.mult),
                waits=oev + [tk_gate[(c, 1)]])
            tk_sq = b.g(lambda e: e.tensor_tensor(s_sq[:, :], s_gg[:, :],
                                                  s_gg[:, :], ALU.mult),
                        waits=[tk_gg])
            tk_ms = b.v(lambda e: e.reduce_sum(
                AP(s_ms, 0, [[GH, 128], [1, GH]]),
                AP(s_sq, 0, [[1024, 128], [128, GH], [1, 128]]), axis=AX.X),
                waits=[tk_sq])
            tk_sdg = b.s(lambda e: e.activation(s_rsg[:, :], s_ms[:, :], AF.Sqrt,
                                                bias=s_kc1[:, 0:1],
                                                scale=s_kc2[:, 0:1]),
                         waits=[tk_ms])
            tk_rsg = b.v(lambda e: e.reciprocal(s_rsg2[:, :], s_rsg[:, :]),
                         waits=[tk_sdg])
            tk_on = b.v(lambda e: e.tensor_tensor(
                s_on[:, :], AP(s_gg, 0, [[1024, 128], [128, GH], [1, 128]]),
                AP(s_rsg2, 0, [[GH, 128], [1, GH], [0, 128]]), ALU.mult),
                waits=[tk_rsg])
            if c == 0:
                tap(f"oh{g}", s_oh0)
                tap(f"gg{g}", s_gg)
                tap(f"rsg{g}", s_rsg2)
                tap(f"pt{g}", s_pt0)
                tap(f"on{g}", s_on)
            def tro(e):
                last = None
                for kl in range(GH):
                    last = e.transpose(pt[0][:, kl * 128:(kl + 1) * 128],
                                       s_on[:, kl * 128:(kl + 1) * 128],
                                       s_idb[:, :])
                return last
            tk_tro = b.t(tro, waits=[tk_on, prev_onT])
            prev_onT = b.s(lambda e, c=c, onT=onT: e.copy(
                onT[:, c * 1024:(c + 1) * 1024], pt[0][:, :]), waits=[tk_tro])
        return prev_onT

    tap("wqTs", s_wqTs)

    b.gate = b.bar()
    tok_qkv0 = emit_group(0, "a")
    b.gate = b.bar()
    tok_g1T, tok_wg2 = emit_g1()
    tap("qT0", s_qT)
    tap("kT0", s_kT)
    tap("k20", s_k2)
    tap("vt0", s_vt)
    tap("g1T", s_g1T)
    b.gate = b.bar()
    tok_S0 = emit_L(0)
    tap("S0", s_S)
    b.gate = b.bar()
    tok_onT1 = emit_scan(0, tok_S0, s_onT1, s_sgg1, tok_g1T, tok_wg2)
    bar_scan0 = [t for t in b.bar() if t[0] in ("T", "V", "S", "G")]

    tap("onT1", s_onT1)
    tap("sgg1", s_sgg1)

    b.gate = b.bar()
    tok_qkv1 = emit_group(1, "a", bufs=(s_whA, s_wh1))
    b.gate = b.bar()
    tok_S1 = emit_L(1)

    b.gate = bar_scan0

    bc_mm_prev = [None]
    bc_post_prev = {0: None, 1: None}

    def bc_build(kind, post):
        for nt in range(4):
            if kind == "ws":
                r0 = nt * 2
                tk_ra = b.da(lambda e, r0=r0: e.dma_start(
                    s_row[0:1, 0:256],
                    AP(agA_out, r0 * 1792 + 1536, [[256, 1], [1, 256]])),
                    waits=[tk_agA, bc_mm_prev[0]])
                tk_r = b.da(lambda e, r0=r0: e.dma_start(
                    s_row[0:1, 256:512],
                    AP(agA_out, (r0 + 1) * 1792 + 1536, [[256, 1], [1, 256]])),
                    waits=[tk_ra])
            elif kind == "fb":
                tk_r = b.da(lambda e, nt=nt: e.dma_start(
                    s_row[0:1, :], fb[0:1, nt * 512:(nt + 1) * 512]),
                    waits=[bc_mm_prev[0]])
            else:
                m = kind + nt * 512
                r, o = m // 1536, m % 1536
                tk_r = b.da(lambda e, r=r, o=o: e.dma_start(
                    s_row[0:1, :], AP(agA_out, r * 1792 + o, [[512, 1], [1, 512]])),
                    waits=[tk_agA, bc_mm_prev[0]])
            tk_m = b.t(lambda e, nt=nt: e.matmul(
                pb[4 + (nt % 2)][:, :], s_ones[0:1, :], s_row[0:1, :], **MMK),
                waits=[tk_r, tk_ones, bc_post_prev[nt % 2]])
            bc_mm_prev[0] = tk_m
            bc_post_prev[nt % 2] = post(nt, tk_m)
        return bc_post_prev[(4 - 1) % 2]

    def copy_post(dst):
        def post(nt, tk_m):
            return b.s(lambda e, nt=nt: e.copy(
                dst[:, nt * 512:(nt + 1) * 512], pb[4 + (nt % 2)][:, :]),
                waits=[tk_m])
        return post

    tk_gab = bc_build(2 * D, copy_post(s_gab))
    tk_sbb = bc_build(3 * D, copy_post(s_sbb))
    tk_gb1 = bc_build(5 * D, copy_post(s_wsgb))
    tk_gb2 = bc_build(5 * D, copy_post(s_bgbb))
    tk_cb1b = bc_build(4 * D, lambda nt, tk_m: b.v(
        lambda e, nt=nt: e.tensor_scalar(s_cb1b[:, nt * 512:(nt + 1) * 512],
                                         pb[4 + (nt % 2)][:, :], 1.0, None,
                                         ALU.add), waits=[tk_m]))
    tk_ws_bc = bc_build("ws", lambda nt, tk_m: b.v(
        lambda e, nt=nt: e.tensor_tensor(s_wsgb[:, nt * 512:(nt + 1) * 512],
                                         pb[4 + (nt % 2)][:, :],
                                         s_wsgb[:, nt * 512:(nt + 1) * 512],
                                         ALU.mult), waits=[tk_m, tk_gb1]))
    tk_fb_bc = bc_build("fb", lambda nt, tk_m: b.v(
        lambda e, nt=nt: e.tensor_tensor(s_bgbb[:, nt * 512:(nt + 1) * 512],
                                         pb[4 + (nt % 2)][:, :],
                                         s_bgbb[:, nt * 512:(nt + 1) * 512],
                                         ALU.mult), waits=[tk_m, tk_gb2]))
    tap("gab", s_gab)
    tap("cb1b", s_cb1b)
    tap("sbb", s_sbb)
    tap("wsgb", s_wsgb)
    tap("bgbb", s_bgbb)

    b.gate = b.bar()
    tok_onT2 = emit_scan(1, tok_S1, s_onT2, s_sgg2, tok_g1T, tok_wg2)
    tap("onT2", s_onT2)
    tap("S1", s_S)

    b.gate = b.bar()
    tk_wo_ld = []
    for hh in range(4):
        def ldwo(e, hh=hh):
            src = AP(wo, hh * 512 * D, [[D, 128], [D * 128, 4], [1, D]])
            dst = AP(s_woP, hh * 4 * D, [[NKC * D, 128], [D, 4], [1, D]])
            return e.dma_start(dst, src)
        tk_wo_ld.append(b.dg(ldwo))
    tk_wo_sc = []
    for k in range(NKC):
        tk_wo_sc.append(b.v(lambda e, k=k: e.tensor_scalar(
            s_woP[:, k * D:(k + 1) * D], s_woP[:, k * D:(k + 1) * D],
            s_nw[:, k:k + 1], None, ALU.mult),
            waits=[tk_wo_ld[k // 4], tk_const["nw"]]))
    tok_woP = tk_wo_sc[-1]

    tk_warm3 = b.s(lambda e: e.activation(s_wrm[:, :], s_kc0[:, 0:1], AF.Sqrt))
    tk_stage = {}
    tk_xqT = {}
    tk_bf_of = {}
    tk_wo_mm = {}
    tk_xs_of = {}
    prev_x = None
    prev_t1 = {}
    prev_bf = None
    prev_trx = {0: None, 1: None}
    prev_sd = None
    prev_rs_read = None
    prev_mth3 = None

    def emit_trx(c):
        bfd_c = s_bfd if c % 2 == 0 else s_bfd1
        for half in range(2):
            def trx(e, half=half, bfd_c=bfd_c):
                last = None
                for kl in range(8):
                    last = e.transpose(
                        pt[half][:, kl * 128:(kl + 1) * 128],
                        bfd_c[:, (half * 8 + kl) * 128:
                              (half * 8 + kl + 1) * 128],
                        s_idb[:, :])
                return last
            tk_trx = b.t(trx, waits=[tk_bf_of[c], tk_xqT.get((c - 1, half))])
            prev_trx[half] = tk_trx
            dstq = s_xqTa if half == 0 else s_xqTb
            tk_xqT[(c, half)] = b.s(lambda e, c=c, dstq=dstq, half=half:
                                    e.copy(
                                        dstq[:, c * 1024:(c + 1) * 1024],
                                        pt[half][:, :]), waits=[tk_trx])

    def emit_ln(c, xb2, tk_res):
        nonlocal prev_mth3, prev_sd, prev_rs_read, prev_bf
        tk_sqd = b.s(lambda e, xb2=xb2: e.activation(
            s_h2d[:, :], xb2[:, :], AF.Square),
            waits=tk_res + [prev_mth3, tk_warm3, prev_bf])
        tk_sum = b.v(lambda e, xb2=xb2: e.reduce_sum(
            s_ca_[:, :], xb2[:, :], axis=AX.X), waits=tk_res + [prev_mth3])
        tk_sq2 = b.v(lambda e: e.reduce_sum(
            s_cb_[:, :], s_h2d[:, :], axis=AX.X), waits=[tk_sqd])
        def mth3(e):
            e.mul(s_cm_[:, :], s_ca_[:, 0:1], 1.0 / 2048.0)
            e.mul(s_cd_[:, :], s_cb_[:, 0:1], 1.0 / 2048.0)
            e.activation(s_ce_[:, :], s_cm_[:, 0:1], AF.Square)
            e.activation(s_cv_[:, :], s_ce_[:, 0:1], AF.Identity,
                         scale=-1.0, bias=s_cd_[:, 0:1])
            return e.activation(s_sd[:, :], s_cv_[:, 0:1], AF.Sqrt,
                                bias=s_kc0[:, 0:1])
        tk_sd = b.s(mth3, waits=[tk_sum, tk_sq2, prev_sd, prev_rs_read, tk_kc])
        prev_mth3 = tk_sd
        prev_sd = tk_sd
        tk_rs = b.v(lambda e: e.reciprocal(s_rs[:, :], s_sd[:, :]), waits=[tk_sd])
        tk_h20 = b.v(lambda e, xb2=xb2: e.tensor_scalar(
            s_h2d[:, :], xb2[:, :], s_cm_[:, 0:1], s_rs[:, :],
            ALU.subtract, ALU.mult), waits=[tk_rs, prev_bf])
        prev_rs_read = tk_h20
        tk_m1 = b.v(lambda e: e.tensor_tensor(s_h2d[:, :], s_h2d[:, :],
                                              s_cb1b[:, :], ALU.mult),
                    waits=[tk_h20, tk_cb1b])
        tk_m2 = b.g(lambda e: e.tensor_tensor(s_h2d[:, :], s_h2d[:, :],
                                              s_sbb[:, :], ALU.add),
                    waits=[tk_m1, tk_sbb])
        tk_amax = b.v(lambda e: e.reduce_max(s_am[:, 0:1], s_h2d[:, :],
                                             axis=AX.X,
                                             apply_absolute_value=True),
                      waits=[tk_m2])
        tk_x1 = b.s(lambda e: e.activation(s_rws[:, :], s_am[:, 0:1], AF.Relu,
                                           bias=s_kcx1[:, 0:1],
                                           scale=1.0 / 127.0),
                    waits=[tk_amax])
        tk_xs = b.s(lambda e, c=c: e.activation(
            s_xsal[:, c:c + 1], s_rws[:, :], AF.Identity,
            bias=s_kcx2[:, 0:1]), waits=[tk_x1])
        tk_xs_of[c] = tk_xs
        tk_rxs = b.v(lambda e, c=c: e.reciprocal(s_rs[:, :],
                                                 s_xsal[:, c:c + 1]),
                     waits=[tk_xs])
        tk_qf = b.v(lambda e: e.tensor_scalar(s_h2d[:, :], s_h2d[:, :],
                                              s_rs[:, :], None, ALU.mult),
                    waits=[tk_rxs])
        tk_i8 = b.v(lambda e: e.tensor_copy(s_i8d[:, :], s_h2d[:, :]),
                    waits=[tk_qf, prev_bf])
        bfd_c = s_bfd if c % 2 == 0 else s_bfd1
        tk_bf = b.s(lambda e, bfd_c=bfd_c: e.activation(
            bfd_c[:, :], s_i8d[:, :], AF.Copy),
            waits=[tk_i8, prev_trx[0], prev_trx[1]])
        prev_bf = tk_bf
        tk_bf_of[c] = tk_bf
        tk_xp = b.g(lambda e, xb2=xb2: e.tensor_tensor(xb2[:, :], xb2[:, :],
                                              s_bgbb[:, :], ALU.add),
                    waits=[tk_m2, tk_fb_bc, tk_amax])
        tk_stage[c] = b.da(lambda e, c=c, xb2=xb2: e.dma_start(
            AP(out_s, c * 128 * D, [[D, 128], [1, D]]), xb2[:, :]),
            waits=[tk_xp])

    pend_ln = {}
    for c in range(NCH):
        if c >= 3:
            emit_trx(c - 3)
        xbuf = s_x0 if c % 2 == 0 else s_x1
        xb2 = (s_x2, s_x2b, s_x2c)[c % 3]
        tk_x = b.ds(lambda e, c=c, xbuf=xbuf: e.dma_start(
            xbuf[:, :], AP(x_s, c * 128 * D, [[D, 128], [1, D]])),
            waits=[prev_x])
        tk_res = []
        for oc in range(4):
            bank = pb[oc % 2]
            def womm(e, c=c, oc=oc, bank=bank):
                last = None
                for k in range(NKC):
                    src = s_onT1 if k < 8 else s_onT2
                    kl = k % 8
                    last = e.matmul(
                        bank[:, :],
                        src[:, c * 1024 + kl * 128:c * 1024 + kl * 128 + 128],
                        s_woP[:, k * D + oc * 512:k * D + (oc + 1) * 512],
                        start=(k == 0), stop=(k == NKC - 1))
                return last
            w2 = [tok_woP, tok_onT1, tok_onT2,
                  prev_t1.get((c, oc - 2)), prev_t1.get((c - 1, oc + 2))]
            tk_mm = b.t(womm, waits=w2)
            tk_wo_mm[(c, oc)] = tk_mm
            tk_t1 = b.v(lambda e, oc=oc, bank=bank, xb2=xb2: e.tensor_tensor(
                xb2[:, oc * 512:(oc + 1) * 512], bank[:, :],
                s_gab[:, oc * 512:(oc + 1) * 512], ALU.mult),
                waits=[tk_mm, tk_gab, tk_stage.get(c - 3)])
            prev_t1[(c, oc)] = tk_t1
            tk_res.append(b.g(lambda e, oc=oc, xbuf=xbuf, xb2=xb2:
                              e.tensor_tensor(
                xb2[:, oc * 512:(oc + 1) * 512],
                xb2[:, oc * 512:(oc + 1) * 512],
                xbuf[:, oc * 512:(oc + 1) * 512], ALU.add),
                waits=[tk_t1, tk_x]))
        prev_x = tk_res[-1]
        pend_ln[c] = (xb2, tk_res)
        if c >= 2:
            emit_ln(c - 2, *pend_ln.pop(c - 2))
    emit_ln(NCH - 2, *pend_ln.pop(NCH - 2))
    emit_ln(NCH - 1, *pend_ln.pop(NCH - 1))
    emit_trx(NCH - 3)
    emit_trx(NCH - 2)
    emit_trx(NCH - 1)

    tap("xqTa", s_xqTa)
    tap("xqTb", s_xqTb)
    tap("xsal", s_xsal)
    tap("end_modT", s_modT)
    tap("end_gab", s_gab)
    tap("end_wsgb", s_wsgb)
    tap("end_bgbb", s_bgbb)
    tap("end_cb1b", s_cb1b)
    tap("end_sbb", s_sbb)
    tap("end_xsal", s_xsal)

    SKIP_P3B = "stop_p3a" in DEBUG_TAPS
    p3a_tail = [prev_bf, prev_trx[0], prev_trx[1], prev_x]
    tk_wq_ld = {}
    for r in (range(NCORES) if not SKIP_P3B else []):
        def ldwq(e, r=r):
            src = AP(agB_out, r * D * 256, [[256, 128], [256 * 128, NKC], [1, 256]])
            dst = AP(s_wqT, r * 256, [[NKC * D, 128], [D, NKC], [1, 256]])
            return e.dma_start(dst, src)
        tk_wq_ld[r] = [b.ds, b.dg][r % 2](
            ldwq, waits=[tk_agB, tk_wo_mm.get((NCH - 1, r // 2))])
    prev_rb = None
    prev_fd = {}
    prev_et1 = {}
    final_dma = []
    for c in (range(NCH) if not SKIP_P3B else []):
        rbuf = s_rb0 if c % 2 == 0 else s_rb1
        tk_rb = b.ds(lambda e, c=c, rbuf=rbuf: e.dma_start(
            rbuf[:, :], AP(out_s, c * 128 * D, [[D, 128], [1, D]])),
            waits=[tk_stage[c], prev_rb] + ([prev_x] if c < 2 else []))
        obuf = s_ob0 if c % 2 == 0 else s_ob1
        tk_oc = []
        for oc in range(4):
            bank = pb[oc % 2]
            def fmm(e, c=c, oc=oc, bank=bank):
                last = None
                for k in range(NKC):
                    src = s_xqTa if k < 8 else s_xqTb
                    kl = k % 8
                    last = e.matmul(
                        bank[:, :],
                        src[:, c * 1024 + kl * 128:c * 1024 + kl * 128 + 128],
                        s_wqT[:, k * D + oc * 512:k * D + (oc + 1) * 512],
                        start=(k == 0), stop=(k == NKC - 1))
                return last
            wmm = [tk_wq_ld[2 * oc], tk_wq_ld[2 * oc + 1],
                   tk_xqT[(c, 0)], tk_xqT[(c, 1)],
                   prev_et1.get((c, oc - 2)), prev_et1.get((c - 1, oc + 2))]
            if c == 0 and oc < 2:
                wmm.append(prev_t1.get((NCH - 1, oc + 2)))
            tk_mm = b.t(fmm, waits=wmm)
            tk_et1 = b.v(lambda e, c=c, bank=bank, oc=oc: e.scalar_tensor_tensor(
                s_t1[:, :], bank[:, :], s_xsal[:, c:c + 1],
                s_wsgb[:, oc * 512:(oc + 1) * 512], ALU.mult, ALU.mult),
                waits=[tk_mm, tk_ws_bc, tk_xs_of[c]] +
                ([tk_stage[NCH - 1]] if c == 0 else []))
            prev_et1[(c, oc)] = tk_et1
            tk_oc.append(b.v(lambda e, oc=oc, obuf=obuf, rbuf=rbuf:
                             e.tensor_tensor(
                                 obuf[:, oc * 512:(oc + 1) * 512], s_t1[:, :],
                                 rbuf[:, oc * 512:(oc + 1) * 512], ALU.add),
                             waits=[tk_et1, tk_rb, prev_fd.get(c - 2)] +
                             (p3a_tail if c < 2 else [])))
        prev_rb = tk_oc[-1]
        final_dma.append(b.da(lambda e, c=c, obuf=obuf: e.dma_start(
            AP(out_s, c * 128 * D, [[D, 128], [1, D]]), obuf[:, :]),
            waits=tk_oc))
        prev_fd[c] = final_dma[-1]

    for nm, t_sb in end_taps:
        shp = [int(x) for x in t_sb.shape]
        dr = nc.dram_tensor("tap_" + nm, shp, t_sb.dtype, kind="ExternalOutput")
        b.ds(lambda e, dr=dr, t_sb=t_sb: e.dma_start(dr[:, :], t_sb[:, :]),
             waits=b.bar())

    b.s(lambda e: None,
        waits=[final_dma[-1] if final_dma else tk_stage[NCH - 1]])

    with nc.Block() as block:
        @block.tensor
        def _(e):
            for f in b.prog["tensor"]:
                f(e)

        @block.vector
        def _(e):
            for f in b.prog["vector"]:
                f(e)

        @block.scalar
        def _(e):
            for f in b.prog["scalar"]:
                f(e)

        @block.gpsimd
        def _(e):
            for f in b.prog["gpsimd"]:
                f(e)

        @block.sync
        def _(e):
            for f in b.prog["sync"]:
                f(e)

    return nc


def _host_constants(core):
    j = np.arange(64, dtype=np.float64)
    theta = BASEF ** (-2.0 * j / 128.0)
    T0 = core * TLOC
    cosx = np.zeros((128, NCH * 64), dtype=np.float32)
    sinx = np.zeros((128, NCH * 64), dtype=np.float32)
    for c in range(NCH):
        n = (T0 + c * 128 + np.arange(128, dtype=np.float64))[:, None]
        ang = n * theta[None, :]
        cosx[:, c * 64:(c + 1) * 64] = np.cos(ang)
        sinx[:, c * 64:(c + 1) * 64] = np.sin(ang)
    i = np.arange(128, dtype=np.float64)
    dt64 = np.zeros((128, H * 128), dtype=np.float64)
    dq64 = np.zeros((128, H * 128), dtype=np.float64)
    dk_c = np.zeros((128, H), dtype=np.float32)
    for h in range(H):
        ll = LOG_LAM[h]
        diff = i[None, :] - i[:, None]
        dt64[:, h * 128:(h + 1) * 128] = np.where(diff >= 0, np.exp(ll * diff), 0.0)
        dq64[:, h * 128:(h + 1) * 128] = np.exp(ll * (i[None, :] + 1.0))
        dk_c[:, h] = np.exp(ll * (128.0 - 1.0 - i)).astype(np.float32)
    coef = np.zeros((128, 7 * H), dtype=np.float32)
    for jj in range(7):
        for h in range(H):
            if jj < core:
                coef[:, jj * H + h] = np.float32(
                    np.exp(np.float64(LOG_LAM[h]) * 128.0 * 8.0 * (core - 1 - jj)))
    return (cosx, sinx,
            dt64.astype(np.float32).astype(ml_dtypes.bfloat16),
            dq64.astype(np.float32).astype(ml_dtypes.bfloat16),
            dk_c, coef)


def kernel(L0_attn_x, L0_attn_Wq, L0_attn_Wk, L0_attn_Wv, L0_attn_Wo,
           L0_attn_Wg1, L0_attn_Wg2, L0_attn_norm_weight, L0_ffn_weight,
           L0_ffn_bias, L0_c, L0_wmod):
    x = np.ascontiguousarray(np.asarray(L0_attn_x, dtype=np.float32).reshape(NTOK, D))
    wq = np.ascontiguousarray(np.asarray(L0_attn_Wq, dtype=np.float32))
    wk = np.ascontiguousarray(np.asarray(L0_attn_Wk, dtype=np.float32))
    wv = np.ascontiguousarray(np.asarray(L0_attn_Wv, dtype=np.float32))
    wo = np.ascontiguousarray(np.asarray(L0_attn_Wo, dtype=np.float32))
    wg1 = np.ascontiguousarray(np.asarray(L0_attn_Wg1, dtype=np.float32))
    wg2 = np.ascontiguousarray(np.asarray(L0_attn_Wg2, dtype=np.float32))
    nw = np.asarray(L0_attn_norm_weight, dtype=np.float32)
    fw = np.ascontiguousarray(np.asarray(L0_ffn_weight, dtype=np.float32))
    fbi = np.asarray(L0_ffn_bias, dtype=np.float32)
    cc = np.asarray(L0_c, dtype=np.float32)
    wm = np.ascontiguousarray(np.asarray(L0_wmod, dtype=np.float32))

    nw_t = np.ascontiguousarray(nw.reshape(16, 128).T)
    c_t = np.ascontiguousarray(cc.reshape(16, 128).T)
    fb = np.ascontiguousarray(fbi.reshape(1, D))
    idb = np.eye(128, dtype=np.float32).astype(ml_dtypes.bfloat16)
    idf = np.eye(128, dtype=np.float32)

    in_maps = []
    for core in range(NCORES):
        cosx, sinx, dt_c, dq_c, dk_c, coef = _host_constants(core)
        gm = np.zeros((1, 1536), np.float32)
        for o in range(1536):
            gidx = core * 1536 + o
            if 2 * D <= gidx < 3 * D or 5 * D <= gidx < 6 * D:
                gm[0, o] = 1.0
        in_maps.append({
            "x_s": np.ascontiguousarray(x[core * TLOC:(core + 1) * TLOC]),
            "wq": wq, "wk": wk, "wv": wv, "wo": wo,
            "wg1": wg1, "wg2_i": wg2,
            "nw_t": nw_t, "fb": fb, "c_t": c_t,
            "wmod_s": np.ascontiguousarray(wm[:, core * 1536:(core + 1) * 1536]),
            "ffnw_s": np.ascontiguousarray(fw[core * 256:(core + 1) * 256]),
            "cosx": cosx, "sinx": sinx, "dt_c": dt_c, "dq_c": dq_c,
            "dk_c": dk_c, "coef": coef, "idb": idb, "idf": idf,
            "gmask": gm,
        })

    nc = build_program()
    res = run_bass_kernel_spmd(nc, in_maps, core_ids=list(range(NCORES)))
    global _LAST_RES
    _LAST_RES = res.results
    out = np.concatenate([res.results[i]["out_s"] for i in range(NCORES)], axis=0)
    return out.reshape(1, NTOK, D).astype(np.float32)


# version 16
# speedup vs baseline: 1.0680x; 1.0680x over previous
import numpy as np
import ml_dtypes

import concourse.bass as bass
import concourse.mybir as mybir
from concourse.bass_utils import run_bass_kernel_spmd

F32 = mybir.dt.float32
BF16 = mybir.dt.bfloat16
I8 = mybir.dt.int8
AF = mybir.ActivationFunctionType
ALU = mybir.AluOpType
AX = mybir.AxisListType
AP = bass.AP

NCORES = 8
D = 2048
H = 16
CK = 128
NTOK = 8192
TLOC = NTOK // NCORES
NCH = TLOC // CK
NKC = D // 128
NG = 2
GH = H // NG
BASEF = 10000.0
LN_EPS = 1e-5
RMS_EPS = 1e-6

LOG_LAM = [-(2.0 ** (-0.5 * (h + 1))) for h in range(H)]
LAMC = [float(np.exp(np.float64(LOG_LAM[h]) * CK)) for h in range(H)]

DEBUG_TAPS = ()
_LAST_RES = None


class Bld:
    ENG = ("tensor", "vector", "scalar", "gpsimd", "sync")

    def __init__(self, nc):
        self.nc = nc
        self.prog = {e: [] for e in self.ENG}
        self.cnt = {k: 0 for k in ("T", "V", "S", "G", "DG", "DS", "DA", "CC")}
        self.sems = {}
        self.waited = {e: {} for e in self.ENG}
        self.gate = []

    def _step(self, eng, sem, inc, waits, fn):
        w2 = []
        wd = self.waited[eng]
        for tok in list(waits) + list(self.gate):
            if tok is None:
                continue
            s, v = tok
            if v <= 0 or wd.get(s, 0) >= v:
                continue
            wd[s] = v
            w2.append((s, v))
        self.cnt[sem] += inc
        token = (sem, self.cnt[sem])

        def run(e):
            for s, v in w2:
                e.wait_ge(self.sems[s], v)
            last = fn(e)
            if last is not None:
                last.then_inc(self.sems[sem], inc)

        self.prog[eng].append(run)
        return token

    def t(self, fn, waits=()):
        return self._step("tensor", "T", 1, waits, fn)

    def v(self, fn, waits=()):
        return self._step("vector", "V", 1, waits, fn)

    def s(self, fn, waits=()):
        return self._step("scalar", "S", 1, waits, fn)

    def g(self, fn, waits=()):
        return self._step("gpsimd", "G", 1, waits, fn)

    def dg(self, fn, waits=()):
        return self._step("gpsimd", "DG", 16, waits, fn)

    def ds(self, fn, waits=()):
        return self._step("sync", "DS", 16, waits, fn)

    def da(self, fn, waits=()):
        return self._step("scalar", "DA", 16, waits, fn)

    def cc(self, fn, waits=()):
        return self._step("gpsimd", "CC", 1, waits, fn)

    def bar(self):
        return [(k, self.cnt[k]) for k in self.cnt]


def build_program():
    nc = bass.Bass("TRN2", target_bir_lowering=False)
    b = Bld(nc)

    def din(name, shape, dt=F32):
        return nc.dram_tensor(name, shape, dt, kind="ExternalInput")

    x_s = din("x_s", [TLOC, D])
    wq = din("wq", [D, D])
    wk = din("wk", [D, D])
    wv = din("wv", [D, D])
    wo = din("wo", [D, D])
    wg1 = din("wg1", [D, 128])
    wg2_i = din("wg2_i", [128, D])
    nw_t = din("nw_t", [128, 16])
    fb = din("fb", [1, D])
    c_t = din("c_t", [128, 16])
    wmod_s = din("wmod_s", [D, 1536])
    ffnw_s = din("ffnw_s", [256, D])
    cosx = din("cosx", [128, NCH * 64])
    sinx = din("sinx", [128, NCH * 64])
    dt_c = din("dt_c", [128, H * 128], BF16)
    dq_c = din("dq_c", [128, H * 128], BF16)
    dk_c = din("dk_c", [128, H])
    coef = din("coef", [128, 7 * H])
    idb = din("idb", [128, 128], BF16)
    idf = din("idf", [128, 128])
    gmask = din("gmask", [1, 1536])

    out_s = nc.dram_tensor("out_s", [TLOC, D], F32, kind="ExternalOutput")

    agA_in = nc.dram_tensor("agA_in", [1, 1792], F32)
    agA_out = nc.dram_tensor("agA_out", [NCORES, 1792], F32)
    agB_in = nc.dram_tensor("agB_in", [D, 256], BF16)
    agB_out = nc.dram_tensor("agB_out", [NCORES * D, 256], BF16)
    agL_in = [nc.dram_tensor(f"agL{g}_in", [128, GH * 128], F32) for g in range(NG)]
    agL_out = [nc.dram_tensor(f"agL{g}_out", [NCORES * 128, GH * 128], F32)
               for g in range(NG)]

    KB = 1024
    off_B = 19 * KB
    off_C = 44 * KB
    off_D = 76 * KB
    off_E = 108 * KB
    off_F = 172 * KB
    off_G = 178 * KB
    off_H = 184 * KB
    LIM = 204 * KB

    SB_BASE = (nc.sbuf_base + 31) // 32 * 32
    assert SB_BASE + LIM <= nc.sbuf_top, (SB_BASE, nc.sbuf_top)

    def sb_at(name, shape, dt, off):
        return nc.alloc_sbuf_tensor_at(name, shape, dt, offset=SB_BASE + off)

    cur = [0]

    def sbA(name, shape, dt):
        t = sb_at(name, shape, dt, cur[0])
        esz = {F32: 4, BF16: 2, I8: 1}[dt]
        cur[0] += (int(np.prod(shape[1:])) * esz + 31) // 32 * 32
        return t

    s_idb = sbA("s_idb", [128, 128], BF16)
    s_idf = sbA("s_idf", [128, 128], F32)
    s_cos = sbA("s_cos", [128, NCH * 64], F32)
    s_sin = sbA("s_sin", [128, NCH * 64], F32)
    s_dt = sbA("s_dt", [128, H * 128], BF16)
    s_dq = sbA("s_dq", [128, H * 128], BF16)
    s_dk = sbA("s_dk", [128, H], F32)
    s_coef = sbA("s_coef", [128, 7 * H], F32)
    s_modT = sbA("s_modT", [128, 96], F32)
    s_ca1 = sbA("s_ca1", [128, 16], F32)
    s_nw = sbA("s_nw", [128, 16], F32)
    s_ct = sbA("s_ct", [128, 16], F32)
    s_silc = sbA("s_silc", [128, 16], F32)
    s_ones = sbA("s_ones", [1, 128], F32)
    s_xsal = sbA("s_xsal", [128, NCH], F32)
    s_sd = sbA("s_sd", [128, 1], F32)
    s_rs = sbA("s_rs", [128, 1], F32)
    s_ms = sbA("s_ms", [128, GH], F32)
    s_rsg = sbA("s_rsg", [128, GH], F32)
    s_am = sbA("s_am", [128, 8], F32)
    s_wsc = sbA("s_wsc", [128, 8], F32)
    s_row = sbA("s_row", [1, 512], F32)
    s_mnat = sbA("s_mnat", [128, 128], F32)
    s_kc0 = sbA("s_kc0", [128, 1], F32)
    s_kc1 = sbA("s_kc1", [128, 1], F32)
    s_kc2 = sbA("s_kc2", [128, 1], F32)
    s_kcw1 = sbA("s_kcw1", [128, 1], F32)
    s_kcw2 = sbA("s_kcw2", [128, 1], F32)
    s_kcx1 = sbA("s_kcx1", [128, 1], F32)
    s_kcx2 = sbA("s_kcx2", [128, 1], F32)
    s_wrm = sbA("s_wrm", [128, 1], F32)
    s_am2 = sbA("s_am2", [128, 1], F32)
    s_rws = sbA("s_rws", [128, 1], F32)
    s_rsg2 = sbA("s_rsg2", [128, GH], F32)
    s_m8 = sbA("s_m8", [128, 8], F32)
    s_m8b = sbA("s_m8b", [128, 8], F32)
    s_z8 = sbA("s_z8", [128, 8], F32)
    s_ca_ = sbA("s_ca_", [128, 1], F32)
    s_cb_ = sbA("s_cb_", [128, 1], F32)
    s_cc_ = sbA("s_cc_", [128, 1], F32)
    s_cd_ = sbA("s_cd_", [128, 1], F32)
    s_ce_ = sbA("s_ce_", [128, 1], F32)
    s_cm_ = sbA("s_cm_", [128, 1], F32)
    s_cv_ = sbA("s_cv_", [128, 1], F32)
    s_qr1 = sbA("s_qr1", [128, 512], BF16)
    assert cur[0] <= off_B, f"A overflow {cur[0]}"

    s_x0 = sb_at("s_x0", [128, D], F32, off_B)
    s_x1 = sb_at("s_x1", [128, D], F32, off_B + 8 * KB)
    s_h0a = sb_at("s_h0a", [128, D], BF16, off_B + 16 * KB)
    s_h0b = sb_at("s_h0b", [128, D], BF16, off_B + 20 * KB)
    s_bc1 = sb_at("s_bc1", [128, 512], F32, off_B + 16 * KB)
    s_bc2 = sb_at("s_bc2", [128, 512], F32, off_B + 18 * KB)
    s_sil0 = sb_at("s_sil0", [128, 512], F32, off_B)
    s_sil1 = sb_at("s_sil1", [128, 512], F32, off_B + 2 * KB)
    s_rt0 = sb_at("s_rt0", [128, 256], F32, off_B + 4 * KB)
    s_rt1 = sb_at("s_rt1", [128, 256], F32, off_B + 5 * KB)
    s_rt2 = sb_at("s_rt2", [128, 256], F32, off_B + 6 * KB)
    s_rt3 = sb_at("s_rt3", [128, 256], F32, off_B + 7 * KB)
    s_qr0 = sb_at("s_qr0", [128, 512], BF16, off_B + 24 * KB)
    s_whA = sb_at("s_whA", [128, NKC * 512], BF16, off_B + 8 * KB)
    s_fw = sb_at("s_fw", [128, D], F32, off_B + 10 * KB)
    s_fi8 = sb_at("s_fi8", [128, D], I8, off_B + 18 * KB)
    s_fbf = sb_at("s_fbf", [128, D], BF16, off_B + 20 * KB)
    s_wqTs = sb_at("s_wqTs", [128, 16 * 256], BF16, off_H + 8 * KB)
    s_pt0 = sb_at("s_pt0", [128, 512], BF16, off_B)
    s_pt1 = sb_at("s_pt1", [128, 512], BF16, off_B + 1 * KB)
    s_qt0 = sb_at("s_qt0", [128, 512], BF16, off_B + 2 * KB)
    s_qt1 = sb_at("s_qt1", [128, 512], BF16, off_B + 3 * KB)
    s_oh0 = sb_at("s_oh0", [128, 1024], F32, off_B + 4 * KB)
    s_oh1 = sb_at("s_oh1", [128, 1024], F32, off_B + 8 * KB)
    s_gg = sb_at("s_gg", [128, 1024], F32, off_B + 12 * KB)
    s_sq = sb_at("s_sq", [128, 1024], F32, off_B + 16 * KB)
    s_on = sb_at("s_on", [128, 1024], BF16, off_B + 20 * KB)
    s_lgj = sb_at("s_lgj", [128, GH * 128], F32, off_B + 16 * KB)
    s_x2 = sb_at("s_x2", [128, D], F32, off_B + 16 * KB)
    s_x2b = sb_at("s_x2b", [128, D], F32, off_C + 16 * KB)
    s_x2c = sb_at("s_x2c", [128, D], F32, off_C + 24 * KB)
    s_rb0 = sb_at("s_rb0", [128, D], F32, off_B)
    s_rb1 = sb_at("s_rb1", [128, D], F32, off_B + 8 * KB)
    s_t1 = sb_at("s_t1", [128, 512], F32, off_B + 16 * KB)

    s_h1T = sb_at("s_h1T", [128, NKC * TLOC], BF16, off_C)
    s_onT2 = sb_at("s_onT2", [128, NCH * 1024], BF16, off_C)
    s_sgg2 = sb_at("s_sgg2", [128, NCH * 1024], BF16, off_C + 16 * KB)
    s_xqTb = sb_at("s_xqTb", [128, NCH * 1024], BF16, off_C)

    s_wh0 = sb_at("s_wh0", [128, NKC * 512], BF16, off_D)
    s_wh1 = sb_at("s_wh1", [128, NKC * 512], BF16, off_D + 16 * KB)
    s_onT1 = sb_at("s_onT1", [128, NCH * 1024], BF16, off_D)
    s_sgg1 = sb_at("s_sgg1", [128, NCH * 1024], BF16, off_D + 16 * KB)
    s_wp0 = sb_at("s_wp0", [128, NKC * 256], BF16, off_D + 16 * KB)
    s_wp1 = sb_at("s_wp1", [128, NKC * 256], BF16, off_D + 24 * KB)
    s_xqTa = sb_at("s_xqTa", [128, NCH * 1024], BF16, off_D)
    s_h2d = sb_at("s_h2d", [128, D], F32, off_D + 16 * KB)
    s_i8d = sb_at("s_i8d", [128, D], I8, off_D + 24 * KB)
    s_bfd = sb_at("s_bfd", [128, D], BF16, off_D + 26 * KB)
    s_bfd1 = sb_at("s_bfd1", [128, D], BF16, off_F)
    s_ob0 = sb_at("s_ob0", [128, D], F32, off_D + 16 * KB)
    s_ob1 = sb_at("s_ob1", [128, D], F32, off_D + 24 * KB)

    s_qT = sb_at("s_qT", [128, GH * TLOC], BF16, off_E)
    s_kT = sb_at("s_kT", [128, GH * TLOC], BF16, off_E + 16 * KB)
    s_k2 = sb_at("s_k2", [128, NCH * GH * 128], BF16, off_E + 32 * KB)
    s_vt = sb_at("s_vt", [128, NCH * GH * 128], BF16, off_E + 48 * KB)
    s_wmod = sb_at("s_wmod", [128, 8 * 1536], F32, off_E + 16 * KB)
    s_woP = sb_at("s_woP", [128, NKC * D], BF16, off_E)
    s_wqT = sb_at("s_wqT", [128, NKC * D], BF16, off_E)

    s_S = sb_at("s_S", [128, GH * 128], F32, off_F)
    s_Sb = sb_at("s_Sb", [128, GH * 128], BF16, off_F + 4 * KB)

    s_g1T = sb_at("s_g1T", [128, TLOC], BF16, off_G)
    s_wg2 = sb_at("s_wg2", [128, D], BF16, off_G + 2 * KB)

    s_modH = sb_at("s_modH", [1, 1536], F32, off_H)
    s_gmask = sb_at("s_gmask", [1, 1536], F32, off_D)
    s_modH2 = sb_at("s_modH2", [1, 1536], F32, off_D + 8 * KB)
    s_mcA = sb_at("s_mcA", [1, 512], F32, off_B)
    s_mcB = sb_at("s_mcB", [1, 512], F32, off_B + 2048)
    s_mcC = sb_at("s_mcC", [1, 512], F32, off_B + 4096)
    s_gab = sb_at("s_gab", [128, D], BF16, off_H)
    s_cb1b = sb_at("s_cb1b", [128, D], BF16, off_H + 4 * KB)
    s_sbb = sb_at("s_sbb", [128, D], BF16, off_H + 8 * KB)
    s_wsgb = sb_at("s_wsgb", [128, D], BF16, off_H + 12 * KB)
    s_bgbb = sb_at("s_bgbb", [128, D], BF16, off_H + 16 * KB)
    assert off_H + 20 * KB <= LIM

    pb = [nc.alloc_psum_tensor(f"pb{i}", [128, 512], F32) for i in range(6)]
    pt = [nc.alloc_psum_tensor(f"pt{i}", [128, 1024], BF16) for i in range(2)]

    for name in ("T", "V", "S", "G", "DG", "DS", "DA", "CC"):
        b.sems[name] = nc.alloc_semaphore(name=name)

    MMK = dict(start=True, stop=True, skip_group_check=True)


    end_taps = []

    def tap(name, t_sb):
        if name.startswith("end_"):
            if name in DEBUG_TAPS:
                end_taps.append((name, t_sb))
            return
        if name not in DEBUG_TAPS:
            return
        shp = [int(x) for x in t_sb.shape]
        dr = nc.dram_tensor("tap_" + name, shp, t_sb.dtype, kind="ExternalOutput")
        b.ds(lambda e, dr=dr, t_sb=t_sb: e.dma_start(dr[:, :], t_sb[:, :]),
             waits=b.bar())


    tk_const = {}
    for nm, t_sb, t_dr in (("idb", s_idb, idb), ("idf", s_idf, idf),
                           ("cos", s_cos, cosx), ("sin", s_sin, sinx),
                           ("dt", s_dt, dt_c), ("dq", s_dq, dq_c),
                           ("dk", s_dk, dk_c), ("coef", s_coef, coef),
                           ("nw", s_nw, nw_t), ("ct", s_ct, c_t)):
        tk_const[nm] = b.ds(lambda e, a=t_sb, c=t_dr: e.dma_start(a[:, :], c[:, :]))
    tk_ones = b.v(lambda e: e.memset(s_ones[:, :], 1.0))
    tk_kc = b.v(lambda e: [e.memset(s_z8[:, :], 0.0),
                           e.memset(s_kcw1[:, :], -1e-8),
                           e.memset(s_kcw2[:, :], 1e-8 / 127.0),
                           e.memset(s_kcx1[:, :], -1e-12),
                           e.memset(s_kcx2[:, :], 1e-12),
                           e.memset(s_kc0[:, :], float(LN_EPS)),
                           e.memset(s_kc1[:, :], float(RMS_EPS)),
                           e.memset(s_kc2[:, :], 1.0 / 128.0)][-1])
    def ld_wm(e, hh):
        srcap = AP(wmod_s, hh * 8 * 128 * 1536,
                   [[1536, 128], [1536 * 128, 8], [1, 1536]])
        dst = AP(s_wmod, 0, [[8 * 1536, 128], [1536, 8], [1, 1536]])
        return e.dma_start(dst, srcap)
    tk_wm0 = b.dg(lambda e: ld_wm(e, 0))
    tk_silc = b.s(lambda e: e.activation(s_silc[:, :], s_ct[:, :], AF.Silu),
                  waits=[tk_const["ct"]])

    tk_warm0 = b.s(lambda e: e.activation(s_wrm[:, :], s_kc0[:, 0:1], AF.Sqrt),
                   waits=[tk_kc, tk_silc])
    h0tok = {}
    prev_sd = None
    prev_rs_read = None
    tk_ev_bank = {}
    tok_h1T_plain = None
    prev_pair_tr = None
    prev_mth = None
    for pair in range(NCH // 2):
        for j in range(2):
            tc = pair * 2 + j
            buf = s_x0 if tc % 2 == 0 else s_x1
            h0 = s_h0a if j == 0 else s_h0b
            tk_x = b.ds(lambda e, tc=tc, buf=buf: e.dma_start(
                buf[:, :], AP(x_s, tc * 128 * D, [[D, 128], [1, D]])),
                waits=[h0tok.get(tc - 2)])
            tk_sqd = b.s(lambda e, buf=buf, h0=h0: e.activation(
                h0[:, :], buf[:, :], AF.Square),
                waits=[tk_x, prev_mth, prev_pair_tr, tk_warm0])
            tk_sum = b.v(lambda e, buf=buf: e.reduce_sum(
                s_ca_[:, :], buf[:, :], axis=AX.X),
                waits=[tk_x, prev_mth, prev_pair_tr])
            tk_sq2 = b.v(lambda e, h0=h0: e.reduce_sum(
                s_cb_[:, :], h0[:, :], axis=AX.X), waits=[tk_sqd])
            def mth(e):
                e.mul(s_cm_[:, :], s_ca_[:, 0:1], 1.0 / 2048.0)
                e.mul(s_cd_[:, :], s_cb_[:, 0:1], 1.0 / 2048.0)
                e.activation(s_ce_[:, :], s_cm_[:, 0:1], AF.Square)
                e.activation(s_cv_[:, :], s_ce_[:, 0:1], AF.Identity,
                             scale=-1.0, bias=s_cd_[:, 0:1])
                return e.activation(s_sd[:, :], s_cv_[:, 0:1], AF.Sqrt,
                                    bias=s_kc0[:, 0:1])
            tk_sd = b.s(mth, waits=[tk_sum, tk_sq2, prev_sd, prev_rs_read,
                                    tk_kc])
            prev_mth = tk_sd
            prev_sd = tk_sd
            tk_rs = b.v(lambda e: e.reciprocal(s_rs[:, :], s_sd[:, :]),
                        waits=[tk_sd])
            h0tok[tc] = b.v(lambda e, buf=buf, h0=h0: e.tensor_scalar(
                h0[:, :], buf[:, :], s_cm_[:, 0:1], s_rs[:, :],
                ALU.subtract, ALU.mult), waits=[tk_rs])
            prev_rs_read = h0tok[tc]
        tca = pair * 2
        for kk in range(NKC):
            bank = pt[kk % 2]
            def tr2(e, kk=kk, bank=bank):
                e.transpose(bank[:, 0:128], s_h0a[:, kk * 128:(kk + 1) * 128],
                            s_idb[:, :])
                return e.transpose(bank[:, 128:256],
                                   s_h0b[:, kk * 128:(kk + 1) * 128], s_idb[:, :])
            tk_tr = b.t(tr2, waits=[h0tok[tca], h0tok[tca + 1], tk_const["idb"],
                                    tk_ev_bank.get(kk % 2)])
            tk_ev = b.s(lambda e, kk=kk, bank=bank, tca=tca: e.activation(
                s_h1T[:, kk * TLOC + tca * 128:kk * TLOC + tca * 128 + 256],
                bank[:, 0:256], AF.Copy), waits=[tk_tr])
            tk_ev_bank[kk % 2] = tk_ev
            tok_h1T_plain = tk_ev
            prev_pair_tr = tk_tr
    bar_P0 = b.bar()

    b.gate = bar_P0
    tk_modmm = []
    for nt in range(3):
        def mmod0(e, nt=nt):
            last = None
            for kl in range(8):
                last = e.matmul(pb[nt][0:1, :], s_silc[:, kl:kl + 1],
                                s_wmod[:, kl * 1536 + nt * 512:
                                       kl * 1536 + (nt + 1) * 512],
                                start=(kl == 0), stop=False)
            return last
        tk_modmm.append(b.t(mmod0, waits=[tk_silc, tk_wm0]))
    tk_wm1 = b.dg(lambda e: ld_wm(e, 1), waits=[tk_modmm[-1]])
    for nt in range(3):
        def mmod1(e, nt=nt):
            last = None
            for kl in range(8):
                last = e.matmul(pb[nt][0:1, :], s_silc[:, 8 + kl:9 + kl],
                                s_wmod[:, kl * 1536 + nt * 512:
                                       kl * 1536 + (nt + 1) * 512],
                                start=False, stop=(kl == 7))
            return last
        tk_modmm[nt] = b.t(mmod1, waits=[tk_wm1])
    tk_gmask = b.ds(lambda e: e.dma_start(s_gmask[:, :], gmask[:, :]))
    tk_modev = []
    for nt in range(3):
        def mev(e, nt=nt):
            ps = pb[nt][0:1, :]
            mh = s_modH[0:1, nt * 512:(nt + 1) * 512]
            mh2 = s_modH2[0:1, nt * 512:(nt + 1) * 512]
            mk = s_gmask[0:1, nt * 512:(nt + 1) * 512]
            e.tensor_scalar(s_mcA[:, :], ps, 0.0, None, ALU.is_ge)
            e.tensor_scalar(s_mcB[:, :], s_mcA[:, :], 2.0, -1.0,
                            ALU.mult, ALU.add)
            e.tensor_scalar(s_mcA[:, :], ps, 0.0, None, ALU.max)
            e.scalar_tensor_tensor(s_mcC[:, :], ps, -1.0, s_mcA[:, :],
                                   ALU.mult, ALU.max)
            e.tensor_scalar(s_mcA[:, :], s_mcC[:, :], 0.25, None, ALU.max)
            e.tensor_tensor(s_mcC[:, :], s_mcB[:, :], s_mcA[:, :], ALU.mult)
            e.tensor_copy(mh, ps)
            e.tensor_tensor(s_mcA[:, :], s_mcC[:, :], mh, ALU.subtract)
            e.tensor_tensor(s_mcB[:, :], s_mcA[:, :], mk, ALU.mult)
            return e.tensor_tensor(mh2, mh, s_mcB[:, :], ALU.add)
        tk_modev.append(b.v(mev, waits=[tk_modmm[nt], tk_gmask]))
    tk_modsave = b.dg(lambda e: e.dma_start(agA_in[0:1, 0:1536], s_modH2[0:1, :]),
                      waits=tk_modev)

    fq_last_v = None
    fq_last_i = None
    tk_ws_done = {}
    for tile in range(2):
        tk_ldf = b.ds(lambda e, tile=tile: e.dma_start(
            s_fw[:, :], AP(ffnw_s, tile * 128 * D, [[D, 128], [1, D]])),
            waits=[fq_last_i])
        tk_a = b.v(lambda e: e.reduce_max(s_am[:, 0:1], s_fw[:, :], axis=AX.X,
                                          apply_absolute_value=True),
                   waits=[tk_ldf])
        tk_w1 = b.s(lambda e: e.activation(s_rws[:, :], s_am[:, 0:1], AF.Relu,
                                           bias=s_kcw1[:, 0:1]),
                    waits=[tk_a, tk_kc])
        tk_w2 = b.s(lambda e, tile=tile: e.activation(
            s_wsc[:, tile:tile + 1], s_rws[:, :], AF.Identity,
            bias=s_kcw2[:, 0:1], scale=1.0 / 127.0), waits=[tk_w1])
        tk_ws = b.v(lambda e, tile=tile: e.reciprocal(
            s_am2[:, 0:1], s_wsc[:, tile:tile + 1]), waits=[tk_w2])
        tk_ws_done[tile] = tk_w2
        tk_q = b.v(lambda e: e.tensor_scalar(
            s_fw[:, :], s_fw[:, :], s_am2[:, 0:1], None, ALU.mult),
            waits=[tk_ws, fq_last_i])
        tk_i = b.v(lambda e: e.tensor_copy(s_fi8[:, :], s_fw[:, :]),
                   waits=[tk_q])
        fq_last_i = tk_i
        tk_bf = b.s(lambda e: e.activation(s_fbf[:, :], s_fi8[:, :], AF.Copy),
                    waits=[tk_i])
        for half in range(2):
            def trf(e, half=half):
                last = None
                for kk in range(8):
                    last = e.transpose(pt[0][:, kk * 128:(kk + 1) * 128],
                                       s_fbf[:, (half * 8 + kk) * 128:
                                             (half * 8 + kk + 1) * 128],
                                       s_idb[:, :])
                return last
            tk_trf = b.t(trf, waits=[tk_bf, tk_const["idb"], fq_last_v])
            def evf(e, tile=tile, half=half):
                dst = AP(s_wqTs, (half * 8) * 256 + tile * 128,
                         [[16 * 256, 128], [256, 8], [1, 128]])
                return e.tensor_copy(dst, AP(pt[0], 0,
                                             [[1024, 128], [128, 8], [1, 128]]))
            fq_last_v = b.v(evf, waits=[tk_trf])
    tk_wqTs_done = fq_last_v

    tk_wssave = [b.dg(lambda e, tile=tile: e.dma_start(
        AP(agA_in, 1536 + tile * 128, [[1, 128], [1, 1]]),
        AP(s_wsc, tile, [[8, 128], [1, 1]])), waits=[tk_ws_done[tile]])
        for tile in range(2)]
    tk_wqsave = b.dg(lambda e: e.dma_start(
        AP(agB_in, 0, [[256, 128], [256 * 128, 16], [1, 256]]),
        AP(s_wqTs, 0, [[16 * 256, 128], [256, 16], [1, 256]])),
        waits=[tk_wqTs_done])
    tk_agA = b.cc(lambda e: e.collective_compute(
        "AllGather", ALU.bypass, replica_groups=[list(range(NCORES))],
        ins=[agA_in.ap().opt()], outs=[agA_out.ap().opt()]),
        waits=[tk_modsave, tk_wqsave] + tk_wssave)
    tk_agB = b.cc(lambda e: e.collective_compute(
        "AllGather", ALU.bypass, replica_groups=[list(range(NCORES))],
        ins=[agB_in.ap().opt()], outs=[agB_out.ap().opt()]), waits=[tk_wqsave])

    tk_mn = [b.ds(lambda e, r=r: e.dma_start(
        AP(s_mnat, r * 12 * 128, [[128, 12], [1, 128]]),
        AP(agA_out, r * 1792, [[128, 12], [1, 128]])), waits=[tk_agA])
        for r in range(NCORES)]
    tk_modT_t = b.t(lambda e: e.transpose(pb[4][:, 0:96],
                                          AP(s_mnat, 0, [[128, 96], [1, 128]]),
                                          s_idf[0:96, 0:96]),
                    waits=tk_mn + [tk_const["idf"], tk_modmm[2]])
    tk_modT = b.v(lambda e: e.tensor_copy(s_modT[:, :], pb[4][:, 0:96]),
                  waits=[tk_modT_t])
    tk_ca1 = b.v(lambda e: e.tensor_scalar(s_ca1[:, :], s_modT[:, 16:32], 1.0,
                                           None, ALU.add), waits=[tk_modT])

    tk_h1m = []
    for k in range(NKC):
        tk_h1m.append(b.v(lambda e, k=k: e.tensor_scalar(
            s_h1T[:, k * TLOC:(k + 1) * TLOC], s_h1T[:, k * TLOC:(k + 1) * TLOC],
            s_ca1[:, k:k + 1], s_modT[:, k:k + 1], ALU.mult, ALU.add),
            waits=[tk_ca1, tok_h1T_plain]))
    tok_h1T = tk_h1m[-1]
    tap("modT", s_modT)
    tap("ca1", s_ca1)
    tap("h1T", s_h1T)



    def qkv_panel(g, which, panel_tok, panelbuf, ncols, col0, prev,
                  rot_gp=True):
        nheads = ncols // 128
        h0l = col0 // 128
        dstT = s_qT if which == "q" else s_kT
        hist = {}
        rot_hist = {}
        trq_hist = {}
        prev = prev or {"ev": None, "hist": {}, "rot": {}, "trq": {}}
        ev_cell = [prev["ev"]]
        pend = [None]

        def histw(tc):
            return hist[tc] if tc >= 0 else prev["hist"].get(tc + 8)

        def rotw(tc):
            return rot_hist[tc] if tc >= 0 else prev["rot"].get(tc + 8, ())

        def trqw(tc):
            return trq_hist[tc] if tc >= 0 else prev["trq"].get(tc + 8)

        def emit_trq(tc, qr, rot_done):
            def trq(e, qr=qr, nheads=nheads):
                last = None
                for hh in range(nheads):
                    last = e.transpose(pt[1][:, hh * 128:(hh + 1) * 128],
                                       qr[:, hh * 128:(hh + 1) * 128],
                                       s_idb[:, :])
                return last
            tk_trq = b.t(trq, waits=list(rot_done) + [ev_cell[0]])
            trq_hist[tc] = tk_trq
            def evq(e, tc=tc, h0l=h0l, nheads=nheads, dstT=dstT):
                dst = AP(dstT, h0l * TLOC + tc * 128,
                         [[GH * TLOC, 128], [TLOC, nheads], [1, 128]])
                return e.tensor_copy(dst, AP(pt[1], 0, [[1024, 128],
                                                        [128, nheads],
                                                        [1, 128]]))
            ev_cell[0] = b.v(evq, waits=[tk_trq])

        for tc in range(NCH):
            bank = pb[tc % 2]
            def mmf(e, tc=tc, panelbuf=panelbuf, ncols=ncols, bank=bank):
                last = None
                for k in range(NKC):
                    last = e.matmul(
                        bank[:, 0:ncols],
                        s_h1T[:, k * TLOC + tc * 128:k * TLOC + tc * 128 + 128],
                        panelbuf[:, k * ncols:(k + 1) * ncols],
                        start=(k == 0), stop=(k == NKC - 1))
                return last
            tk_mm = b.t(mmf, waits=[panel_tok, tok_h1T, histw(tc - 2)])
            sil = s_sil0 if tc % 2 == 0 else s_sil1
            qr = s_qr0 if tc % 2 == 0 else s_qr1
            if which in ("q", "k"):
                tk_sil = b.s(lambda e, bank=bank, sil=sil, ncols=ncols:
                             e.activation(sil[:, 0:ncols], bank[:, 0:ncols],
                                          AF.Silu),
                             waits=[tk_mm] + list(rotw(tc - 2)))
                hist[tc] = tk_sil
                ca = AP(s_cos, tc * 64, [[NCH * 64, 128], [0, nheads], [1, 64]])
                sa = AP(s_sin, tc * 64, [[NCH * 64, 128], [0, nheads], [1, 64]])
                x1 = AP(sil, 0, [[512, 128], [128, nheads], [1, 64]])
                x2 = AP(sil, 64, [[512, 128], [128, nheads], [1, 64]])
                r1 = AP(qr, 0, [[512, 128], [128, nheads], [1, 64]])
                r2 = AP(qr, 64, [[512, 128], [128, nheads], [1, 64]])
                ta = AP(s_rt0 if tc % 2 == 0 else s_rt1, 0,
                        [[256, 128], [64, nheads], [1, 64]])
                tb = AP(s_rt2 if tc % 2 == 0 else s_rt3, 0,
                        [[256, 128], [64, nheads], [1, 64]])
                tk_r1 = b.v(lambda e, x1=x1, x2=x2, ta=ta, r1=r1, ca=ca, sa=sa: [
                    e.tensor_tensor(ta, x2, sa, ALU.mult),
                    e.tensor_tensor(r1, x1, ca, ALU.mult)][-1],
                    waits=[tk_sil, tk_const["cos"], tk_const["sin"],
                           trqw(tc - 2)])
                tk_r1b = b.v(lambda e, r1=r1, ta=ta: e.tensor_tensor(
                    r1, r1, ta, ALU.subtract), waits=[tk_r1])
                eng2 = b.g if rot_gp else b.v
                tk_r2 = eng2(lambda e, x1=x1, tb=tb, sa=sa: e.tensor_tensor(
                    tb, x1, sa, ALU.mult),
                    waits=[tk_sil, trqw(tc - 2)])
                tk_r2b = eng2(lambda e, r2=r2, x2=x2, ca=ca, tb=tb: [
                    e.tensor_tensor(r2, x2, ca, ALU.mult),
                    e.tensor_tensor(r2, r2, tb, ALU.add)][-1], waits=[tk_r2])
                rot_done = (tk_r1b, tk_r2b)
                rot_hist[tc] = rot_done
                if which == "k":
                    dkap = AP(s_dk, g * GH + h0l, [[H, 128], [1, nheads], [0, 128]])
                    qrap = AP(qr, 0, [[512, 128], [128, nheads], [1, 128]])
                    k2ap = AP(s_k2, tc * 1024 + h0l * 128,
                              [[NCH * GH * 128, 128], [128, nheads], [1, 128]])
                    b.v(lambda e, k2ap=k2ap, qrap=qrap, dkap=dkap: e.tensor_tensor(
                        k2ap, qrap, dkap, ALU.mult),
                        waits=list(rot_done) + [tk_const["dk"]])
                if pend[0] is not None:
                    emit_trq(*pend[0])
                pend[0] = (tc, qr, rot_done)
            else:
                def evv(e, bank=bank, tc=tc, h0l=h0l, ncols=ncols):
                    return e.copy(
                        s_vt[:, tc * 1024 + h0l * 128:tc * 1024 + h0l * 128 + ncols],
                        bank[:, 0:ncols])
                hist[tc] = b.s(evv, waits=[tk_mm])
                ev_cell[0] = hist[tc]
        if pend[0] is not None:
            emit_trq(*pend[0])
        return {"ev": ev_cell[0], "hist": hist, "rot": rot_hist,
                "trq": trq_hist}

    def emit_group(g, mode, bufs=None):
        ncols = 512 if mode == "a" else 256
        if bufs is None:
            bufs = (s_wh0, s_wh1) if mode == "a" else (s_wp0, s_wp1)
        npan = 1024 // ncols
        order = [("q", wq), ("k", wk), ("v", wv)]
        panlist = [(which, wsrc, p) for which, wsrc in order for p in range(npan)]
        panel_res = {}
        prev = None
        for i, (which, wsrc, p) in enumerate(panlist):
            buf = bufs[i % 2]
            def ldp(e, wsrc=wsrc, p=p, ncols=ncols, buf=buf):
                src = AP(wsrc, g * 1024 + p * ncols,
                         [[D, 128], [D * 128, NKC], [1, ncols]])
                dst = AP(buf, 0, [[NKC * ncols, 128], [ncols, NKC], [1, ncols]])
                return e.dma_start(dst, src)
            pv = panel_res.get(i - 2)
            tk_ld = b.dg(ldp, waits=[pv["hist"].get(NCH - 1)] if pv else [])
            panel_res[i] = qkv_panel(g, which, tk_ld, buf, ncols, p * ncols,
                                     prev)
            prev = panel_res[i]
        last = panel_res[len(panlist) - 1]
        return last["ev"] or last["hist"][NCH - 1]

    def emit_g1():
        tk_ld = b.dg(lambda e: e.dma_start(
            AP(s_wh0, 0, [[NKC * 512, 128], [128, NKC], [1, 128]]),
            AP(wg1, 0, [[128, 128], [128 * 128, NKC], [1, 128]])))
        tk_ld2 = b.dg(lambda e: e.dma_start(s_wg2[:, :], wg2_i[:, :]))
        last = None
        for tt in range(2):
            def mmf(e, tt=tt):
                lastx = None
                for k in range(NKC):
                    lastx = e.matmul(
                        pb[4][:, :], s_wh0[:, k * 128:(k + 1) * 128],
                        s_h1T[:, k * TLOC + tt * 512:k * TLOC + (tt + 1) * 512],
                        start=(k == 0), stop=(k == NKC - 1))
                return lastx
            tk_mm = b.t(mmf, waits=[tk_ld, tok_h1T, last])
            last = b.v(lambda e, tt=tt: e.tensor_copy(
                s_g1T[:, tt * 512:(tt + 1) * 512], pb[4][:, :]), waits=[tk_mm])
        return last, tk_ld2

    def emit_L(g):
        tk_z = b.v(lambda e: e.memset(s_S[:, :], 0.0))
        last_stt = {0: tk_z, 1: tk_z}
        for c in range(NCH):
            for hb in range(2):
                bank = pb[2 + hb]
                def umm(e, c=c, hb=hb, bank=bank):
                    last = None
                    for q in range(4):
                        o = c * 1024 + (hb * 4 + q) * 128
                        last = e.matmul(bank[:, q * 128:(q + 1) * 128],
                                        s_k2[:, o:o + 128], s_vt[:, o:o + 128],
                                        **MMK)
                    return last
                tk_u = b.t(umm, waits=[last_stt[hb]])
                def sttf(e, c=c, hb=hb, bank=bank, g=g):
                    last = None
                    for q in range(4):
                        hl = hb * 4 + q
                        last = e.scalar_tensor_tensor(
                            s_S[:, hl * 128:(hl + 1) * 128],
                            s_S[:, hl * 128:(hl + 1) * 128], LAMC[g * GH + hl],
                            bank[:, q * 128:(q + 1) * 128], ALU.mult, ALU.add)
                    return last
                last_stt[hb] = b.v(sttf, waits=[tk_u])
        tk_save = b.dg(lambda e, g=g: e.dma_start(agL_in[g][:, :], s_S[:, :]),
                       waits=[last_stt[0], last_stt[1]])
        tk_ag = b.cc(lambda e, g=g: e.collective_compute(
            "AllGather", ALU.bypass, replica_groups=[list(range(NCORES))],
            ins=[agL_in[g].ap().opt()], outs=[agL_out[g].ap().opt()]),
            waits=[tk_save])
        return tk_ag

    def emit_L_comb(g, tk_ag, extra):
        last = tk_ag
        for j in range(7):
            tk_ld = b.ds(lambda e, j=j, g=g: e.dma_start(
                s_lgj[:, :], AP(agL_out[g], j * 128 * (GH * 128),
                                [[GH * 128, 128], [1, GH * 128]])),
                waits=[last] + (extra if j == 0 else []))
            def comb(e, j=j, g=g):
                lastx = None
                for hl in range(GH):
                    col = j * H + g * GH + hl
                    if j == 0:
                        lastx = e.tensor_scalar(
                            s_S[:, hl * 128:(hl + 1) * 128],
                            s_lgj[:, hl * 128:(hl + 1) * 128],
                            s_coef[:, col:col + 1], None, ALU.mult)
                    else:
                        lastx = e.scalar_tensor_tensor(
                            s_S[:, hl * 128:(hl + 1) * 128],
                            s_lgj[:, hl * 128:(hl + 1) * 128],
                            s_coef[:, col:col + 1],
                            s_S[:, hl * 128:(hl + 1) * 128], ALU.mult, ALU.add)
                return lastx
            last = b.v(comb, waits=[tk_ld, tk_const["coef"]])
        tk_sb = b.v(lambda e: e.tensor_copy(s_Sb[:, :], s_S[:, :]),
                    waits=[last])
        return tk_sb

    def emit_scan(g, tok_Sb, onT, sgg, tok_g1, tok_wg2):
        tk_gate = {}
        prev = None
        for c in range(NCH):
            for half in range(2):
                bank = pb[4 + ((c * 2 + half) % 2)]
                def gmm(e, c=c, half=half, bank=bank):
                    return e.matmul(
                        bank[:, :], s_g1T[:, c * 128:(c + 1) * 128],
                        s_wg2[:, g * 1024 + half * 512:g * 1024 + (half + 1) * 512],
                        **MMK)
                tk_mm = b.t(gmm, waits=[tok_g1, tok_wg2, prev])
                prev = b.s(lambda e, c=c, half=half, bank=bank, sgg=sgg:
                           e.activation(
                               sgg[:, c * 1024 + half * 512:
                                   c * 1024 + (half + 1) * 512],
                               bank[:, :], AF.Sigmoid), waits=[tk_mm])
                tk_gate[(c, half)] = prev
        prev_sb = tok_Sb
        prev_onT = None
        prev_pt = {0: None, 1: None}
        prev_oev = {0: None, 1: None}
        prev_stt = {0: tk_gate[(NCH - 1, 1)], 1: tk_gate[(NCH - 1, 1)]}
        prev_omm = {0: None, 1: None}
        for c in range(NCH):
            oev = []
            stts = []
            for hb in range(2):
                psc = pb[hb]
                po = pb[2 + hb]
                pu = pb[4 + hb]
                hg0 = g * GH + hb * 4
                def smm(e, c=c, hb=hb, psc=psc):
                    last = None
                    for q in range(4):
                        o = (hb * 4 + q) * TLOC + c * 128
                        last = e.matmul(psc[:, q * 128:(q + 1) * 128],
                                        s_kT[:, o:o + 128], s_qT[:, o:o + 128],
                                        **MMK)
                    return last
                tk_sc = b.t(smm, waits=[prev_pt[hb]])
                pts = s_pt0 if hb == 0 else s_pt1
                qts = s_qt0 if hb == 0 else s_qt1
                tk_pt = b.v(lambda e, psc=psc, pts=pts, hg0=hg0: e.tensor_tensor(
                    pts[:, :], psc[:, :], s_dt[:, hg0 * 128:(hg0 + 4) * 128],
                    ALU.mult), waits=[tk_sc, tk_const["dt"]])
                prev_pt[hb] = tk_pt
                tk_qt = b.g(lambda e, c=c, hb=hb, qts=qts, hg0=hg0:
                            e.tensor_tensor(
                                qts[:, :],
                                AP(s_qT, (hb * 4) * TLOC + c * 128,
                                   [[GH * TLOC, 128], [TLOC, 4], [1, 128]]),
                                AP(s_dq, hg0 * 128,
                                   [[H * 128, 128], [128, 4], [1, 128]]),
                                ALU.mult),
                            waits=[tk_const["dq"], prev_omm[hb]])
                def omm(e, c=c, hb=hb, po=po, pts=pts, qts=qts):
                    last = None
                    for q in range(4):
                        hl = hb * 4 + q
                        o = c * 1024 + hl * 128
                        e.matmul(po[:, q * 128:(q + 1) * 128],
                                 pts[:, q * 128:(q + 1) * 128],
                                 s_vt[:, o:o + 128],
                                 start=True, stop=False, skip_group_check=True)
                        last = e.matmul(po[:, q * 128:(q + 1) * 128],
                                        qts[:, q * 128:(q + 1) * 128],
                                        s_Sb[:, hl * 128:(hl + 1) * 128],
                                        start=False, stop=True,
                                        skip_group_check=True)
                    return last
                tk_o = b.t(omm, waits=[tk_pt, tk_qt, prev_sb, prev_oev[hb]])
                prev_omm[hb] = tk_o
                def umm(e, c=c, hb=hb, pu=pu):
                    last = None
                    for q in range(4):
                        o = c * 1024 + (hb * 4 + q) * 128
                        last = e.matmul(pu[:, q * 128:(q + 1) * 128],
                                        s_k2[:, o:o + 128], s_vt[:, o:o + 128],
                                        **MMK)
                    return last
                tk_u = b.t(umm, waits=[prev_stt[hb]])
                oh = s_oh0 if c % 2 == 0 else s_oh1
                oev.append(b.s(lambda e, po=po, hb=hb, oh=oh: e.copy(
                    oh[:, hb * 512:(hb + 1) * 512], po[:, :]), waits=[tk_o]))
                prev_oev[hb] = oev[-1]
                def sttf(e, c=c, hb=hb, pu=pu, g=g):
                    last = None
                    for q in range(4):
                        hl = hb * 4 + q
                        last = e.scalar_tensor_tensor(
                            s_S[:, hl * 128:(hl + 1) * 128],
                            s_S[:, hl * 128:(hl + 1) * 128], LAMC[g * GH + hl],
                            pu[:, q * 128:(q + 1) * 128], ALU.mult, ALU.add)
                    return last
                stts.append(b.v(sttf, waits=[tk_u, tk_o]))
                prev_stt[hb] = stts[-1]
            prev_sb = b.s(lambda e: e.activation(s_Sb[:, :], s_S[:, :], AF.Copy),
                          waits=stts)
            oh = s_oh0 if c % 2 == 0 else s_oh1
            tk_gg = b.v(lambda e, oh=oh, c=c, sgg=sgg: e.tensor_tensor(
                s_gg[:, :], oh[:, :], sgg[:, c * 1024:(c + 1) * 1024], ALU.mult),
                waits=oev + [tk_gate[(c, 1)]])
            tk_sq = b.g(lambda e: e.tensor_tensor(s_sq[:, :], s_gg[:, :],
                                                  s_gg[:, :], ALU.mult),
                        waits=[tk_gg])
            tk_ms = b.v(lambda e: e.reduce_sum(
                AP(s_ms, 0, [[GH, 128], [1, GH]]),
                AP(s_sq, 0, [[1024, 128], [128, GH], [1, 128]]), axis=AX.X),
                waits=[tk_sq])
            tk_sdg = b.s(lambda e: e.activation(s_rsg[:, :], s_ms[:, :], AF.Sqrt,
                                                bias=s_kc1[:, 0:1],
                                                scale=s_kc2[:, 0:1]),
                         waits=[tk_ms])
            tk_rsg = b.v(lambda e: e.reciprocal(s_rsg2[:, :], s_rsg[:, :]),
                         waits=[tk_sdg])
            tk_on = b.v(lambda e: e.tensor_tensor(
                s_on[:, :], AP(s_gg, 0, [[1024, 128], [128, GH], [1, 128]]),
                AP(s_rsg2, 0, [[GH, 128], [1, GH], [0, 128]]), ALU.mult),
                waits=[tk_rsg])
            if c == 0:
                tap(f"oh{g}", s_oh0)
                tap(f"gg{g}", s_gg)
                tap(f"rsg{g}", s_rsg2)
                tap(f"pt{g}", s_pt0)
                tap(f"on{g}", s_on)
            def tro(e):
                last = None
                for kl in range(GH):
                    last = e.transpose(pt[0][:, kl * 128:(kl + 1) * 128],
                                       s_on[:, kl * 128:(kl + 1) * 128],
                                       s_idb[:, :])
                return last
            tk_tro = b.t(tro, waits=[tk_on, prev_onT])
            prev_onT = b.s(lambda e, c=c, onT=onT: e.copy(
                onT[:, c * 1024:(c + 1) * 1024], pt[0][:, :]), waits=[tk_tro])
        return prev_onT

    tap("wqTs", s_wqTs)

    def pld(g, wsrc, p, buf, waits):
        def ldp(e, wsrc=wsrc, p=p, buf=buf, g=g):
            src = AP(wsrc, g * 1024 + p * 512,
                     [[D, 128], [D * 128, NKC], [1, 512]])
            dst = AP(buf, 0, [[NKC * 512, 128], [512, NKC], [1, 512]])
            return e.dma_start(dst, src)
        return b.dg(ldp, waits=waits)

    def emit_grp(g, bufA, bufB):
        b.gate = b.bar()
        rK0 = qkv_panel(g, "k", pld(g, wk, 0, bufA, []), bufA, 512, 0, None)
        rK1 = qkv_panel(g, "k", pld(g, wk, 1, bufB, []), bufB, 512, 512, rK0)
        rV0 = qkv_panel(g, "v", pld(g, wv, 0, bufA, [rK0["hist"][NCH - 1]]),
                        bufA, 512, 0, rK1)
        rV1 = qkv_panel(g, "v", pld(g, wv, 1, bufB, [rK1["hist"][NCH - 1]]),
                        bufB, 512, 512, rV0)
        qld0 = pld(g, wq, 0, bufA, [rV0["hist"][NCH - 1]])
        qld1 = pld(g, wq, 1, bufB, [rV1["hist"][NCH - 1]])
        b.gate = b.bar()
        tk_ag = emit_L(g)
        rQ0 = qkv_panel(g, "q", qld0, bufA, 512, 0, rV1, rot_gp=False)
        rQ1 = qkv_panel(g, "q", qld1, bufB, 512, 512, rQ0, rot_gp=False)
        tok_S = emit_L_comb(g, tk_ag, [rQ0["hist"][NCH - 1],
                                       rQ1["hist"][NCH - 1]])
        return tok_S, rQ1

    tok_S0, _ = emit_grp(0, s_wh0, s_wh1)
    b.gate = b.bar()
    tok_g1T, tok_wg2 = emit_g1()
    tap("qT0", s_qT)
    tap("kT0", s_kT)
    tap("k20", s_k2)
    tap("vt0", s_vt)
    tap("g1T", s_g1T)
    tap("S0", s_S)
    b.gate = b.bar()
    tok_onT1 = emit_scan(0, tok_S0, s_onT1, s_sgg1, tok_g1T, tok_wg2)
    bar_scan0 = [t for t in b.bar() if t[0] in ("T", "V", "S", "G")]

    tap("onT1", s_onT1)
    tap("sgg1", s_sgg1)

    tok_S1, _ = emit_grp(1, s_whA, s_wh1)

    b.gate = bar_scan0

    bc_mm_prev = [None]
    bc_post_prev = {0: None, 1: None}

    def bc_build(kind, post):
        for nt in range(4):
            if kind == "ws":
                r0 = nt * 2
                tk_ra = b.da(lambda e, r0=r0: e.dma_start(
                    s_row[0:1, 0:256],
                    AP(agA_out, r0 * 1792 + 1536, [[256, 1], [1, 256]])),
                    waits=[tk_agA, bc_mm_prev[0]])
                tk_r = b.da(lambda e, r0=r0: e.dma_start(
                    s_row[0:1, 256:512],
                    AP(agA_out, (r0 + 1) * 1792 + 1536, [[256, 1], [1, 256]])),
                    waits=[tk_ra])
            elif kind == "fb":
                tk_r = b.da(lambda e, nt=nt: e.dma_start(
                    s_row[0:1, :], fb[0:1, nt * 512:(nt + 1) * 512]),
                    waits=[bc_mm_prev[0]])
            else:
                m = kind + nt * 512
                r, o = m // 1536, m % 1536
                tk_r = b.da(lambda e, r=r, o=o: e.dma_start(
                    s_row[0:1, :], AP(agA_out, r * 1792 + o, [[512, 1], [1, 512]])),
                    waits=[tk_agA, bc_mm_prev[0]])
            tk_m = b.t(lambda e, nt=nt: e.matmul(
                pb[4 + (nt % 2)][:, :], s_ones[0:1, :], s_row[0:1, :], **MMK),
                waits=[tk_r, tk_ones, bc_post_prev[nt % 2]])
            bc_mm_prev[0] = tk_m
            bc_post_prev[nt % 2] = post(nt, tk_m)
        return bc_post_prev[(4 - 1) % 2]

    def copy_post(dst):
        def post(nt, tk_m):
            return b.s(lambda e, nt=nt: e.copy(
                dst[:, nt * 512:(nt + 1) * 512], pb[4 + (nt % 2)][:, :]),
                waits=[tk_m])
        return post

    tk_gab = bc_build(2 * D, copy_post(s_gab))
    tk_sbb = bc_build(3 * D, copy_post(s_sbb))
    tk_gb1 = bc_build(5 * D, copy_post(s_wsgb))
    tk_gb2 = bc_build(5 * D, copy_post(s_bgbb))
    tk_cb1b = bc_build(4 * D, lambda nt, tk_m: b.v(
        lambda e, nt=nt: e.tensor_scalar(s_cb1b[:, nt * 512:(nt + 1) * 512],
                                         pb[4 + (nt % 2)][:, :], 1.0, None,
                                         ALU.add), waits=[tk_m]))
    tk_ws_bc = bc_build("ws", lambda nt, tk_m: b.v(
        lambda e, nt=nt: e.tensor_tensor(s_wsgb[:, nt * 512:(nt + 1) * 512],
                                         pb[4 + (nt % 2)][:, :],
                                         s_wsgb[:, nt * 512:(nt + 1) * 512],
                                         ALU.mult), waits=[tk_m, tk_gb1]))
    tk_fb_bc = bc_build("fb", lambda nt, tk_m: b.v(
        lambda e, nt=nt: e.tensor_tensor(s_bgbb[:, nt * 512:(nt + 1) * 512],
                                         pb[4 + (nt % 2)][:, :],
                                         s_bgbb[:, nt * 512:(nt + 1) * 512],
                                         ALU.mult), waits=[tk_m, tk_gb2]))
    tap("gab", s_gab)
    tap("cb1b", s_cb1b)
    tap("sbb", s_sbb)
    tap("wsgb", s_wsgb)
    tap("bgbb", s_bgbb)

    b.gate = b.bar()
    tok_onT2 = emit_scan(1, tok_S1, s_onT2, s_sgg2, tok_g1T, tok_wg2)
    tap("onT2", s_onT2)
    tap("S1", s_S)

    b.gate = b.bar()
    tk_wo_ld = []
    for hh in range(4):
        def ldwo(e, hh=hh):
            src = AP(wo, hh * 512 * D, [[D, 128], [D * 128, 4], [1, D]])
            dst = AP(s_woP, hh * 4 * D, [[NKC * D, 128], [D, 4], [1, D]])
            return e.dma_start(dst, src)
        tk_wo_ld.append(b.dg(ldwo))
    tk_wo_sc = []
    for k in range(NKC):
        tk_wo_sc.append(b.v(lambda e, k=k: e.tensor_scalar(
            s_woP[:, k * D:(k + 1) * D], s_woP[:, k * D:(k + 1) * D],
            s_nw[:, k:k + 1], None, ALU.mult),
            waits=[tk_wo_ld[k // 4], tk_const["nw"]]))
    tok_woP = tk_wo_sc[-1]

    tk_warm3 = b.s(lambda e: e.activation(s_wrm[:, :], s_kc0[:, 0:1], AF.Sqrt))
    tk_stage = {}
    tk_xqT = {}
    tk_bf_of = {}
    tk_wo_mm = {}
    tk_xs_of = {}
    prev_x = None
    prev_t1 = {}
    prev_bf = None
    prev_trx = {0: None, 1: None}
    prev_sd = None
    prev_rs_read = None
    prev_mth3 = None

    def emit_trx(c):
        bfd_c = s_bfd if c % 2 == 0 else s_bfd1
        for half in range(2):
            def trx(e, half=half, bfd_c=bfd_c):
                last = None
                for kl in range(8):
                    last = e.transpose(
                        pt[half][:, kl * 128:(kl + 1) * 128],
                        bfd_c[:, (half * 8 + kl) * 128:
                              (half * 8 + kl + 1) * 128],
                        s_idb[:, :])
                return last
            tk_trx = b.t(trx, waits=[tk_bf_of[c], tk_xqT.get((c - 1, half))])
            prev_trx[half] = tk_trx
            dstq = s_xqTa if half == 0 else s_xqTb
            tk_xqT[(c, half)] = b.s(lambda e, c=c, dstq=dstq, half=half:
                                    e.copy(
                                        dstq[:, c * 1024:(c + 1) * 1024],
                                        pt[half][:, :]), waits=[tk_trx])

    def emit_ln(c, xb2, tk_res):
        nonlocal prev_mth3, prev_sd, prev_rs_read, prev_bf
        tk_sqd = b.s(lambda e, xb2=xb2: e.activation(
            s_h2d[:, :], xb2[:, :], AF.Square),
            waits=tk_res + [prev_mth3, tk_warm3, prev_bf])
        tk_sum = b.v(lambda e, xb2=xb2: e.reduce_sum(
            s_ca_[:, :], xb2[:, :], axis=AX.X), waits=tk_res + [prev_mth3])
        tk_sq2 = b.v(lambda e: e.reduce_sum(
            s_cb_[:, :], s_h2d[:, :], axis=AX.X), waits=[tk_sqd])
        def mth3(e):
            e.mul(s_cm_[:, :], s_ca_[:, 0:1], 1.0 / 2048.0)
            e.mul(s_cd_[:, :], s_cb_[:, 0:1], 1.0 / 2048.0)
            e.activation(s_ce_[:, :], s_cm_[:, 0:1], AF.Square)
            e.activation(s_cv_[:, :], s_ce_[:, 0:1], AF.Identity,
                         scale=-1.0, bias=s_cd_[:, 0:1])
            return e.activation(s_sd[:, :], s_cv_[:, 0:1], AF.Sqrt,
                                bias=s_kc0[:, 0:1])
        tk_sd = b.s(mth3, waits=[tk_sum, tk_sq2, prev_sd, prev_rs_read, tk_kc])
        prev_mth3 = tk_sd
        prev_sd = tk_sd
        tk_rs = b.v(lambda e: e.reciprocal(s_rs[:, :], s_sd[:, :]), waits=[tk_sd])
        tk_h20 = b.v(lambda e, xb2=xb2: e.tensor_scalar(
            s_h2d[:, :], xb2[:, :], s_cm_[:, 0:1], s_rs[:, :],
            ALU.subtract, ALU.mult), waits=[tk_rs, prev_bf])
        prev_rs_read = tk_h20
        tk_m1 = b.g(lambda e: e.tensor_tensor(s_h2d[:, :], s_h2d[:, :],
                                              s_cb1b[:, :], ALU.mult),
                    waits=[tk_h20, tk_cb1b])
        tk_m2 = b.g(lambda e: e.tensor_tensor(s_h2d[:, :], s_h2d[:, :],
                                              s_sbb[:, :], ALU.add),
                    waits=[tk_m1, tk_sbb])
        tk_amax = b.v(lambda e: e.reduce_max(s_am[:, 0:1], s_h2d[:, :],
                                             axis=AX.X,
                                             apply_absolute_value=True),
                      waits=[tk_m2])
        tk_x1 = b.s(lambda e: e.activation(s_rws[:, :], s_am[:, 0:1], AF.Relu,
                                           bias=s_kcx1[:, 0:1],
                                           scale=1.0 / 127.0),
                    waits=[tk_amax])
        tk_xs = b.s(lambda e, c=c: e.activation(
            s_xsal[:, c:c + 1], s_rws[:, :], AF.Identity,
            bias=s_kcx2[:, 0:1]), waits=[tk_x1])
        tk_xs_of[c] = tk_xs
        tk_rxs = b.v(lambda e, c=c: e.reciprocal(s_rs[:, :],
                                                 s_xsal[:, c:c + 1]),
                     waits=[tk_xs])
        tk_qf = b.v(lambda e: e.tensor_scalar(s_h2d[:, :], s_h2d[:, :],
                                              s_rs[:, :], None, ALU.mult),
                    waits=[tk_rxs])
        tk_i8 = b.v(lambda e: e.tensor_copy(s_i8d[:, :], s_h2d[:, :]),
                    waits=[tk_qf, prev_bf])
        bfd_c = s_bfd if c % 2 == 0 else s_bfd1
        tk_bf = b.s(lambda e, bfd_c=bfd_c: e.activation(
            bfd_c[:, :], s_i8d[:, :], AF.Copy),
            waits=[tk_i8, prev_trx[0], prev_trx[1]])
        prev_bf = tk_bf
        tk_bf_of[c] = tk_bf
        tk_xp = b.g(lambda e, xb2=xb2: e.tensor_tensor(xb2[:, :], xb2[:, :],
                                              s_bgbb[:, :], ALU.add),
                    waits=[tk_m2, tk_fb_bc, tk_amax])
        tk_stage[c] = b.da(lambda e, c=c, xb2=xb2: e.dma_start(
            AP(out_s, c * 128 * D, [[D, 128], [1, D]]), xb2[:, :]),
            waits=[tk_xp])

    pend_ln = {}
    for c in range(NCH):
        if c >= 3:
            emit_trx(c - 3)
        xbuf = s_x0 if c % 2 == 0 else s_x1
        xb2 = (s_x2, s_x2b, s_x2c)[c % 3]
        tk_x = b.ds(lambda e, c=c, xbuf=xbuf: e.dma_start(
            xbuf[:, :], AP(x_s, c * 128 * D, [[D, 128], [1, D]])),
            waits=[prev_x])
        tk_res = []
        for oc in range(4):
            bank = pb[oc % 2]
            def womm(e, c=c, oc=oc, bank=bank):
                last = None
                for k in range(NKC):
                    src = s_onT1 if k < 8 else s_onT2
                    kl = k % 8
                    last = e.matmul(
                        bank[:, :],
                        src[:, c * 1024 + kl * 128:c * 1024 + kl * 128 + 128],
                        s_woP[:, k * D + oc * 512:k * D + (oc + 1) * 512],
                        start=(k == 0), stop=(k == NKC - 1))
                return last
            w2 = [tok_woP, tok_onT1, tok_onT2,
                  prev_t1.get((c, oc - 2)), prev_t1.get((c - 1, oc + 2))]
            tk_mm = b.t(womm, waits=w2)
            tk_wo_mm[(c, oc)] = tk_mm
            tk_t1 = b.v(lambda e, oc=oc, bank=bank, xb2=xb2: e.tensor_tensor(
                xb2[:, oc * 512:(oc + 1) * 512], bank[:, :],
                s_gab[:, oc * 512:(oc + 1) * 512], ALU.mult),
                waits=[tk_mm, tk_gab, tk_stage.get(c - 3)])
            prev_t1[(c, oc)] = tk_t1
            tk_res.append(b.g(lambda e, oc=oc, xbuf=xbuf, xb2=xb2:
                              e.tensor_tensor(
                xb2[:, oc * 512:(oc + 1) * 512],
                xb2[:, oc * 512:(oc + 1) * 512],
                xbuf[:, oc * 512:(oc + 1) * 512], ALU.add),
                waits=[tk_t1, tk_x]))
        prev_x = tk_res[-1]
        pend_ln[c] = (xb2, tk_res)
        if c >= 1:
            emit_ln(c - 1, *pend_ln.pop(c - 1))
    emit_ln(NCH - 1, *pend_ln.pop(NCH - 1))
    emit_trx(NCH - 3)
    emit_trx(NCH - 2)
    emit_trx(NCH - 1)

    tap("xqTa", s_xqTa)
    tap("xqTb", s_xqTb)
    tap("xsal", s_xsal)
    tap("end_modT", s_modT)
    tap("end_gab", s_gab)
    tap("end_wsgb", s_wsgb)
    tap("end_bgbb", s_bgbb)
    tap("end_cb1b", s_cb1b)
    tap("end_sbb", s_sbb)
    tap("end_xsal", s_xsal)

    SKIP_P3B = "stop_p3a" in DEBUG_TAPS
    p3a_tail = [prev_bf, prev_trx[0], prev_trx[1], prev_x]
    tk_wq_ld = {}
    for r in (range(NCORES) if not SKIP_P3B else []):
        def ldwq(e, r=r):
            src = AP(agB_out, r * D * 256, [[256, 128], [256 * 128, NKC], [1, 256]])
            dst = AP(s_wqT, r * 256, [[NKC * D, 128], [D, NKC], [1, 256]])
            return e.dma_start(dst, src)
        tk_wq_ld[r] = [b.ds, b.dg][r % 2](
            ldwq, waits=[tk_agB, tk_wo_mm.get((NCH - 1, r // 2))])
    prev_rb = None
    prev_fd = {}
    prev_et1 = {}
    final_dma = []
    for c in (range(NCH) if not SKIP_P3B else []):
        rbuf = s_rb0 if c % 2 == 0 else s_rb1
        tk_rb = b.ds(lambda e, c=c, rbuf=rbuf: e.dma_start(
            rbuf[:, :], AP(out_s, c * 128 * D, [[D, 128], [1, D]])),
            waits=[tk_stage[c], prev_rb] + ([prev_x] if c < 2 else []))
        obuf = s_ob0 if c % 2 == 0 else s_ob1
        tk_oc = []
        for oc in range(4):
            bank = pb[oc % 2]
            def fmm(e, c=c, oc=oc, bank=bank):
                last = None
                for k in range(NKC):
                    src = s_xqTa if k < 8 else s_xqTb
                    kl = k % 8
                    last = e.matmul(
                        bank[:, :],
                        src[:, c * 1024 + kl * 128:c * 1024 + kl * 128 + 128],
                        s_wqT[:, k * D + oc * 512:k * D + (oc + 1) * 512],
                        start=(k == 0), stop=(k == NKC - 1))
                return last
            wmm = [tk_wq_ld[2 * oc], tk_wq_ld[2 * oc + 1],
                   tk_xqT[(c, 0)], tk_xqT[(c, 1)],
                   prev_et1.get((c, oc - 2)), prev_et1.get((c - 1, oc + 2))]
            if c == 0 and oc < 2:
                wmm.append(prev_t1.get((NCH - 1, oc + 2)))
            tk_mm = b.t(fmm, waits=wmm)
            tk_et1 = b.v(lambda e, c=c, bank=bank, oc=oc: e.scalar_tensor_tensor(
                s_t1[:, :], bank[:, :], s_xsal[:, c:c + 1],
                s_wsgb[:, oc * 512:(oc + 1) * 512], ALU.mult, ALU.mult),
                waits=[tk_mm, tk_ws_bc, tk_xs_of[c]] +
                ([tk_stage[NCH - 1]] if c == 0 else []))
            prev_et1[(c, oc)] = tk_et1
            tk_oc.append(b.v(lambda e, oc=oc, obuf=obuf, rbuf=rbuf:
                             e.tensor_tensor(
                                 obuf[:, oc * 512:(oc + 1) * 512], s_t1[:, :],
                                 rbuf[:, oc * 512:(oc + 1) * 512], ALU.add),
                             waits=[tk_et1, tk_rb, prev_fd.get(c - 2)] +
                             (p3a_tail if c < 2 else [])))
        prev_rb = tk_oc[-1]
        final_dma.append(b.da(lambda e, c=c, obuf=obuf: e.dma_start(
            AP(out_s, c * 128 * D, [[D, 128], [1, D]]), obuf[:, :]),
            waits=tk_oc))
        prev_fd[c] = final_dma[-1]

    for nm, t_sb in end_taps:
        shp = [int(x) for x in t_sb.shape]
        dr = nc.dram_tensor("tap_" + nm, shp, t_sb.dtype, kind="ExternalOutput")
        b.ds(lambda e, dr=dr, t_sb=t_sb: e.dma_start(dr[:, :], t_sb[:, :]),
             waits=b.bar())

    b.s(lambda e: None,
        waits=[final_dma[-1] if final_dma else tk_stage[NCH - 1]])

    with nc.Block() as block:
        @block.tensor
        def _(e):
            for f in b.prog["tensor"]:
                f(e)

        @block.vector
        def _(e):
            for f in b.prog["vector"]:
                f(e)

        @block.scalar
        def _(e):
            for f in b.prog["scalar"]:
                f(e)

        @block.gpsimd
        def _(e):
            for f in b.prog["gpsimd"]:
                f(e)

        @block.sync
        def _(e):
            for f in b.prog["sync"]:
                f(e)

    return nc


def _host_constants(core):
    j = np.arange(64, dtype=np.float64)
    theta = BASEF ** (-2.0 * j / 128.0)
    T0 = core * TLOC
    cosx = np.zeros((128, NCH * 64), dtype=np.float32)
    sinx = np.zeros((128, NCH * 64), dtype=np.float32)
    for c in range(NCH):
        n = (T0 + c * 128 + np.arange(128, dtype=np.float64))[:, None]
        ang = n * theta[None, :]
        cosx[:, c * 64:(c + 1) * 64] = np.cos(ang)
        sinx[:, c * 64:(c + 1) * 64] = np.sin(ang)
    i = np.arange(128, dtype=np.float64)
    dt64 = np.zeros((128, H * 128), dtype=np.float64)
    dq64 = np.zeros((128, H * 128), dtype=np.float64)
    dk_c = np.zeros((128, H), dtype=np.float32)
    for h in range(H):
        ll = LOG_LAM[h]
        diff = i[None, :] - i[:, None]
        dt64[:, h * 128:(h + 1) * 128] = np.where(diff >= 0, np.exp(ll * diff), 0.0)
        dq64[:, h * 128:(h + 1) * 128] = np.exp(ll * (i[None, :] + 1.0))
        dk_c[:, h] = np.exp(ll * (128.0 - 1.0 - i)).astype(np.float32)
    coef = np.zeros((128, 7 * H), dtype=np.float32)
    for jj in range(7):
        for h in range(H):
            if jj < core:
                coef[:, jj * H + h] = np.float32(
                    np.exp(np.float64(LOG_LAM[h]) * 128.0 * 8.0 * (core - 1 - jj)))
    return (cosx, sinx,
            dt64.astype(np.float32).astype(ml_dtypes.bfloat16),
            dq64.astype(np.float32).astype(ml_dtypes.bfloat16),
            dk_c, coef)


def kernel(L0_attn_x, L0_attn_Wq, L0_attn_Wk, L0_attn_Wv, L0_attn_Wo,
           L0_attn_Wg1, L0_attn_Wg2, L0_attn_norm_weight, L0_ffn_weight,
           L0_ffn_bias, L0_c, L0_wmod):
    x = np.ascontiguousarray(np.asarray(L0_attn_x, dtype=np.float32).reshape(NTOK, D))
    wq = np.ascontiguousarray(np.asarray(L0_attn_Wq, dtype=np.float32))
    wk = np.ascontiguousarray(np.asarray(L0_attn_Wk, dtype=np.float32))
    wv = np.ascontiguousarray(np.asarray(L0_attn_Wv, dtype=np.float32))
    wo = np.ascontiguousarray(np.asarray(L0_attn_Wo, dtype=np.float32))
    wg1 = np.ascontiguousarray(np.asarray(L0_attn_Wg1, dtype=np.float32))
    wg2 = np.ascontiguousarray(np.asarray(L0_attn_Wg2, dtype=np.float32))
    nw = np.asarray(L0_attn_norm_weight, dtype=np.float32)
    fw = np.ascontiguousarray(np.asarray(L0_ffn_weight, dtype=np.float32))
    fbi = np.asarray(L0_ffn_bias, dtype=np.float32)
    cc = np.asarray(L0_c, dtype=np.float32)
    wm = np.ascontiguousarray(np.asarray(L0_wmod, dtype=np.float32))

    nw_t = np.ascontiguousarray(nw.reshape(16, 128).T)
    c_t = np.ascontiguousarray(cc.reshape(16, 128).T)
    fb = np.ascontiguousarray(fbi.reshape(1, D))
    idb = np.eye(128, dtype=np.float32).astype(ml_dtypes.bfloat16)
    idf = np.eye(128, dtype=np.float32)

    in_maps = []
    for core in range(NCORES):
        cosx, sinx, dt_c, dq_c, dk_c, coef = _host_constants(core)
        gm = np.zeros((1, 1536), np.float32)
        for o in range(1536):
            gidx = core * 1536 + o
            if 2 * D <= gidx < 3 * D or 5 * D <= gidx < 6 * D:
                gm[0, o] = 1.0
        in_maps.append({
            "x_s": np.ascontiguousarray(x[core * TLOC:(core + 1) * TLOC]),
            "wq": wq, "wk": wk, "wv": wv, "wo": wo,
            "wg1": wg1, "wg2_i": wg2,
            "nw_t": nw_t, "fb": fb, "c_t": c_t,
            "wmod_s": np.ascontiguousarray(wm[:, core * 1536:(core + 1) * 1536]),
            "ffnw_s": np.ascontiguousarray(fw[core * 256:(core + 1) * 256]),
            "cosx": cosx, "sinx": sinx, "dt_c": dt_c, "dq_c": dq_c,
            "dk_c": dk_c, "coef": coef, "idb": idb, "idf": idf,
            "gmask": gm,
        })

    nc = build_program()
    res = run_bass_kernel_spmd(nc, in_maps, core_ids=list(range(NCORES)))
    global _LAST_RES
    _LAST_RES = res.results
    out = np.concatenate([res.results[i]["out_s"] for i in range(NCORES)], axis=0)
    return out.reshape(1, NTOK, D).astype(np.float32)


# version 17 (final)
# speedup vs baseline: 1.0800x; 1.0113x over previous
import numpy as np
import ml_dtypes

import concourse.bass as bass
import concourse.mybir as mybir
from concourse.bass_utils import run_bass_kernel_spmd

F32 = mybir.dt.float32
BF16 = mybir.dt.bfloat16
I8 = mybir.dt.int8
AF = mybir.ActivationFunctionType
ALU = mybir.AluOpType
AX = mybir.AxisListType
AP = bass.AP

NCORES = 8
D = 2048
H = 16
CK = 128
NTOK = 8192
TLOC = NTOK // NCORES
NCH = TLOC // CK
NKC = D // 128
NG = 2
GH = H // NG
BASEF = 10000.0
LN_EPS = 1e-5
RMS_EPS = 1e-6

LOG_LAM = [-(2.0 ** (-0.5 * (h + 1))) for h in range(H)]
LAMC = [float(np.exp(np.float64(LOG_LAM[h]) * CK)) for h in range(H)]

DEBUG_TAPS = ()
_LAST_RES = None


class Bld:
    ENG = ("tensor", "vector", "scalar", "gpsimd", "sync")

    def __init__(self, nc):
        self.nc = nc
        self.prog = {e: [] for e in self.ENG}
        self.cnt = {k: 0 for k in ("T", "V", "S", "G", "DG", "DS", "DA", "CC")}
        self.sems = {}
        self.waited = {e: {} for e in self.ENG}
        self.gate = []

    def _step(self, eng, sem, inc, waits, fn):
        w2 = []
        wd = self.waited[eng]
        for tok in list(waits) + list(self.gate):
            if tok is None:
                continue
            s, v = tok
            if v <= 0 or wd.get(s, 0) >= v:
                continue
            wd[s] = v
            w2.append((s, v))
        self.cnt[sem] += inc
        token = (sem, self.cnt[sem])

        def run(e):
            for s, v in w2:
                e.wait_ge(self.sems[s], v)
            last = fn(e)
            if last is not None:
                last.then_inc(self.sems[sem], inc)

        self.prog[eng].append(run)
        return token

    def t(self, fn, waits=()):
        return self._step("tensor", "T", 1, waits, fn)

    def v(self, fn, waits=()):
        return self._step("vector", "V", 1, waits, fn)

    def s(self, fn, waits=()):
        return self._step("scalar", "S", 1, waits, fn)

    def g(self, fn, waits=()):
        return self._step("gpsimd", "G", 1, waits, fn)

    def dg(self, fn, waits=()):
        return self._step("gpsimd", "DG", 16, waits, fn)

    def ds(self, fn, waits=()):
        return self._step("sync", "DS", 16, waits, fn)

    def da(self, fn, waits=()):
        return self._step("scalar", "DA", 16, waits, fn)

    def cc(self, fn, waits=()):
        return self._step("gpsimd", "CC", 1, waits, fn)

    def bar(self):
        return [(k, self.cnt[k]) for k in self.cnt]


def build_program():
    nc = bass.Bass("TRN2", target_bir_lowering=False)
    b = Bld(nc)

    def din(name, shape, dt=F32):
        return nc.dram_tensor(name, shape, dt, kind="ExternalInput")

    x_s = din("x_s", [TLOC, D])
    wq = din("wq", [D, D])
    wk = din("wk", [D, D])
    wv = din("wv", [D, D])
    wo = din("wo", [D, D])
    wg1 = din("wg1", [D, 128])
    wg2_i = din("wg2_i", [128, D])
    nw_t = din("nw_t", [128, 16])
    fb = din("fb", [1, D])
    c_t = din("c_t", [128, 16])
    wmod_s = din("wmod_s", [D, 1536])
    ffnw_s = din("ffnw_s", [256, D])
    cosx = din("cosx", [128, NCH * 64])
    sinx = din("sinx", [128, NCH * 64])
    dt_c = din("dt_c", [128, H * 128], BF16)
    dq_c = din("dq_c", [128, H * 128], BF16)
    dk_c = din("dk_c", [128, H])
    coef = din("coef", [128, 7 * H])
    idb = din("idb", [128, 128], BF16)
    idf = din("idf", [128, 128])
    gmask = din("gmask", [1, 1536])

    out_s = nc.dram_tensor("out_s", [TLOC, D], F32, kind="ExternalOutput")

    agA_in = nc.dram_tensor("agA_in", [1, 1792], F32)
    agA_out = nc.dram_tensor("agA_out", [NCORES, 1792], F32)
    agB_in = nc.dram_tensor("agB_in", [D, 256], BF16)
    agB_out = nc.dram_tensor("agB_out", [NCORES * D, 256], BF16)
    agL_in = [nc.dram_tensor(f"agL{g}_in", [128, GH * 128], F32) for g in range(NG)]
    agL_out = [nc.dram_tensor(f"agL{g}_out", [NCORES * 128, GH * 128], F32)
               for g in range(NG)]

    KB = 1024
    off_B = 19 * KB
    off_C = 44 * KB
    off_D = 76 * KB
    off_E = 108 * KB
    off_F = 172 * KB
    off_G = 178 * KB
    off_H = 184 * KB
    LIM = 204 * KB

    SB_BASE = (nc.sbuf_base + 31) // 32 * 32
    assert SB_BASE + LIM <= nc.sbuf_top, (SB_BASE, nc.sbuf_top)

    def sb_at(name, shape, dt, off):
        return nc.alloc_sbuf_tensor_at(name, shape, dt, offset=SB_BASE + off)

    cur = [0]

    def sbA(name, shape, dt):
        t = sb_at(name, shape, dt, cur[0])
        esz = {F32: 4, BF16: 2, I8: 1}[dt]
        cur[0] += (int(np.prod(shape[1:])) * esz + 31) // 32 * 32
        return t

    s_idb = sbA("s_idb", [128, 128], BF16)
    s_idf = sbA("s_idf", [128, 128], F32)
    s_cos = sbA("s_cos", [128, NCH * 64], F32)
    s_sin = sbA("s_sin", [128, NCH * 64], F32)
    s_dt = sbA("s_dt", [128, H * 128], BF16)
    s_dq = sbA("s_dq", [128, H * 128], BF16)
    s_dk = sbA("s_dk", [128, H], F32)
    s_coef = sbA("s_coef", [128, 7 * H], F32)
    s_modT = sbA("s_modT", [128, 96], F32)
    s_ca1 = sbA("s_ca1", [128, 16], F32)
    s_nw = sbA("s_nw", [128, 16], F32)
    s_ct = sbA("s_ct", [128, 16], F32)
    s_silc = sbA("s_silc", [128, 16], F32)
    s_ones = sbA("s_ones", [1, 128], F32)
    s_xsal = sbA("s_xsal", [128, NCH], F32)
    s_sd = sbA("s_sd", [128, 1], F32)
    s_rs = sbA("s_rs", [128, 1], F32)
    s_ms = sbA("s_ms", [128, GH], F32)
    s_rsg = sbA("s_rsg", [128, GH], F32)
    s_am = sbA("s_am", [128, 8], F32)
    s_wsc = sbA("s_wsc", [128, 8], F32)
    s_row = sbA("s_row", [1, 512], F32)
    s_mnat = sbA("s_mnat", [128, 128], F32)
    s_kc0 = sbA("s_kc0", [128, 1], F32)
    s_kc1 = sbA("s_kc1", [128, 1], F32)
    s_kc2 = sbA("s_kc2", [128, 1], F32)
    s_kcw1 = sbA("s_kcw1", [128, 1], F32)
    s_kcw2 = sbA("s_kcw2", [128, 1], F32)
    s_kcx1 = sbA("s_kcx1", [128, 1], F32)
    s_kcx2 = sbA("s_kcx2", [128, 1], F32)
    s_wrm = sbA("s_wrm", [128, 1], F32)
    s_am2 = sbA("s_am2", [128, 1], F32)
    s_rws = sbA("s_rws", [128, 1], F32)
    s_rsg2 = sbA("s_rsg2", [128, GH], F32)
    s_m8 = sbA("s_m8", [128, 8], F32)
    s_m8b = sbA("s_m8b", [128, 8], F32)
    s_z8 = sbA("s_z8", [128, 8], F32)
    s_ca_ = sbA("s_ca_", [128, 1], F32)
    s_cb_ = sbA("s_cb_", [128, 1], F32)
    s_cc_ = sbA("s_cc_", [128, 1], F32)
    s_cd_ = sbA("s_cd_", [128, 1], F32)
    s_ce_ = sbA("s_ce_", [128, 1], F32)
    s_cm_ = sbA("s_cm_", [128, 1], F32)
    s_cv_ = sbA("s_cv_", [128, 1], F32)
    s_qr1 = sbA("s_qr1", [128, 512], BF16)
    assert cur[0] <= off_B, f"A overflow {cur[0]}"

    s_x0 = sb_at("s_x0", [128, D], F32, off_B)
    s_x1 = sb_at("s_x1", [128, D], F32, off_B + 8 * KB)
    s_h0a = sb_at("s_h0a", [128, D], BF16, off_B + 16 * KB)
    s_h0b = sb_at("s_h0b", [128, D], BF16, off_B + 20 * KB)
    s_bc1 = sb_at("s_bc1", [128, 512], F32, off_B + 16 * KB)
    s_bc2 = sb_at("s_bc2", [128, 512], F32, off_B + 18 * KB)
    s_sil0 = sb_at("s_sil0", [128, 512], F32, off_B)
    s_sil1 = sb_at("s_sil1", [128, 512], F32, off_B + 2 * KB)
    s_rt0 = sb_at("s_rt0", [128, 256], F32, off_B + 4 * KB)
    s_rt1 = sb_at("s_rt1", [128, 256], F32, off_B + 5 * KB)
    s_rt2 = sb_at("s_rt2", [128, 256], F32, off_B + 6 * KB)
    s_rt3 = sb_at("s_rt3", [128, 256], F32, off_B + 7 * KB)
    s_qr0 = sb_at("s_qr0", [128, 512], BF16, off_B + 24 * KB)
    s_whA = sb_at("s_whA", [128, NKC * 512], BF16, off_B + 8 * KB)
    s_fw = sb_at("s_fw", [128, D], F32, off_D + 16 * KB)
    s_fi8 = sb_at("s_fi8", [128, D], I8, off_D + 24 * KB)
    s_fbf = sb_at("s_fbf", [128, D], BF16, off_D + 26 * KB)
    s_wqTs = sb_at("s_wqTs", [128, 16 * 256], BF16, off_H + 8 * KB)
    s_pt0 = sb_at("s_pt0", [128, 512], BF16, off_B)
    s_pt1 = sb_at("s_pt1", [128, 512], BF16, off_B + 1 * KB)
    s_qt0 = sb_at("s_qt0", [128, 512], BF16, off_B + 2 * KB)
    s_qt1 = sb_at("s_qt1", [128, 512], BF16, off_B + 3 * KB)
    s_oh0 = sb_at("s_oh0", [128, 1024], F32, off_B + 4 * KB)
    s_oh1 = sb_at("s_oh1", [128, 1024], F32, off_B + 8 * KB)
    s_gg = sb_at("s_gg", [128, 1024], F32, off_B + 12 * KB)
    s_sq = sb_at("s_sq", [128, 1024], F32, off_B + 16 * KB)
    s_on = sb_at("s_on", [128, 1024], BF16, off_B + 20 * KB)
    s_lgj = sb_at("s_lgj", [128, GH * 128], F32, off_B + 16 * KB)
    s_x2 = sb_at("s_x2", [128, D], F32, off_B + 16 * KB)
    s_x2b = sb_at("s_x2b", [128, D], F32, off_C + 16 * KB)
    s_x2c = sb_at("s_x2c", [128, D], F32, off_C + 24 * KB)
    s_rb0 = sb_at("s_rb0", [128, D], F32, off_B)
    s_rb1 = sb_at("s_rb1", [128, D], F32, off_B + 8 * KB)
    s_t1 = sb_at("s_t1", [128, 512], F32, off_B + 16 * KB)

    s_h1T = sb_at("s_h1T", [128, NKC * TLOC], BF16, off_C)
    s_onT2 = sb_at("s_onT2", [128, NCH * 1024], BF16, off_C)
    s_sgg2 = sb_at("s_sgg2", [128, NCH * 1024], BF16, off_C + 16 * KB)
    s_xqTb = sb_at("s_xqTb", [128, NCH * 1024], BF16, off_C)

    s_wh0 = sb_at("s_wh0", [128, NKC * 512], BF16, off_D)
    s_wh1 = sb_at("s_wh1", [128, NKC * 512], BF16, off_D + 16 * KB)
    s_onT1 = sb_at("s_onT1", [128, NCH * 1024], BF16, off_D)
    s_sgg1 = sb_at("s_sgg1", [128, NCH * 1024], BF16, off_D + 16 * KB)
    s_wp0 = sb_at("s_wp0", [128, NKC * 256], BF16, off_D + 16 * KB)
    s_wp1 = sb_at("s_wp1", [128, NKC * 256], BF16, off_D + 24 * KB)
    s_xqTa = sb_at("s_xqTa", [128, NCH * 1024], BF16, off_D)
    s_h2d = sb_at("s_h2d", [128, D], F32, off_D + 16 * KB)
    s_i8d = sb_at("s_i8d", [128, D], I8, off_D + 24 * KB)
    s_bfd = sb_at("s_bfd", [128, D], BF16, off_D + 26 * KB)
    s_bfd1 = sb_at("s_bfd1", [128, D], BF16, off_F)
    s_ob0 = sb_at("s_ob0", [128, D], F32, off_D + 16 * KB)
    s_ob1 = sb_at("s_ob1", [128, D], F32, off_D + 24 * KB)

    s_qT = sb_at("s_qT", [128, GH * TLOC], BF16, off_E)
    s_kT = sb_at("s_kT", [128, GH * TLOC], BF16, off_E + 16 * KB)
    s_k2 = sb_at("s_k2", [128, NCH * GH * 128], BF16, off_E + 32 * KB)
    s_vt = sb_at("s_vt", [128, NCH * GH * 128], BF16, off_E + 48 * KB)
    s_wmod = sb_at("s_wmod", [128, 8 * 1536], F32, off_E + 16 * KB)
    s_woP = sb_at("s_woP", [128, NKC * D], BF16, off_E)
    s_wqT = sb_at("s_wqT", [128, NKC * D], BF16, off_E)

    s_S = sb_at("s_S", [128, GH * 128], F32, off_F)
    s_Sb = sb_at("s_Sb", [128, GH * 128], BF16, off_F + 4 * KB)

    s_g1T = sb_at("s_g1T", [128, TLOC], BF16, off_G)
    s_wg2 = sb_at("s_wg2", [128, D], BF16, off_G + 2 * KB)

    s_modH = sb_at("s_modH", [1, 1536], F32, off_H)
    s_gmask = sb_at("s_gmask", [1, 1536], F32, off_D)
    s_modH2 = sb_at("s_modH2", [1, 1536], F32, off_D + 8 * KB)
    s_mcA = sb_at("s_mcA", [1, 512], F32, off_B)
    s_mcB = sb_at("s_mcB", [1, 512], F32, off_B + 2048)
    s_mcC = sb_at("s_mcC", [1, 512], F32, off_B + 4096)
    s_gab = sb_at("s_gab", [128, D], BF16, off_H)
    s_cb1b = sb_at("s_cb1b", [128, D], BF16, off_H + 4 * KB)
    s_sbb = sb_at("s_sbb", [128, D], BF16, off_H + 8 * KB)
    s_wsgb = sb_at("s_wsgb", [128, D], BF16, off_H + 12 * KB)
    s_bgbb = sb_at("s_bgbb", [128, D], BF16, off_H + 16 * KB)
    assert off_H + 20 * KB <= LIM

    pb = [nc.alloc_psum_tensor(f"pb{i}", [128, 512], F32) for i in range(6)]
    pt = [nc.alloc_psum_tensor(f"pt{i}", [128, 1024], BF16) for i in range(2)]

    for name in ("T", "V", "S", "G", "DG", "DS", "DA", "CC"):
        b.sems[name] = nc.alloc_semaphore(name=name)

    MMK = dict(start=True, stop=True, skip_group_check=True)


    end_taps = []

    def tap(name, t_sb):
        if name.startswith("end_"):
            if name in DEBUG_TAPS:
                end_taps.append((name, t_sb))
            return
        if name not in DEBUG_TAPS:
            return
        shp = [int(x) for x in t_sb.shape]
        dr = nc.dram_tensor("tap_" + name, shp, t_sb.dtype, kind="ExternalOutput")
        b.ds(lambda e, dr=dr, t_sb=t_sb: e.dma_start(dr[:, :], t_sb[:, :]),
             waits=b.bar())


    tk_const = {}
    for nm, t_sb, t_dr in (("idb", s_idb, idb), ("idf", s_idf, idf),
                           ("cos", s_cos, cosx), ("sin", s_sin, sinx),
                           ("dt", s_dt, dt_c), ("dq", s_dq, dq_c),
                           ("dk", s_dk, dk_c), ("coef", s_coef, coef),
                           ("nw", s_nw, nw_t), ("ct", s_ct, c_t)):
        tk_const[nm] = b.ds(lambda e, a=t_sb, c=t_dr: e.dma_start(a[:, :], c[:, :]))
    tk_ones = b.v(lambda e: e.memset(s_ones[:, :], 1.0))
    tk_kc = b.v(lambda e: [e.memset(s_z8[:, :], 0.0),
                           e.memset(s_kcw1[:, :], -1e-8),
                           e.memset(s_kcw2[:, :], 1e-8 / 127.0),
                           e.memset(s_kcx1[:, :], -1e-12),
                           e.memset(s_kcx2[:, :], 1e-12),
                           e.memset(s_kc0[:, :], float(LN_EPS)),
                           e.memset(s_kc1[:, :], float(RMS_EPS)),
                           e.memset(s_kc2[:, :], 1.0 / 128.0)][-1])
    def ld_wm(e, hh):
        srcap = AP(wmod_s, hh * 8 * 128 * 1536,
                   [[1536, 128], [1536 * 128, 8], [1, 1536]])
        dst = AP(s_wmod, 0, [[8 * 1536, 128], [1536, 8], [1, 1536]])
        return e.dma_start(dst, srcap)
    tk_wm0 = b.dg(lambda e: ld_wm(e, 0))
    tk_silc = b.s(lambda e: e.activation(s_silc[:, :], s_ct[:, :], AF.Silu),
                  waits=[tk_const["ct"]])

    fq_last_v = None
    fq_last_i = None
    tk_ws_done = {}
    for tile in range(2):
        tk_ldf = b.da(lambda e, tile=tile: e.dma_start(
            s_fw[:, :], AP(ffnw_s, tile * 128 * D, [[D, 128], [1, D]])),
            waits=[fq_last_i])
        tk_a = b.v(lambda e: e.reduce_max(s_am[:, 0:1], s_fw[:, :], axis=AX.X,
                                          apply_absolute_value=True),
                   waits=[tk_ldf])
        tk_w1 = b.s(lambda e: e.activation(s_rws[:, :], s_am[:, 0:1], AF.Relu,
                                           bias=s_kcw1[:, 0:1]),
                    waits=[tk_a, tk_kc])
        tk_w2 = b.s(lambda e, tile=tile: e.activation(
            s_wsc[:, tile:tile + 1], s_rws[:, :], AF.Identity,
            bias=s_kcw2[:, 0:1], scale=1.0 / 127.0), waits=[tk_w1])
        tk_ws = b.v(lambda e, tile=tile: e.reciprocal(
            s_am2[:, 0:1], s_wsc[:, tile:tile + 1]), waits=[tk_w2])
        tk_ws_done[tile] = tk_w2
        tk_q = b.v(lambda e: e.tensor_scalar(
            s_fw[:, :], s_fw[:, :], s_am2[:, 0:1], None, ALU.mult),
            waits=[tk_ws, fq_last_i])
        tk_i = b.v(lambda e: e.tensor_copy(s_fi8[:, :], s_fw[:, :]),
                   waits=[tk_q])
        fq_last_i = tk_i
        tk_bf = b.s(lambda e: e.activation(s_fbf[:, :], s_fi8[:, :], AF.Copy),
                    waits=[tk_i])
        for half in range(2):
            def trf(e, half=half):
                last = None
                for kk in range(8):
                    last = e.transpose(pt[0][:, kk * 128:(kk + 1) * 128],
                                       s_fbf[:, (half * 8 + kk) * 128:
                                             (half * 8 + kk + 1) * 128],
                                       s_idb[:, :])
                return last
            tk_trf = b.t(trf, waits=[tk_bf, tk_const["idb"], fq_last_v])
            def evf(e, tile=tile, half=half):
                dst = AP(s_wqTs, (half * 8) * 256 + tile * 128,
                         [[16 * 256, 128], [256, 8], [1, 128]])
                return e.tensor_copy(dst, AP(pt[0], 0,
                                             [[1024, 128], [128, 8], [1, 128]]))
            fq_last_v = b.v(evf, waits=[tk_trf])
    tk_wqTs_done = fq_last_v

    tk_wssave = [b.dg(lambda e, tile=tile: e.dma_start(
        AP(agA_in, 1536 + tile * 128, [[1, 128], [1, 1]]),
        AP(s_wsc, tile, [[8, 128], [1, 1]])), waits=[tk_ws_done[tile]])
        for tile in range(2)]
    tk_wqsave = b.dg(lambda e: e.dma_start(
        AP(agB_in, 0, [[256, 128], [256 * 128, 16], [1, 256]]),
        AP(s_wqTs, 0, [[16 * 256, 128], [256, 16], [1, 256]])),
        waits=[tk_wqTs_done])

    tk_warm0 = b.s(lambda e: e.activation(s_wrm[:, :], s_kc0[:, 0:1], AF.Sqrt),
                   waits=[tk_kc, tk_silc])
    h0tok = {}
    prev_sd = None
    prev_rs_read = None
    tk_ev_bank = {0: tk_wqTs_done}
    tok_h1T_plain = None
    prev_pair_tr = None
    prev_mth = None
    for pair in range(NCH // 2):
        for j in range(2):
            tc = pair * 2 + j
            buf = s_x0 if tc % 2 == 0 else s_x1
            h0 = s_h0a if j == 0 else s_h0b
            tk_x = b.ds(lambda e, tc=tc, buf=buf: e.dma_start(
                buf[:, :], AP(x_s, tc * 128 * D, [[D, 128], [1, D]])),
                waits=[h0tok.get(tc - 2)])
            tk_sqd = b.s(lambda e, buf=buf, h0=h0: e.activation(
                h0[:, :], buf[:, :], AF.Square),
                waits=[tk_x, prev_mth, prev_pair_tr, tk_warm0])
            tk_sum = b.v(lambda e, buf=buf: e.reduce_sum(
                s_ca_[:, :], buf[:, :], axis=AX.X),
                waits=[tk_x, prev_mth, prev_pair_tr])
            tk_sq2 = b.v(lambda e, h0=h0: e.reduce_sum(
                s_cb_[:, :], h0[:, :], axis=AX.X), waits=[tk_sqd])
            def mth(e):
                e.mul(s_cm_[:, :], s_ca_[:, 0:1], 1.0 / 2048.0)
                e.mul(s_cd_[:, :], s_cb_[:, 0:1], 1.0 / 2048.0)
                e.activation(s_ce_[:, :], s_cm_[:, 0:1], AF.Square)
                e.activation(s_cv_[:, :], s_ce_[:, 0:1], AF.Identity,
                             scale=-1.0, bias=s_cd_[:, 0:1])
                return e.activation(s_sd[:, :], s_cv_[:, 0:1], AF.Sqrt,
                                    bias=s_kc0[:, 0:1])
            tk_sd = b.s(mth, waits=[tk_sum, tk_sq2, prev_sd, prev_rs_read,
                                    tk_kc])
            prev_mth = tk_sd
            prev_sd = tk_sd
            tk_rs = b.v(lambda e: e.reciprocal(s_rs[:, :], s_sd[:, :]),
                        waits=[tk_sd])
            h0tok[tc] = b.v(lambda e, buf=buf, h0=h0: e.tensor_scalar(
                h0[:, :], buf[:, :], s_cm_[:, 0:1], s_rs[:, :],
                ALU.subtract, ALU.mult), waits=[tk_rs])
            prev_rs_read = h0tok[tc]
        tca = pair * 2
        for kk in range(NKC):
            bank = pt[kk % 2]
            def tr2(e, kk=kk, bank=bank):
                e.transpose(bank[:, 0:128], s_h0a[:, kk * 128:(kk + 1) * 128],
                            s_idb[:, :])
                return e.transpose(bank[:, 128:256],
                                   s_h0b[:, kk * 128:(kk + 1) * 128], s_idb[:, :])
            tk_tr = b.t(tr2, waits=[h0tok[tca], h0tok[tca + 1], tk_const["idb"],
                                    tk_ev_bank.get(kk % 2)])
            tk_ev = b.s(lambda e, kk=kk, bank=bank, tca=tca: e.activation(
                s_h1T[:, kk * TLOC + tca * 128:kk * TLOC + tca * 128 + 256],
                bank[:, 0:256], AF.Copy), waits=[tk_tr])
            tk_ev_bank[kk % 2] = tk_ev
            tok_h1T_plain = tk_ev
            prev_pair_tr = tk_tr
    bar_P0 = b.bar()

    b.gate = bar_P0
    tk_modmm = []
    for nt in range(3):
        def mmod0(e, nt=nt):
            last = None
            for kl in range(8):
                last = e.matmul(pb[nt][0:1, :], s_silc[:, kl:kl + 1],
                                s_wmod[:, kl * 1536 + nt * 512:
                                       kl * 1536 + (nt + 1) * 512],
                                start=(kl == 0), stop=False)
            return last
        tk_modmm.append(b.t(mmod0, waits=[tk_silc, tk_wm0]))
    tk_wm1 = b.dg(lambda e: ld_wm(e, 1), waits=[tk_modmm[-1]])
    for nt in range(3):
        def mmod1(e, nt=nt):
            last = None
            for kl in range(8):
                last = e.matmul(pb[nt][0:1, :], s_silc[:, 8 + kl:9 + kl],
                                s_wmod[:, kl * 1536 + nt * 512:
                                       kl * 1536 + (nt + 1) * 512],
                                start=False, stop=(kl == 7))
            return last
        tk_modmm[nt] = b.t(mmod1, waits=[tk_wm1])
    tk_gmask = b.ds(lambda e: e.dma_start(s_gmask[:, :], gmask[:, :]))
    tk_modev = []
    for nt in range(3):
        def mev(e, nt=nt):
            ps = pb[nt][0:1, :]
            mh = s_modH[0:1, nt * 512:(nt + 1) * 512]
            mh2 = s_modH2[0:1, nt * 512:(nt + 1) * 512]
            mk = s_gmask[0:1, nt * 512:(nt + 1) * 512]
            e.tensor_scalar(s_mcA[:, :], ps, 0.0, None, ALU.is_ge)
            e.tensor_scalar(s_mcB[:, :], s_mcA[:, :], 2.0, -1.0,
                            ALU.mult, ALU.add)
            e.tensor_scalar(s_mcA[:, :], ps, 0.0, None, ALU.max)
            e.scalar_tensor_tensor(s_mcC[:, :], ps, -1.0, s_mcA[:, :],
                                   ALU.mult, ALU.max)
            e.tensor_scalar(s_mcA[:, :], s_mcC[:, :], 0.25, None, ALU.max)
            e.tensor_tensor(s_mcC[:, :], s_mcB[:, :], s_mcA[:, :], ALU.mult)
            e.tensor_copy(mh, ps)
            e.tensor_tensor(s_mcA[:, :], s_mcC[:, :], mh, ALU.subtract)
            e.tensor_tensor(s_mcB[:, :], s_mcA[:, :], mk, ALU.mult)
            return e.tensor_tensor(mh2, mh, s_mcB[:, :], ALU.add)
        tk_modev.append(b.v(mev, waits=[tk_modmm[nt], tk_gmask]))
    tk_modsave = b.dg(lambda e: e.dma_start(agA_in[0:1, 0:1536], s_modH2[0:1, :]),
                      waits=tk_modev)

    tk_agA = b.cc(lambda e: e.collective_compute(
        "AllGather", ALU.bypass, replica_groups=[list(range(NCORES))],
        ins=[agA_in.ap().opt()], outs=[agA_out.ap().opt()]),
        waits=[tk_modsave, tk_wqsave] + tk_wssave)

    tk_mn = [b.ds(lambda e, r=r: e.dma_start(
        AP(s_mnat, r * 12 * 128, [[128, 12], [1, 128]]),
        AP(agA_out, r * 1792, [[128, 12], [1, 128]])), waits=[tk_agA])
        for r in range(NCORES)]
    tk_modT_t = b.t(lambda e: e.transpose(pb[4][:, 0:96],
                                          AP(s_mnat, 0, [[128, 96], [1, 128]]),
                                          s_idf[0:96, 0:96]),
                    waits=tk_mn + [tk_const["idf"], tk_modmm[2]])
    tk_modT = b.v(lambda e: e.tensor_copy(s_modT[:, :], pb[4][:, 0:96]),
                  waits=[tk_modT_t])
    tk_ca1 = b.v(lambda e: e.tensor_scalar(s_ca1[:, :], s_modT[:, 16:32], 1.0,
                                           None, ALU.add), waits=[tk_modT])

    tk_h1m = []
    for k in range(NKC):
        tk_h1m.append(b.v(lambda e, k=k: e.tensor_scalar(
            s_h1T[:, k * TLOC:(k + 1) * TLOC], s_h1T[:, k * TLOC:(k + 1) * TLOC],
            s_ca1[:, k:k + 1], s_modT[:, k:k + 1], ALU.mult, ALU.add),
            waits=[tk_ca1, tok_h1T_plain]))
    tok_h1T = tk_h1m[-1]
    tap("modT", s_modT)
    tap("ca1", s_ca1)
    tap("h1T", s_h1T)



    def qkv_panel(g, which, panel_tok, panelbuf, ncols, col0, prev,
                  rot_gp=True):
        nheads = ncols // 128
        h0l = col0 // 128
        dstT = s_qT if which == "q" else s_kT
        hist = {}
        rot_hist = {}
        trq_hist = {}
        prev = prev or {"ev": None, "hist": {}, "rot": {}, "trq": {}}
        ev_cell = [prev["ev"]]
        pend = [None]

        def histw(tc):
            return hist[tc] if tc >= 0 else prev["hist"].get(tc + 8)

        def rotw(tc):
            return rot_hist[tc] if tc >= 0 else prev["rot"].get(tc + 8, ())

        def trqw(tc):
            return trq_hist[tc] if tc >= 0 else prev["trq"].get(tc + 8)

        def emit_trq(tc, qr, rot_done):
            def trq(e, qr=qr, nheads=nheads):
                last = None
                for hh in range(nheads):
                    last = e.transpose(pt[1][:, hh * 128:(hh + 1) * 128],
                                       qr[:, hh * 128:(hh + 1) * 128],
                                       s_idb[:, :])
                return last
            tk_trq = b.t(trq, waits=list(rot_done) + [ev_cell[0]])
            trq_hist[tc] = tk_trq
            def evq(e, tc=tc, h0l=h0l, nheads=nheads, dstT=dstT):
                dst = AP(dstT, h0l * TLOC + tc * 128,
                         [[GH * TLOC, 128], [TLOC, nheads], [1, 128]])
                return e.tensor_copy(dst, AP(pt[1], 0, [[1024, 128],
                                                        [128, nheads],
                                                        [1, 128]]))
            ev_cell[0] = b.v(evq, waits=[tk_trq])

        for tc in range(NCH):
            bank = pb[tc % 2]
            def mmf(e, tc=tc, panelbuf=panelbuf, ncols=ncols, bank=bank):
                last = None
                for k in range(NKC):
                    last = e.matmul(
                        bank[:, 0:ncols],
                        s_h1T[:, k * TLOC + tc * 128:k * TLOC + tc * 128 + 128],
                        panelbuf[:, k * ncols:(k + 1) * ncols],
                        start=(k == 0), stop=(k == NKC - 1))
                return last
            tk_mm = b.t(mmf, waits=[panel_tok, tok_h1T, histw(tc - 2)])
            sil = s_sil0 if tc % 2 == 0 else s_sil1
            qr = s_qr0 if tc % 2 == 0 else s_qr1
            if which in ("q", "k"):
                tk_sil = b.s(lambda e, bank=bank, sil=sil, ncols=ncols:
                             e.activation(sil[:, 0:ncols], bank[:, 0:ncols],
                                          AF.Silu),
                             waits=[tk_mm] + list(rotw(tc - 2)))
                hist[tc] = tk_sil
                ca = AP(s_cos, tc * 64, [[NCH * 64, 128], [0, nheads], [1, 64]])
                sa = AP(s_sin, tc * 64, [[NCH * 64, 128], [0, nheads], [1, 64]])
                x1 = AP(sil, 0, [[512, 128], [128, nheads], [1, 64]])
                x2 = AP(sil, 64, [[512, 128], [128, nheads], [1, 64]])
                r1 = AP(qr, 0, [[512, 128], [128, nheads], [1, 64]])
                r2 = AP(qr, 64, [[512, 128], [128, nheads], [1, 64]])
                ta = AP(s_rt0 if tc % 2 == 0 else s_rt1, 0,
                        [[256, 128], [64, nheads], [1, 64]])
                tb = AP(s_rt2 if tc % 2 == 0 else s_rt3, 0,
                        [[256, 128], [64, nheads], [1, 64]])
                tk_r1 = b.v(lambda e, x1=x1, x2=x2, ta=ta, r1=r1, ca=ca, sa=sa: [
                    e.tensor_tensor(ta, x2, sa, ALU.mult),
                    e.tensor_tensor(r1, x1, ca, ALU.mult)][-1],
                    waits=[tk_sil, tk_const["cos"], tk_const["sin"],
                           trqw(tc - 2)])
                tk_r1b = b.v(lambda e, r1=r1, ta=ta: e.tensor_tensor(
                    r1, r1, ta, ALU.subtract), waits=[tk_r1])
                eng2 = b.g if rot_gp else b.v
                tk_r2 = eng2(lambda e, x1=x1, tb=tb, sa=sa: e.tensor_tensor(
                    tb, x1, sa, ALU.mult),
                    waits=[tk_sil, trqw(tc - 2)])
                tk_r2b = eng2(lambda e, r2=r2, x2=x2, ca=ca, tb=tb: [
                    e.tensor_tensor(r2, x2, ca, ALU.mult),
                    e.tensor_tensor(r2, r2, tb, ALU.add)][-1], waits=[tk_r2])
                rot_done = (tk_r1b, tk_r2b)
                rot_hist[tc] = rot_done
                if which == "k":
                    dkap = AP(s_dk, g * GH + h0l, [[H, 128], [1, nheads], [0, 128]])
                    qrap = AP(qr, 0, [[512, 128], [128, nheads], [1, 128]])
                    k2ap = AP(s_k2, tc * 1024 + h0l * 128,
                              [[NCH * GH * 128, 128], [128, nheads], [1, 128]])
                    b.v(lambda e, k2ap=k2ap, qrap=qrap, dkap=dkap: e.tensor_tensor(
                        k2ap, qrap, dkap, ALU.mult),
                        waits=list(rot_done) + [tk_const["dk"]])
                if pend[0] is not None:
                    emit_trq(*pend[0])
                pend[0] = (tc, qr, rot_done)
            else:
                def evv(e, bank=bank, tc=tc, h0l=h0l, ncols=ncols):
                    return e.copy(
                        s_vt[:, tc * 1024 + h0l * 128:tc * 1024 + h0l * 128 + ncols],
                        bank[:, 0:ncols])
                hist[tc] = b.s(evv, waits=[tk_mm])
                ev_cell[0] = hist[tc]
        if pend[0] is not None:
            emit_trq(*pend[0])
        return {"ev": ev_cell[0], "hist": hist, "rot": rot_hist,
                "trq": trq_hist}

    def emit_group(g, mode, bufs=None):
        ncols = 512 if mode == "a" else 256
        if bufs is None:
            bufs = (s_wh0, s_wh1) if mode == "a" else (s_wp0, s_wp1)
        npan = 1024 // ncols
        order = [("q", wq), ("k", wk), ("v", wv)]
        panlist = [(which, wsrc, p) for which, wsrc in order for p in range(npan)]
        panel_res = {}
        prev = None
        for i, (which, wsrc, p) in enumerate(panlist):
            buf = bufs[i % 2]
            def ldp(e, wsrc=wsrc, p=p, ncols=ncols, buf=buf):
                src = AP(wsrc, g * 1024 + p * ncols,
                         [[D, 128], [D * 128, NKC], [1, ncols]])
                dst = AP(buf, 0, [[NKC * ncols, 128], [ncols, NKC], [1, ncols]])
                return e.dma_start(dst, src)
            pv = panel_res.get(i - 2)
            tk_ld = b.dg(ldp, waits=[pv["hist"].get(NCH - 1)] if pv else [])
            panel_res[i] = qkv_panel(g, which, tk_ld, buf, ncols, p * ncols,
                                     prev)
            prev = panel_res[i]
        last = panel_res[len(panlist) - 1]
        return last["ev"] or last["hist"][NCH - 1]

    def emit_g1():
        tk_ld = b.dg(lambda e: e.dma_start(
            AP(s_wh0, 0, [[NKC * 512, 128], [128, NKC], [1, 128]]),
            AP(wg1, 0, [[128, 128], [128 * 128, NKC], [1, 128]])))
        tk_ld2 = b.dg(lambda e: e.dma_start(s_wg2[:, :], wg2_i[:, :]))
        last = None
        for tt in range(2):
            def mmf(e, tt=tt):
                lastx = None
                for k in range(NKC):
                    lastx = e.matmul(
                        pb[4][:, :], s_wh0[:, k * 128:(k + 1) * 128],
                        s_h1T[:, k * TLOC + tt * 512:k * TLOC + (tt + 1) * 512],
                        start=(k == 0), stop=(k == NKC - 1))
                return lastx
            tk_mm = b.t(mmf, waits=[tk_ld, tok_h1T, last])
            last = b.v(lambda e, tt=tt: e.tensor_copy(
                s_g1T[:, tt * 512:(tt + 1) * 512], pb[4][:, :]), waits=[tk_mm])
        return last, tk_ld2

    def emit_L(g):
        tk_z = b.v(lambda e: e.memset(s_S[:, :], 0.0))
        last_stt = {0: tk_z, 1: tk_z}
        for c in range(NCH):
            for hb in range(2):
                bank = pb[2 + hb]
                def umm(e, c=c, hb=hb, bank=bank):
                    last = None
                    for q in range(4):
                        o = c * 1024 + (hb * 4 + q) * 128
                        last = e.matmul(bank[:, q * 128:(q + 1) * 128],
                                        s_k2[:, o:o + 128], s_vt[:, o:o + 128],
                                        **MMK)
                    return last
                tk_u = b.t(umm, waits=[last_stt[hb]])
                def sttf(e, c=c, hb=hb, bank=bank, g=g):
                    last = None
                    for q in range(4):
                        hl = hb * 4 + q
                        last = e.scalar_tensor_tensor(
                            s_S[:, hl * 128:(hl + 1) * 128],
                            s_S[:, hl * 128:(hl + 1) * 128], LAMC[g * GH + hl],
                            bank[:, q * 128:(q + 1) * 128], ALU.mult, ALU.add)
                    return last
                last_stt[hb] = b.v(sttf, waits=[tk_u])
        tk_save = b.dg(lambda e, g=g: e.dma_start(agL_in[g][:, :], s_S[:, :]),
                       waits=[last_stt[0], last_stt[1]])
        tk_ag = b.cc(lambda e, g=g: e.collective_compute(
            "AllGather", ALU.bypass, replica_groups=[list(range(NCORES))],
            ins=[agL_in[g].ap().opt()], outs=[agL_out[g].ap().opt()]),
            waits=[tk_save])
        return tk_ag

    def emit_L_comb(g, tk_ag, extra):
        last = tk_ag
        for j in range(7):
            tk_ld = b.ds(lambda e, j=j, g=g: e.dma_start(
                s_lgj[:, :], AP(agL_out[g], j * 128 * (GH * 128),
                                [[GH * 128, 128], [1, GH * 128]])),
                waits=[last] + (extra if j == 0 else []))
            def comb(e, j=j, g=g):
                lastx = None
                for hl in range(GH):
                    col = j * H + g * GH + hl
                    if j == 0:
                        lastx = e.tensor_scalar(
                            s_S[:, hl * 128:(hl + 1) * 128],
                            s_lgj[:, hl * 128:(hl + 1) * 128],
                            s_coef[:, col:col + 1], None, ALU.mult)
                    else:
                        lastx = e.scalar_tensor_tensor(
                            s_S[:, hl * 128:(hl + 1) * 128],
                            s_lgj[:, hl * 128:(hl + 1) * 128],
                            s_coef[:, col:col + 1],
                            s_S[:, hl * 128:(hl + 1) * 128], ALU.mult, ALU.add)
                return lastx
            last = b.v(comb, waits=[tk_ld, tk_const["coef"]])
        tk_sb = b.v(lambda e: e.tensor_copy(s_Sb[:, :], s_S[:, :]),
                    waits=[last])
        return tk_sb

    def emit_scan(g, tok_Sb, onT, sgg, tok_g1, tok_wg2):
        tk_gate = {}
        prev = None
        for c in range(NCH):
            for half in range(2):
                bank = pb[4 + ((c * 2 + half) % 2)]
                def gmm(e, c=c, half=half, bank=bank):
                    return e.matmul(
                        bank[:, :], s_g1T[:, c * 128:(c + 1) * 128],
                        s_wg2[:, g * 1024 + half * 512:g * 1024 + (half + 1) * 512],
                        **MMK)
                tk_mm = b.t(gmm, waits=[tok_g1, tok_wg2, prev])
                prev = b.s(lambda e, c=c, half=half, bank=bank, sgg=sgg:
                           e.activation(
                               sgg[:, c * 1024 + half * 512:
                                   c * 1024 + (half + 1) * 512],
                               bank[:, :], AF.Sigmoid), waits=[tk_mm])
                tk_gate[(c, half)] = prev
        prev_sb = tok_Sb
        prev_onT = None
        prev_pt = {0: None, 1: None}
        prev_oev = {0: None, 1: None}
        prev_stt = {0: tk_gate[(NCH - 1, 1)], 1: tk_gate[(NCH - 1, 1)]}
        prev_omm = {0: None, 1: None}
        for c in range(NCH):
            oev = []
            stts = []
            for hb in range(2):
                psc = pb[hb]
                po = pb[2 + hb]
                pu = pb[4 + hb]
                hg0 = g * GH + hb * 4
                def smm(e, c=c, hb=hb, psc=psc):
                    last = None
                    for q in range(4):
                        o = (hb * 4 + q) * TLOC + c * 128
                        last = e.matmul(psc[:, q * 128:(q + 1) * 128],
                                        s_kT[:, o:o + 128], s_qT[:, o:o + 128],
                                        **MMK)
                    return last
                tk_sc = b.t(smm, waits=[prev_pt[hb]])
                pts = s_pt0 if hb == 0 else s_pt1
                qts = s_qt0 if hb == 0 else s_qt1
                tk_pt = b.v(lambda e, psc=psc, pts=pts, hg0=hg0: e.tensor_tensor(
                    pts[:, :], psc[:, :], s_dt[:, hg0 * 128:(hg0 + 4) * 128],
                    ALU.mult), waits=[tk_sc, tk_const["dt"]])
                prev_pt[hb] = tk_pt
                tk_qt = b.g(lambda e, c=c, hb=hb, qts=qts, hg0=hg0:
                            e.tensor_tensor(
                                qts[:, :],
                                AP(s_qT, (hb * 4) * TLOC + c * 128,
                                   [[GH * TLOC, 128], [TLOC, 4], [1, 128]]),
                                AP(s_dq, hg0 * 128,
                                   [[H * 128, 128], [128, 4], [1, 128]]),
                                ALU.mult),
                            waits=[tk_const["dq"], prev_omm[hb]])
                def omm(e, c=c, hb=hb, po=po, pts=pts, qts=qts):
                    last = None
                    for q in range(4):
                        hl = hb * 4 + q
                        o = c * 1024 + hl * 128
                        e.matmul(po[:, q * 128:(q + 1) * 128],
                                 pts[:, q * 128:(q + 1) * 128],
                                 s_vt[:, o:o + 128],
                                 start=True, stop=False, skip_group_check=True)
                        last = e.matmul(po[:, q * 128:(q + 1) * 128],
                                        qts[:, q * 128:(q + 1) * 128],
                                        s_Sb[:, hl * 128:(hl + 1) * 128],
                                        start=False, stop=True,
                                        skip_group_check=True)
                    return last
                tk_o = b.t(omm, waits=[tk_pt, tk_qt, prev_sb, prev_oev[hb]])
                prev_omm[hb] = tk_o
                def umm(e, c=c, hb=hb, pu=pu):
                    last = None
                    for q in range(4):
                        o = c * 1024 + (hb * 4 + q) * 128
                        last = e.matmul(pu[:, q * 128:(q + 1) * 128],
                                        s_k2[:, o:o + 128], s_vt[:, o:o + 128],
                                        **MMK)
                    return last
                tk_u = b.t(umm, waits=[prev_stt[hb]])
                oh = s_oh0 if c % 2 == 0 else s_oh1
                oev.append(b.s(lambda e, po=po, hb=hb, oh=oh: e.copy(
                    oh[:, hb * 512:(hb + 1) * 512], po[:, :]), waits=[tk_o]))
                prev_oev[hb] = oev[-1]
                def sttf(e, c=c, hb=hb, pu=pu, g=g):
                    last = None
                    for q in range(4):
                        hl = hb * 4 + q
                        last = e.scalar_tensor_tensor(
                            s_S[:, hl * 128:(hl + 1) * 128],
                            s_S[:, hl * 128:(hl + 1) * 128], LAMC[g * GH + hl],
                            pu[:, q * 128:(q + 1) * 128], ALU.mult, ALU.add)
                    return last
                stts.append(b.v(sttf, waits=[tk_u, tk_o]))
                prev_stt[hb] = stts[-1]
            prev_sb = b.s(lambda e: e.activation(s_Sb[:, :], s_S[:, :], AF.Copy),
                          waits=stts)
            oh = s_oh0 if c % 2 == 0 else s_oh1
            tk_gg = b.v(lambda e, oh=oh, c=c, sgg=sgg: e.tensor_tensor(
                s_gg[:, :], oh[:, :], sgg[:, c * 1024:(c + 1) * 1024], ALU.mult),
                waits=oev + [tk_gate[(c, 1)]])
            tk_sq = b.g(lambda e: e.tensor_tensor(s_sq[:, :], s_gg[:, :],
                                                  s_gg[:, :], ALU.mult),
                        waits=[tk_gg])
            tk_ms = b.v(lambda e: e.reduce_sum(
                AP(s_ms, 0, [[GH, 128], [1, GH]]),
                AP(s_sq, 0, [[1024, 128], [128, GH], [1, 128]]), axis=AX.X),
                waits=[tk_sq])
            tk_sdg = b.s(lambda e: e.activation(s_rsg[:, :], s_ms[:, :], AF.Sqrt,
                                                bias=s_kc1[:, 0:1],
                                                scale=s_kc2[:, 0:1]),
                         waits=[tk_ms])
            tk_rsg = b.v(lambda e: e.reciprocal(s_rsg2[:, :], s_rsg[:, :]),
                         waits=[tk_sdg])
            tk_on = b.v(lambda e: e.tensor_tensor(
                s_on[:, :], AP(s_gg, 0, [[1024, 128], [128, GH], [1, 128]]),
                AP(s_rsg2, 0, [[GH, 128], [1, GH], [0, 128]]), ALU.mult),
                waits=[tk_rsg])
            if c == 0:
                tap(f"oh{g}", s_oh0)
                tap(f"gg{g}", s_gg)
                tap(f"rsg{g}", s_rsg2)
                tap(f"pt{g}", s_pt0)
                tap(f"on{g}", s_on)
            def tro(e):
                last = None
                for kl in range(GH):
                    last = e.transpose(pt[0][:, kl * 128:(kl + 1) * 128],
                                       s_on[:, kl * 128:(kl + 1) * 128],
                                       s_idb[:, :])
                return last
            tk_tro = b.t(tro, waits=[tk_on, prev_onT])
            prev_onT = b.s(lambda e, c=c, onT=onT: e.copy(
                onT[:, c * 1024:(c + 1) * 1024], pt[0][:, :]), waits=[tk_tro])
        return prev_onT

    tap("wqTs", s_wqTs)

    def pld(g, wsrc, p, buf, waits):
        def ldp(e, wsrc=wsrc, p=p, buf=buf, g=g):
            src = AP(wsrc, g * 1024 + p * 512,
                     [[D, 128], [D * 128, NKC], [1, 512]])
            dst = AP(buf, 0, [[NKC * 512, 128], [512, NKC], [1, 512]])
            return e.dma_start(dst, src)
        return b.dg(ldp, waits=waits)

    def emit_grp(g, bufA, bufB):
        b.gate = b.bar()
        rK0 = qkv_panel(g, "k", pld(g, wk, 0, bufA, []), bufA, 512, 0, None)
        rK1 = qkv_panel(g, "k", pld(g, wk, 1, bufB, []), bufB, 512, 512, rK0)
        rV0 = qkv_panel(g, "v", pld(g, wv, 0, bufA, [rK0["hist"][NCH - 1]]),
                        bufA, 512, 0, rK1)
        rV1 = qkv_panel(g, "v", pld(g, wv, 1, bufB, [rK1["hist"][NCH - 1]]),
                        bufB, 512, 512, rV0)
        qld0 = pld(g, wq, 0, bufA, [rV0["hist"][NCH - 1]])
        qld1 = pld(g, wq, 1, bufB, [rV1["hist"][NCH - 1]])
        b.gate = b.bar()
        tk_ag = emit_L(g)
        rQ0 = qkv_panel(g, "q", qld0, bufA, 512, 0, rV1, rot_gp=False)
        rQ1 = qkv_panel(g, "q", qld1, bufB, 512, 512, rQ0, rot_gp=False)
        tok_S = emit_L_comb(g, tk_ag, [rQ0["hist"][NCH - 1],
                                       rQ1["hist"][NCH - 1]])
        return tok_S, rQ1

    tok_S0, _ = emit_grp(0, s_wh0, s_wh1)
    tk_agB = b.cc(lambda e: e.collective_compute(
        "AllGather", ALU.bypass, replica_groups=[list(range(NCORES))],
        ins=[agB_in.ap().opt()], outs=[agB_out.ap().opt()]), waits=[tk_wqsave])
    b.gate = b.bar()
    tok_g1T, tok_wg2 = emit_g1()
    tap("qT0", s_qT)
    tap("kT0", s_kT)
    tap("k20", s_k2)
    tap("vt0", s_vt)
    tap("g1T", s_g1T)
    tap("S0", s_S)
    b.gate = b.bar()
    tok_onT1 = emit_scan(0, tok_S0, s_onT1, s_sgg1, tok_g1T, tok_wg2)
    bar_scan0 = [t for t in b.bar() if t[0] in ("T", "V", "S", "G")]

    tap("onT1", s_onT1)
    tap("sgg1", s_sgg1)

    tok_S1, _ = emit_grp(1, s_whA, s_wh1)

    b.gate = bar_scan0

    bc_mm_prev = [None]
    bc_post_prev = {0: None, 1: None}

    def bc_build(kind, post):
        for nt in range(4):
            if kind == "ws":
                r0 = nt * 2
                tk_ra = b.da(lambda e, r0=r0: e.dma_start(
                    s_row[0:1, 0:256],
                    AP(agA_out, r0 * 1792 + 1536, [[256, 1], [1, 256]])),
                    waits=[tk_agA, bc_mm_prev[0]])
                tk_r = b.da(lambda e, r0=r0: e.dma_start(
                    s_row[0:1, 256:512],
                    AP(agA_out, (r0 + 1) * 1792 + 1536, [[256, 1], [1, 256]])),
                    waits=[tk_ra])
            elif kind == "fb":
                tk_r = b.da(lambda e, nt=nt: e.dma_start(
                    s_row[0:1, :], fb[0:1, nt * 512:(nt + 1) * 512]),
                    waits=[bc_mm_prev[0]])
            else:
                m = kind + nt * 512
                r, o = m // 1536, m % 1536
                tk_r = b.da(lambda e, r=r, o=o: e.dma_start(
                    s_row[0:1, :], AP(agA_out, r * 1792 + o, [[512, 1], [1, 512]])),
                    waits=[tk_agA, bc_mm_prev[0]])
            tk_m = b.t(lambda e, nt=nt: e.matmul(
                pb[4 + (nt % 2)][:, :], s_ones[0:1, :], s_row[0:1, :], **MMK),
                waits=[tk_r, tk_ones, bc_post_prev[nt % 2]])
            bc_mm_prev[0] = tk_m
            bc_post_prev[nt % 2] = post(nt, tk_m)
        return bc_post_prev[(4 - 1) % 2]

    def copy_post(dst):
        def post(nt, tk_m):
            return b.s(lambda e, nt=nt: e.copy(
                dst[:, nt * 512:(nt + 1) * 512], pb[4 + (nt % 2)][:, :]),
                waits=[tk_m])
        return post

    tk_gab = bc_build(2 * D, copy_post(s_gab))
    tk_sbb = bc_build(3 * D, copy_post(s_sbb))
    tk_gb1 = bc_build(5 * D, copy_post(s_wsgb))
    tk_gb2 = bc_build(5 * D, copy_post(s_bgbb))
    tk_cb1b = bc_build(4 * D, lambda nt, tk_m: b.v(
        lambda e, nt=nt: e.tensor_scalar(s_cb1b[:, nt * 512:(nt + 1) * 512],
                                         pb[4 + (nt % 2)][:, :], 1.0, None,
                                         ALU.add), waits=[tk_m]))
    tk_ws_bc = bc_build("ws", lambda nt, tk_m: b.v(
        lambda e, nt=nt: e.tensor_tensor(s_wsgb[:, nt * 512:(nt + 1) * 512],
                                         pb[4 + (nt % 2)][:, :],
                                         s_wsgb[:, nt * 512:(nt + 1) * 512],
                                         ALU.mult), waits=[tk_m, tk_gb1]))
    tk_fb_bc = bc_build("fb", lambda nt, tk_m: b.v(
        lambda e, nt=nt: e.tensor_tensor(s_bgbb[:, nt * 512:(nt + 1) * 512],
                                         pb[4 + (nt % 2)][:, :],
                                         s_bgbb[:, nt * 512:(nt + 1) * 512],
                                         ALU.mult), waits=[tk_m, tk_gb2]))
    tap("gab", s_gab)
    tap("cb1b", s_cb1b)
    tap("sbb", s_sbb)
    tap("wsgb", s_wsgb)
    tap("bgbb", s_bgbb)

    b.gate = b.bar()
    tok_onT2 = emit_scan(1, tok_S1, s_onT2, s_sgg2, tok_g1T, tok_wg2)
    tap("onT2", s_onT2)
    tap("S1", s_S)

    b.gate = b.bar()
    tk_wo_ld = []
    for hh in range(4):
        def ldwo(e, hh=hh):
            src = AP(wo, hh * 512 * D, [[D, 128], [D * 128, 4], [1, D]])
            dst = AP(s_woP, hh * 4 * D, [[NKC * D, 128], [D, 4], [1, D]])
            return e.dma_start(dst, src)
        tk_wo_ld.append(b.dg(ldwo))
    tk_wo_sc = []
    for k in range(NKC):
        tk_wo_sc.append(b.v(lambda e, k=k: e.tensor_scalar(
            s_woP[:, k * D:(k + 1) * D], s_woP[:, k * D:(k + 1) * D],
            s_nw[:, k:k + 1], None, ALU.mult),
            waits=[tk_wo_ld[k // 4], tk_const["nw"]]))
    tok_woP = tk_wo_sc[-1]

    tk_warm3 = b.s(lambda e: e.activation(s_wrm[:, :], s_kc0[:, 0:1], AF.Sqrt))
    tk_stage = {}
    tk_xqT = {}
    tk_bf_of = {}
    tk_wo_mm = {}
    tk_xs_of = {}
    prev_x = None
    prev_t1 = {}
    prev_bf = None
    prev_trx = {0: None, 1: None}
    prev_sd = None
    prev_rs_read = None
    prev_mth3 = None

    def emit_trx(c):
        bfd_c = s_bfd if c % 2 == 0 else s_bfd1
        for half in range(2):
            def trx(e, half=half, bfd_c=bfd_c):
                last = None
                for kl in range(8):
                    last = e.transpose(
                        pt[half][:, kl * 128:(kl + 1) * 128],
                        bfd_c[:, (half * 8 + kl) * 128:
                              (half * 8 + kl + 1) * 128],
                        s_idb[:, :])
                return last
            tk_trx = b.t(trx, waits=[tk_bf_of[c], tk_xqT.get((c - 1, half))])
            prev_trx[half] = tk_trx
            dstq = s_xqTa if half == 0 else s_xqTb
            tk_xqT[(c, half)] = b.s(lambda e, c=c, dstq=dstq, half=half:
                                    e.copy(
                                        dstq[:, c * 1024:(c + 1) * 1024],
                                        pt[half][:, :]), waits=[tk_trx])

    def emit_ln(c, xb2, tk_res):
        nonlocal prev_mth3, prev_sd, prev_rs_read, prev_bf
        tk_sqd = b.s(lambda e, xb2=xb2: e.activation(
            s_h2d[:, :], xb2[:, :], AF.Square),
            waits=tk_res + [prev_mth3, tk_warm3, prev_bf])
        tk_sum = b.v(lambda e, xb2=xb2: e.reduce_sum(
            s_ca_[:, :], xb2[:, :], axis=AX.X), waits=tk_res + [prev_mth3])
        tk_sq2 = b.v(lambda e: e.reduce_sum(
            s_cb_[:, :], s_h2d[:, :], axis=AX.X), waits=[tk_sqd])
        def mth3(e):
            e.mul(s_cm_[:, :], s_ca_[:, 0:1], 1.0 / 2048.0)
            e.mul(s_cd_[:, :], s_cb_[:, 0:1], 1.0 / 2048.0)
            e.activation(s_ce_[:, :], s_cm_[:, 0:1], AF.Square)
            e.activation(s_cv_[:, :], s_ce_[:, 0:1], AF.Identity,
                         scale=-1.0, bias=s_cd_[:, 0:1])
            return e.activation(s_sd[:, :], s_cv_[:, 0:1], AF.Sqrt,
                                bias=s_kc0[:, 0:1])
        tk_sd = b.s(mth3, waits=[tk_sum, tk_sq2, prev_sd, prev_rs_read, tk_kc])
        prev_mth3 = tk_sd
        prev_sd = tk_sd
        tk_rs = b.v(lambda e: e.reciprocal(s_rs[:, :], s_sd[:, :]), waits=[tk_sd])
        tk_h20 = b.v(lambda e, xb2=xb2: e.tensor_scalar(
            s_h2d[:, :], xb2[:, :], s_cm_[:, 0:1], s_rs[:, :],
            ALU.subtract, ALU.mult), waits=[tk_rs, prev_bf])
        prev_rs_read = tk_h20
        tk_m1 = b.g(lambda e: e.tensor_tensor(s_h2d[:, :], s_h2d[:, :],
                                              s_cb1b[:, :], ALU.mult),
                    waits=[tk_h20, tk_cb1b])
        tk_m2 = b.g(lambda e: e.tensor_tensor(s_h2d[:, :], s_h2d[:, :],
                                              s_sbb[:, :], ALU.add),
                    waits=[tk_m1, tk_sbb])
        tk_amax = b.v(lambda e: e.reduce_max(s_am[:, 0:1], s_h2d[:, :],
                                             axis=AX.X,
                                             apply_absolute_value=True),
                      waits=[tk_m2])
        tk_x1 = b.s(lambda e: e.activation(s_rws[:, :], s_am[:, 0:1], AF.Relu,
                                           bias=s_kcx1[:, 0:1],
                                           scale=1.0 / 127.0),
                    waits=[tk_amax])
        tk_xs = b.s(lambda e, c=c: e.activation(
            s_xsal[:, c:c + 1], s_rws[:, :], AF.Identity,
            bias=s_kcx2[:, 0:1]), waits=[tk_x1])
        tk_xs_of[c] = tk_xs
        tk_rxs = b.v(lambda e, c=c: e.reciprocal(s_rs[:, :],
                                                 s_xsal[:, c:c + 1]),
                     waits=[tk_xs])
        tk_qf = b.v(lambda e: e.tensor_scalar(s_h2d[:, :], s_h2d[:, :],
                                              s_rs[:, :], None, ALU.mult),
                    waits=[tk_rxs])
        tk_i8 = b.v(lambda e: e.tensor_copy(s_i8d[:, :], s_h2d[:, :]),
                    waits=[tk_qf, prev_bf])
        bfd_c = s_bfd if c % 2 == 0 else s_bfd1
        tk_bf = b.s(lambda e, bfd_c=bfd_c: e.activation(
            bfd_c[:, :], s_i8d[:, :], AF.Copy),
            waits=[tk_i8, prev_trx[0], prev_trx[1]])
        prev_bf = tk_bf
        tk_bf_of[c] = tk_bf
        tk_xp = b.g(lambda e, xb2=xb2: e.tensor_tensor(xb2[:, :], xb2[:, :],
                                              s_bgbb[:, :], ALU.add),
                    waits=[tk_m2, tk_fb_bc, tk_amax])
        tk_stage[c] = b.da(lambda e, c=c, xb2=xb2: e.dma_start(
            AP(out_s, c * 128 * D, [[D, 128], [1, D]]), xb2[:, :]),
            waits=[tk_xp])

    pend_ln = {}
    for c in range(NCH):
        if c >= 3:
            emit_trx(c - 3)
        xbuf = s_x0 if c % 2 == 0 else s_x1
        xb2 = (s_x2, s_x2b, s_x2c)[c % 3]
        tk_x = b.ds(lambda e, c=c, xbuf=xbuf: e.dma_start(
            xbuf[:, :], AP(x_s, c * 128 * D, [[D, 128], [1, D]])),
            waits=[prev_x])
        tk_res = []
        for oc in range(4):
            bank = pb[oc % 2]
            def womm(e, c=c, oc=oc, bank=bank):
                last = None
                for k in range(NKC):
                    src = s_onT1 if k < 8 else s_onT2
                    kl = k % 8
                    last = e.matmul(
                        bank[:, :],
                        src[:, c * 1024 + kl * 128:c * 1024 + kl * 128 + 128],
                        s_woP[:, k * D + oc * 512:k * D + (oc + 1) * 512],
                        start=(k == 0), stop=(k == NKC - 1))
                return last
            w2 = [tok_woP, tok_onT1, tok_onT2,
                  prev_t1.get((c, oc - 2)), prev_t1.get((c - 1, oc + 2))]
            tk_mm = b.t(womm, waits=w2)
            tk_wo_mm[(c, oc)] = tk_mm
            tk_t1 = b.v(lambda e, oc=oc, bank=bank, xb2=xb2: e.tensor_tensor(
                xb2[:, oc * 512:(oc + 1) * 512], bank[:, :],
                s_gab[:, oc * 512:(oc + 1) * 512], ALU.mult),
                waits=[tk_mm, tk_gab, tk_stage.get(c - 3)])
            prev_t1[(c, oc)] = tk_t1
            tk_res.append(b.g(lambda e, oc=oc, xbuf=xbuf, xb2=xb2:
                              e.tensor_tensor(
                xb2[:, oc * 512:(oc + 1) * 512],
                xb2[:, oc * 512:(oc + 1) * 512],
                xbuf[:, oc * 512:(oc + 1) * 512], ALU.add),
                waits=[tk_t1, tk_x]))
        prev_x = tk_res[-1]
        pend_ln[c] = (xb2, tk_res)
        if c >= 1:
            emit_ln(c - 1, *pend_ln.pop(c - 1))
    emit_ln(NCH - 1, *pend_ln.pop(NCH - 1))
    emit_trx(NCH - 3)
    emit_trx(NCH - 2)
    emit_trx(NCH - 1)

    tap("xqTa", s_xqTa)
    tap("xqTb", s_xqTb)
    tap("xsal", s_xsal)
    tap("end_modT", s_modT)
    tap("end_gab", s_gab)
    tap("end_wsgb", s_wsgb)
    tap("end_bgbb", s_bgbb)
    tap("end_cb1b", s_cb1b)
    tap("end_sbb", s_sbb)
    tap("end_xsal", s_xsal)

    SKIP_P3B = "stop_p3a" in DEBUG_TAPS
    p3a_tail = [prev_bf, prev_trx[0], prev_trx[1], prev_x]
    tk_wq_ld = {}
    for r in (range(NCORES) if not SKIP_P3B else []):
        def ldwq(e, r=r):
            src = AP(agB_out, r * D * 256, [[256, 128], [256 * 128, NKC], [1, 256]])
            dst = AP(s_wqT, r * 256, [[NKC * D, 128], [D, NKC], [1, 256]])
            return e.dma_start(dst, src)
        tk_wq_ld[r] = [b.ds, b.dg][r % 2](
            ldwq, waits=[tk_agB, tk_wo_mm.get((NCH - 1, r // 2))])
    prev_rb = None
    prev_fd = {}
    prev_et1 = {}
    final_dma = []
    for c in (range(NCH) if not SKIP_P3B else []):
        rbuf = s_rb0 if c % 2 == 0 else s_rb1
        tk_rb = b.ds(lambda e, c=c, rbuf=rbuf: e.dma_start(
            rbuf[:, :], AP(out_s, c * 128 * D, [[D, 128], [1, D]])),
            waits=[tk_stage[c], prev_rb] + ([prev_x] if c < 2 else []))
        obuf = s_ob0 if c % 2 == 0 else s_ob1
        tk_oc = []
        for oc in range(4):
            bank = pb[oc % 2]
            def fmm(e, c=c, oc=oc, bank=bank):
                last = None
                for k in range(NKC):
                    src = s_xqTa if k < 8 else s_xqTb
                    kl = k % 8
                    last = e.matmul(
                        bank[:, :],
                        src[:, c * 1024 + kl * 128:c * 1024 + kl * 128 + 128],
                        s_wqT[:, k * D + oc * 512:k * D + (oc + 1) * 512],
                        start=(k == 0), stop=(k == NKC - 1))
                return last
            wmm = [tk_wq_ld[2 * oc], tk_wq_ld[2 * oc + 1],
                   tk_xqT[(c, 0)], tk_xqT[(c, 1)],
                   prev_et1.get((c, oc - 2)), prev_et1.get((c - 1, oc + 2))]
            if c == 0 and oc < 2:
                wmm.append(prev_t1.get((NCH - 1, oc + 2)))
            tk_mm = b.t(fmm, waits=wmm)
            tk_et1 = b.v(lambda e, c=c, bank=bank, oc=oc: e.scalar_tensor_tensor(
                s_t1[:, :], bank[:, :], s_xsal[:, c:c + 1],
                s_wsgb[:, oc * 512:(oc + 1) * 512], ALU.mult, ALU.mult),
                waits=[tk_mm, tk_ws_bc, tk_xs_of[c]] +
                ([tk_stage[NCH - 1]] if c == 0 else []))
            prev_et1[(c, oc)] = tk_et1
            tk_oc.append(b.v(lambda e, oc=oc, obuf=obuf, rbuf=rbuf:
                             e.tensor_tensor(
                                 obuf[:, oc * 512:(oc + 1) * 512], s_t1[:, :],
                                 rbuf[:, oc * 512:(oc + 1) * 512], ALU.add),
                             waits=[tk_et1, tk_rb, prev_fd.get(c - 2)] +
                             (p3a_tail if c < 2 else [])))
        prev_rb = tk_oc[-1]
        final_dma.append(b.da(lambda e, c=c, obuf=obuf: e.dma_start(
            AP(out_s, c * 128 * D, [[D, 128], [1, D]]), obuf[:, :]),
            waits=tk_oc))
        prev_fd[c] = final_dma[-1]

    for nm, t_sb in end_taps:
        shp = [int(x) for x in t_sb.shape]
        dr = nc.dram_tensor("tap_" + nm, shp, t_sb.dtype, kind="ExternalOutput")
        b.ds(lambda e, dr=dr, t_sb=t_sb: e.dma_start(dr[:, :], t_sb[:, :]),
             waits=b.bar())

    b.s(lambda e: None,
        waits=[final_dma[-1] if final_dma else tk_stage[NCH - 1]])

    with nc.Block() as block:
        @block.tensor
        def _(e):
            for f in b.prog["tensor"]:
                f(e)

        @block.vector
        def _(e):
            for f in b.prog["vector"]:
                f(e)

        @block.scalar
        def _(e):
            for f in b.prog["scalar"]:
                f(e)

        @block.gpsimd
        def _(e):
            for f in b.prog["gpsimd"]:
                f(e)

        @block.sync
        def _(e):
            for f in b.prog["sync"]:
                f(e)

    return nc


def _host_constants(core):
    j = np.arange(64, dtype=np.float64)
    theta = BASEF ** (-2.0 * j / 128.0)
    T0 = core * TLOC
    cosx = np.zeros((128, NCH * 64), dtype=np.float32)
    sinx = np.zeros((128, NCH * 64), dtype=np.float32)
    for c in range(NCH):
        n = (T0 + c * 128 + np.arange(128, dtype=np.float64))[:, None]
        ang = n * theta[None, :]
        cosx[:, c * 64:(c + 1) * 64] = np.cos(ang)
        sinx[:, c * 64:(c + 1) * 64] = np.sin(ang)
    i = np.arange(128, dtype=np.float64)
    dt64 = np.zeros((128, H * 128), dtype=np.float64)
    dq64 = np.zeros((128, H * 128), dtype=np.float64)
    dk_c = np.zeros((128, H), dtype=np.float32)
    for h in range(H):
        ll = LOG_LAM[h]
        diff = i[None, :] - i[:, None]
        dt64[:, h * 128:(h + 1) * 128] = np.where(diff >= 0, np.exp(ll * diff), 0.0)
        dq64[:, h * 128:(h + 1) * 128] = np.exp(ll * (i[None, :] + 1.0))
        dk_c[:, h] = np.exp(ll * (128.0 - 1.0 - i)).astype(np.float32)
    coef = np.zeros((128, 7 * H), dtype=np.float32)
    for jj in range(7):
        for h in range(H):
            if jj < core:
                coef[:, jj * H + h] = np.float32(
                    np.exp(np.float64(LOG_LAM[h]) * 128.0 * 8.0 * (core - 1 - jj)))
    return (cosx, sinx,
            dt64.astype(np.float32).astype(ml_dtypes.bfloat16),
            dq64.astype(np.float32).astype(ml_dtypes.bfloat16),
            dk_c, coef)


def kernel(L0_attn_x, L0_attn_Wq, L0_attn_Wk, L0_attn_Wv, L0_attn_Wo,
           L0_attn_Wg1, L0_attn_Wg2, L0_attn_norm_weight, L0_ffn_weight,
           L0_ffn_bias, L0_c, L0_wmod):
    x = np.ascontiguousarray(np.asarray(L0_attn_x, dtype=np.float32).reshape(NTOK, D))
    wq = np.ascontiguousarray(np.asarray(L0_attn_Wq, dtype=np.float32))
    wk = np.ascontiguousarray(np.asarray(L0_attn_Wk, dtype=np.float32))
    wv = np.ascontiguousarray(np.asarray(L0_attn_Wv, dtype=np.float32))
    wo = np.ascontiguousarray(np.asarray(L0_attn_Wo, dtype=np.float32))
    wg1 = np.ascontiguousarray(np.asarray(L0_attn_Wg1, dtype=np.float32))
    wg2 = np.ascontiguousarray(np.asarray(L0_attn_Wg2, dtype=np.float32))
    nw = np.asarray(L0_attn_norm_weight, dtype=np.float32)
    fw = np.ascontiguousarray(np.asarray(L0_ffn_weight, dtype=np.float32))
    fbi = np.asarray(L0_ffn_bias, dtype=np.float32)
    cc = np.asarray(L0_c, dtype=np.float32)
    wm = np.ascontiguousarray(np.asarray(L0_wmod, dtype=np.float32))

    nw_t = np.ascontiguousarray(nw.reshape(16, 128).T)
    c_t = np.ascontiguousarray(cc.reshape(16, 128).T)
    fb = np.ascontiguousarray(fbi.reshape(1, D))
    idb = np.eye(128, dtype=np.float32).astype(ml_dtypes.bfloat16)
    idf = np.eye(128, dtype=np.float32)

    in_maps = []
    for core in range(NCORES):
        cosx, sinx, dt_c, dq_c, dk_c, coef = _host_constants(core)
        gm = np.zeros((1, 1536), np.float32)
        for o in range(1536):
            gidx = core * 1536 + o
            if 2 * D <= gidx < 3 * D or 5 * D <= gidx < 6 * D:
                gm[0, o] = 1.0
        in_maps.append({
            "x_s": np.ascontiguousarray(x[core * TLOC:(core + 1) * TLOC]),
            "wq": wq, "wk": wk, "wv": wv, "wo": wo,
            "wg1": wg1, "wg2_i": wg2,
            "nw_t": nw_t, "fb": fb, "c_t": c_t,
            "wmod_s": np.ascontiguousarray(wm[:, core * 1536:(core + 1) * 1536]),
            "ffnw_s": np.ascontiguousarray(fw[core * 256:(core + 1) * 256]),
            "cosx": cosx, "sinx": sinx, "dt_c": dt_c, "dq_c": dq_c,
            "dk_c": dk_c, "coef": coef, "idb": idb, "idf": idf,
            "gmask": gm,
        })

    nc = build_program()
    res = run_bass_kernel_spmd(nc, in_maps, core_ids=list(range(NCORES)))
    global _LAST_RES
    _LAST_RES = res.results
    out = np.concatenate([res.results[i]["out_s"] for i in range(NCORES)], axis=0)
    return out.reshape(1, NTOK, D).astype(np.float32)


# version 18
# speedup vs baseline: 1.0937x; 1.0126x over previous
import numpy as np
import ml_dtypes

import concourse.bass as bass
import concourse.mybir as mybir
from concourse.bass_utils import run_bass_kernel_spmd

F32 = mybir.dt.float32
BF16 = mybir.dt.bfloat16
I8 = mybir.dt.int8
AF = mybir.ActivationFunctionType
ALU = mybir.AluOpType
AX = mybir.AxisListType
AP = bass.AP

NCORES = 8
D = 2048
H = 16
CK = 128
NTOK = 8192
TLOC = NTOK // NCORES
NCH = TLOC // CK
NKC = D // 128
NG = 2
GH = H // NG
BASEF = 10000.0
LN_EPS = 1e-5
RMS_EPS = 1e-6

LOG_LAM = [-(2.0 ** (-0.5 * (h + 1))) for h in range(H)]
LAMC = [float(np.exp(np.float64(LOG_LAM[h]) * CK)) for h in range(H)]

DEBUG_TAPS = ()
_LAST_RES = None


class Bld:
    ENG = ("tensor", "vector", "scalar", "gpsimd", "sync")

    def __init__(self, nc):
        self.nc = nc
        self.prog = {e: [] for e in self.ENG}
        self.cnt = {k: 0 for k in ("T", "V", "S", "G", "DG", "DS", "DA", "CC")}
        self.sems = {}
        self.waited = {e: {} for e in self.ENG}
        self.gate = []

    def _step(self, eng, sem, inc, waits, fn):
        w2 = []
        wd = self.waited[eng]
        for tok in list(waits) + list(self.gate):
            if tok is None:
                continue
            s, v = tok
            if v <= 0 or wd.get(s, 0) >= v:
                continue
            wd[s] = v
            w2.append((s, v))
        self.cnt[sem] += inc
        token = (sem, self.cnt[sem])

        def run(e):
            for s, v in w2:
                e.wait_ge(self.sems[s], v)
            last = fn(e)
            if last is not None:
                last.then_inc(self.sems[sem], inc)

        self.prog[eng].append(run)
        return token

    def t(self, fn, waits=()):
        return self._step("tensor", "T", 1, waits, fn)

    def v(self, fn, waits=()):
        return self._step("vector", "V", 1, waits, fn)

    def s(self, fn, waits=()):
        return self._step("scalar", "S", 1, waits, fn)

    def g(self, fn, waits=()):
        return self._step("gpsimd", "G", 1, waits, fn)

    def dg(self, fn, waits=()):
        return self._step("gpsimd", "DG", 16, waits, fn)

    def ds(self, fn, waits=()):
        return self._step("sync", "DS", 16, waits, fn)

    def da(self, fn, waits=()):
        return self._step("scalar", "DA", 16, waits, fn)

    def cc(self, fn, waits=()):
        return self._step("gpsimd", "CC", 1, waits, fn)

    def bar(self):
        return [(k, self.cnt[k]) for k in self.cnt]


def build_program():
    nc = bass.Bass("TRN2", target_bir_lowering=False)
    b = Bld(nc)

    def din(name, shape, dt=F32):
        return nc.dram_tensor(name, shape, dt, kind="ExternalInput")

    x_s = din("x_s", [TLOC, D])
    wq = din("wq", [D, D])
    wk = din("wk", [D, D])
    wv = din("wv", [D, D])
    wo = din("wo", [D, D])
    wg1 = din("wg1", [D, 128])
    wg2_i = din("wg2_i", [128, D])
    nw_t = din("nw_t", [128, 16])
    fb = din("fb", [1, D])
    c_t = din("c_t", [128, 16])
    wmod_s = din("wmod_s", [D, 1536])
    ffnw_s = din("ffnw_s", [256, D])
    cosx = din("cosx", [128, NCH * 64])
    sinx = din("sinx", [128, NCH * 64])
    dt_c = din("dt_c", [128, H * 128], BF16)
    dq_c = din("dq_c", [128, H * 128], BF16)
    dk_c = din("dk_c", [128, H])
    coef = din("coef", [128, 7 * H])
    idb = din("idb", [128, 128], BF16)
    idf = din("idf", [128, 128])
    gmask = din("gmask", [1, 1536])

    out_s = nc.dram_tensor("out_s", [TLOC, D], F32, kind="ExternalOutput")

    agA_in = nc.dram_tensor("agA_in", [1, 1792], F32)
    agA_out = nc.dram_tensor("agA_out", [NCORES, 1792], F32)
    agB_in = nc.dram_tensor("agB_in", [D, 256], BF16)
    agB_out = nc.dram_tensor("agB_out", [NCORES * D, 256], BF16)
    agL_in = [nc.dram_tensor(f"agL{g}_in", [128, GH * 128], F32) for g in range(NG)]
    agL_out = [nc.dram_tensor(f"agL{g}_out", [NCORES * 128, GH * 128], F32)
               for g in range(NG)]

    KB = 1024
    off_B = 19 * KB
    off_C = 44 * KB
    off_D = 76 * KB
    off_E = 108 * KB
    off_F = 172 * KB
    off_G = 178 * KB
    off_H = 184 * KB
    LIM = 204 * KB

    SB_BASE = (nc.sbuf_base + 31) // 32 * 32
    assert SB_BASE + LIM <= nc.sbuf_top, (SB_BASE, nc.sbuf_top)

    def sb_at(name, shape, dt, off):
        return nc.alloc_sbuf_tensor_at(name, shape, dt, offset=SB_BASE + off)

    cur = [0]

    def sbA(name, shape, dt):
        t = sb_at(name, shape, dt, cur[0])
        esz = {F32: 4, BF16: 2, I8: 1}[dt]
        cur[0] += (int(np.prod(shape[1:])) * esz + 31) // 32 * 32
        return t

    s_idb = sbA("s_idb", [128, 128], BF16)
    s_idf = sbA("s_idf", [128, 128], F32)
    s_cos = sbA("s_cos", [128, NCH * 64], F32)
    s_sin = sbA("s_sin", [128, NCH * 64], F32)
    s_dt = sbA("s_dt", [128, H * 128], BF16)
    s_dq = sbA("s_dq", [128, H * 128], BF16)
    s_dk = sbA("s_dk", [128, H], F32)
    s_coef = sbA("s_coef", [128, 7 * H], F32)
    s_modT = sbA("s_modT", [128, 96], F32)
    s_ca1 = sbA("s_ca1", [128, 16], F32)
    s_nw = sbA("s_nw", [128, 16], F32)
    s_ct = sbA("s_ct", [128, 16], F32)
    s_silc = sbA("s_silc", [128, 16], F32)
    s_ones = sbA("s_ones", [1, 128], F32)
    s_xsal = sbA("s_xsal", [128, NCH], F32)
    s_sd = sbA("s_sd", [128, 1], F32)
    s_rs = sbA("s_rs", [128, 1], F32)
    s_ms = sbA("s_ms", [128, GH], F32)
    s_rsg = sbA("s_rsg", [128, GH], F32)
    s_am = sbA("s_am", [128, 8], F32)
    s_wsc = sbA("s_wsc", [128, 8], F32)
    s_row = sbA("s_row", [1, 512], F32)
    s_mnat = sbA("s_mnat", [128, 128], F32)
    s_kc0 = sbA("s_kc0", [128, 1], F32)
    s_kc1 = sbA("s_kc1", [128, 1], F32)
    s_kc2 = sbA("s_kc2", [128, 1], F32)
    s_kcw1 = sbA("s_kcw1", [128, 1], F32)
    s_kcw2 = sbA("s_kcw2", [128, 1], F32)
    s_kcx1 = sbA("s_kcx1", [128, 1], F32)
    s_kcx2 = sbA("s_kcx2", [128, 1], F32)
    s_wrm = sbA("s_wrm", [128, 1], F32)
    s_am2 = sbA("s_am2", [128, 1], F32)
    s_rws = sbA("s_rws", [128, 1], F32)
    s_rsg2 = sbA("s_rsg2", [128, GH], F32)
    s_m8 = sbA("s_m8", [128, 8], F32)
    s_m8b = sbA("s_m8b", [128, 8], F32)
    s_z8 = sbA("s_z8", [128, 8], F32)
    s_ca_ = sbA("s_ca_", [128, 1], F32)
    s_cb_ = sbA("s_cb_", [128, 1], F32)
    s_cc_ = sbA("s_cc_", [128, 1], F32)
    s_cd_ = sbA("s_cd_", [128, 1], F32)
    s_ce_ = sbA("s_ce_", [128, 1], F32)
    s_cm_ = sbA("s_cm_", [128, 1], F32)
    s_cv_ = sbA("s_cv_", [128, 1], F32)
    s_qr1 = sbA("s_qr1", [128, 512], BF16)
    s_ca2 = sbA("s_ca2", [128, 1], F32)
    s_cb2 = sbA("s_cb2", [128, 1], F32)
    s_cd2 = sbA("s_cd2", [128, 1], F32)
    s_ce2 = sbA("s_ce2", [128, 1], F32)
    s_cm2 = sbA("s_cm2", [128, 1], F32)
    s_cv2 = sbA("s_cv2", [128, 1], F32)
    s_sd2 = sbA("s_sd2", [128, 1], F32)
    s_rs2 = sbA("s_rs2", [128, 1], F32)
    assert cur[0] <= off_B, f"A overflow {cur[0]}"

    s_x0 = sb_at("s_x0", [128, D], F32, off_B)
    s_x1 = sb_at("s_x1", [128, D], F32, off_B + 8 * KB)
    s_h0a = sb_at("s_h0a", [128, D], BF16, off_B + 16 * KB)
    s_h0b = sb_at("s_h0b", [128, D], BF16, off_B + 20 * KB)
    s_bc1 = sb_at("s_bc1", [128, 512], F32, off_B + 16 * KB)
    s_bc2 = sb_at("s_bc2", [128, 512], F32, off_B + 18 * KB)
    s_sil0 = sb_at("s_sil0", [128, 512], F32, off_B)
    s_sil1 = sb_at("s_sil1", [128, 512], F32, off_B + 2 * KB)
    s_rt0 = sb_at("s_rt0", [128, 256], F32, off_B + 4 * KB)
    s_rt1 = sb_at("s_rt1", [128, 256], F32, off_B + 5 * KB)
    s_rt2 = sb_at("s_rt2", [128, 256], F32, off_B + 6 * KB)
    s_rt3 = sb_at("s_rt3", [128, 256], F32, off_B + 7 * KB)
    s_qr0 = sb_at("s_qr0", [128, 512], BF16, off_B + 24 * KB)
    s_whA = sb_at("s_whA", [128, NKC * 512], BF16, off_B + 8 * KB)
    s_fw = sb_at("s_fw", [128, D], F32, off_D + 16 * KB)
    s_fi8 = sb_at("s_fi8", [128, D], I8, off_D + 24 * KB)
    s_fbf = sb_at("s_fbf", [128, D], BF16, off_D + 26 * KB)
    s_wqTs = sb_at("s_wqTs", [128, 16 * 256], BF16, off_H + 8 * KB)
    s_pt0 = sb_at("s_pt0", [128, 512], BF16, off_B)
    s_pt1 = sb_at("s_pt1", [128, 512], BF16, off_B + 1 * KB)
    s_qt0 = sb_at("s_qt0", [128, 512], BF16, off_B + 2 * KB)
    s_qt1 = sb_at("s_qt1", [128, 512], BF16, off_B + 3 * KB)
    s_oh0 = sb_at("s_oh0", [128, 1024], F32, off_B + 4 * KB)
    s_oh1 = sb_at("s_oh1", [128, 1024], F32, off_B + 8 * KB)
    s_gg = sb_at("s_gg", [128, 1024], F32, off_B + 12 * KB)
    s_sq = sb_at("s_sq", [128, 1024], F32, off_B + 16 * KB)
    s_on = sb_at("s_on", [128, 1024], BF16, off_B + 20 * KB)
    s_lgj = sb_at("s_lgj", [128, GH * 128], F32, off_B + 16 * KB)
    s_x2 = sb_at("s_x2", [128, D], F32, off_B + 16 * KB)
    s_x2b = sb_at("s_x2b", [128, D], F32, off_C + 16 * KB)
    s_x2c = sb_at("s_x2c", [128, D], F32, off_C + 24 * KB)
    s_rb0 = sb_at("s_rb0", [128, D], F32, off_B)
    s_rb1 = sb_at("s_rb1", [128, D], F32, off_B + 8 * KB)
    s_t1 = sb_at("s_t1", [128, 512], F32, off_B + 16 * KB)

    s_h1T = sb_at("s_h1T", [128, NKC * TLOC], BF16, off_C)
    s_onT2 = sb_at("s_onT2", [128, NCH * 1024], BF16, off_C)
    s_sgg2 = sb_at("s_sgg2", [128, NCH * 1024], BF16, off_C + 16 * KB)
    s_xqTb = sb_at("s_xqTb", [128, NCH * 1024], BF16, off_C)

    s_wh0 = sb_at("s_wh0", [128, NKC * 512], BF16, off_D)
    s_wh1 = sb_at("s_wh1", [128, NKC * 512], BF16, off_D + 16 * KB)
    s_onT1 = sb_at("s_onT1", [128, NCH * 1024], BF16, off_D)
    s_sgg1 = sb_at("s_sgg1", [128, NCH * 1024], BF16, off_D + 16 * KB)
    s_wp0 = sb_at("s_wp0", [128, NKC * 256], BF16, off_D + 16 * KB)
    s_wp1 = sb_at("s_wp1", [128, NKC * 256], BF16, off_D + 24 * KB)
    s_xqTa = sb_at("s_xqTa", [128, NCH * 1024], BF16, off_D)
    s_h2d = sb_at("s_h2d", [128, D], F32, off_D + 16 * KB)
    s_i8d = sb_at("s_i8d", [128, D], I8, off_D + 24 * KB)
    s_bfd = sb_at("s_bfd", [128, D], BF16, off_D + 26 * KB)
    s_bfd1 = sb_at("s_bfd1", [128, D], BF16, off_F)
    s_ob0 = sb_at("s_ob0", [128, D], F32, off_D + 16 * KB)
    s_ob1 = sb_at("s_ob1", [128, D], F32, off_D + 24 * KB)

    s_qT = sb_at("s_qT", [128, GH * TLOC], BF16, off_E)
    s_kT = sb_at("s_kT", [128, GH * TLOC], BF16, off_E + 16 * KB)
    s_k2 = sb_at("s_k2", [128, NCH * GH * 128], BF16, off_E + 32 * KB)
    s_vt = sb_at("s_vt", [128, NCH * GH * 128], BF16, off_E + 48 * KB)
    s_wmod = sb_at("s_wmod", [128, 8 * 1536], F32, off_E + 16 * KB)
    s_woP = sb_at("s_woP", [128, NKC * D], BF16, off_E)
    s_wqT = sb_at("s_wqT", [128, NKC * D], BF16, off_E)

    s_S = sb_at("s_S", [128, GH * 128], F32, off_F)
    s_Sb = sb_at("s_Sb", [128, GH * 128], BF16, off_F + 4 * KB)

    s_g1T = sb_at("s_g1T", [128, TLOC], BF16, off_G)
    s_wg2 = sb_at("s_wg2", [128, D], BF16, off_G + 2 * KB)

    s_modH = sb_at("s_modH", [1, 1536], F32, off_H)
    s_gmask = sb_at("s_gmask", [1, 1536], F32, off_D)
    s_modH2 = sb_at("s_modH2", [1, 1536], F32, off_D + 8 * KB)
    s_mcA = sb_at("s_mcA", [1, 512], F32, off_B)
    s_mcB = sb_at("s_mcB", [1, 512], F32, off_B + 2048)
    s_mcC = sb_at("s_mcC", [1, 512], F32, off_B + 4096)
    s_gab = sb_at("s_gab", [128, D], BF16, off_H)
    s_cb1b = sb_at("s_cb1b", [128, D], BF16, off_H + 4 * KB)
    s_sbb = sb_at("s_sbb", [128, D], BF16, off_H + 8 * KB)
    s_wsgb = sb_at("s_wsgb", [128, D], BF16, off_H + 12 * KB)
    s_bgbb = sb_at("s_bgbb", [128, D], BF16, off_H + 16 * KB)
    assert off_H + 20 * KB <= LIM

    pb = [nc.alloc_psum_tensor(f"pb{i}", [128, 512], F32) for i in range(6)]
    pt = [nc.alloc_psum_tensor(f"pt{i}", [128, 1024], BF16) for i in range(2)]

    for name in ("T", "V", "S", "G", "DG", "DS", "DA", "CC"):
        b.sems[name] = nc.alloc_semaphore(name=name)

    MMK = dict(start=True, stop=True, skip_group_check=True)


    end_taps = []

    def tap(name, t_sb):
        if name.startswith("end_"):
            if name in DEBUG_TAPS:
                end_taps.append((name, t_sb))
            return
        if name not in DEBUG_TAPS:
            return
        shp = [int(x) for x in t_sb.shape]
        dr = nc.dram_tensor("tap_" + name, shp, t_sb.dtype, kind="ExternalOutput")
        b.ds(lambda e, dr=dr, t_sb=t_sb: e.dma_start(dr[:, :], t_sb[:, :]),
             waits=b.bar())


    tk_const = {}
    for nm, t_sb, t_dr in (("idb", s_idb, idb), ("idf", s_idf, idf),
                           ("cos", s_cos, cosx), ("sin", s_sin, sinx),
                           ("dt", s_dt, dt_c), ("dq", s_dq, dq_c),
                           ("dk", s_dk, dk_c), ("coef", s_coef, coef),
                           ("nw", s_nw, nw_t), ("ct", s_ct, c_t)):
        tk_const[nm] = b.ds(lambda e, a=t_sb, c=t_dr: e.dma_start(a[:, :], c[:, :]))
    tk_ones = b.v(lambda e: e.memset(s_ones[:, :], 1.0))
    tk_kc = b.v(lambda e: [e.memset(s_z8[:, :], 0.0),
                           e.memset(s_kcw1[:, :], -1e-8),
                           e.memset(s_kcw2[:, :], 1e-8 / 127.0),
                           e.memset(s_kcx1[:, :], -1e-12),
                           e.memset(s_kcx2[:, :], 1e-12),
                           e.memset(s_kc0[:, :], float(LN_EPS)),
                           e.memset(s_kc1[:, :], float(RMS_EPS)),
                           e.memset(s_kc2[:, :], 1.0 / 128.0)][-1])
    def ld_wm(e, hh):
        srcap = AP(wmod_s, hh * 8 * 128 * 1536,
                   [[1536, 128], [1536 * 128, 8], [1, 1536]])
        dst = AP(s_wmod, 0, [[8 * 1536, 128], [1536, 8], [1, 1536]])
        return e.dma_start(dst, srcap)
    tk_wm0 = b.dg(lambda e: ld_wm(e, 0))
    tk_silc = b.s(lambda e: e.activation(s_silc[:, :], s_ct[:, :], AF.Silu),
                  waits=[tk_const["ct"]])

    fq_last_v = None
    fq_last_i = None
    tk_ws_done = {}
    for tile in range(2):
        tk_ldf = b.da(lambda e, tile=tile: e.dma_start(
            s_fw[:, :], AP(ffnw_s, tile * 128 * D, [[D, 128], [1, D]])),
            waits=[fq_last_i])
        tk_a = b.v(lambda e: e.reduce_max(s_am[:, 0:1], s_fw[:, :], axis=AX.X,
                                          apply_absolute_value=True),
                   waits=[tk_ldf])
        tk_w1 = b.s(lambda e: e.activation(s_rws[:, :], s_am[:, 0:1], AF.Relu,
                                           bias=s_kcw1[:, 0:1]),
                    waits=[tk_a, tk_kc])
        tk_w2 = b.s(lambda e, tile=tile: e.activation(
            s_wsc[:, tile:tile + 1], s_rws[:, :], AF.Identity,
            bias=s_kcw2[:, 0:1], scale=1.0 / 127.0), waits=[tk_w1])
        tk_ws = b.v(lambda e, tile=tile: e.reciprocal(
            s_am2[:, 0:1], s_wsc[:, tile:tile + 1]), waits=[tk_w2])
        tk_ws_done[tile] = tk_w2
        tk_q = b.v(lambda e: e.tensor_scalar(
            s_fw[:, :], s_fw[:, :], s_am2[:, 0:1], None, ALU.mult),
            waits=[tk_ws, fq_last_i])
        tk_i = b.v(lambda e: e.tensor_copy(s_fi8[:, :], s_fw[:, :]),
                   waits=[tk_q])
        fq_last_i = tk_i
        tk_bf = b.s(lambda e: e.activation(s_fbf[:, :], s_fi8[:, :], AF.Copy),
                    waits=[tk_i])
        for half in range(2):
            def trf(e, half=half):
                last = None
                for kk in range(8):
                    last = e.transpose(pt[0][:, kk * 128:(kk + 1) * 128],
                                       s_fbf[:, (half * 8 + kk) * 128:
                                             (half * 8 + kk + 1) * 128],
                                       s_idb[:, :])
                return last
            tk_trf = b.t(trf, waits=[tk_bf, tk_const["idb"], fq_last_v])
            def evf(e, tile=tile, half=half):
                dst = AP(s_wqTs, (half * 8) * 256 + tile * 128,
                         [[16 * 256, 128], [256, 8], [1, 128]])
                return e.tensor_copy(dst, AP(pt[0], 0,
                                             [[1024, 128], [128, 8], [1, 128]]))
            fq_last_v = b.v(evf, waits=[tk_trf])
    tk_wqTs_done = fq_last_v

    tk_wssave = [b.dg(lambda e, tile=tile: e.dma_start(
        AP(agA_in, 1536 + tile * 128, [[1, 128], [1, 1]]),
        AP(s_wsc, tile, [[8, 128], [1, 1]])), waits=[tk_ws_done[tile]])
        for tile in range(2)]
    tk_wqsave = b.dg(lambda e: e.dma_start(
        AP(agB_in, 0, [[256, 128], [256 * 128, 16], [1, 256]]),
        AP(s_wqTs, 0, [[16 * 256, 128], [256, 16], [1, 256]])),
        waits=[tk_wqTs_done])

    tk_warm0 = b.s(lambda e: e.activation(s_wrm[:, :], s_kc0[:, 0:1], AF.Sqrt),
                   waits=[tk_kc, tk_silc])
    h0tok = {}
    tk_ev_bank = {0: tk_wqTs_done}
    tok_h1T_plain = None
    prev_pair_tr = None
    mth_of = {}
    cells = ((s_ca_, s_cb_, s_cd_, s_ce_, s_cm_, s_cv_, s_sd, s_rs),
             (s_ca2, s_cb2, s_cd2, s_ce2, s_cm2, s_cv2, s_sd2, s_rs2))
    for pair in range(NCH // 2):
        for j in range(2):
            tc = pair * 2 + j
            ca, cb, cd, ce, cm, cv, sd, rs = cells[tc % 2]
            buf = s_x0 if tc % 2 == 0 else s_x1
            h0 = s_h0a if j == 0 else s_h0b
            tk_x = b.ds(lambda e, tc=tc, buf=buf: e.dma_start(
                buf[:, :], AP(x_s, tc * 128 * D, [[D, 128], [1, D]])),
                waits=[h0tok.get(tc - 2)])
            tk_sqd = b.s(lambda e, buf=buf, h0=h0: e.activation(
                h0[:, :], buf[:, :], AF.Square),
                waits=[tk_x, prev_pair_tr, tk_warm0])
            tk_sum = b.v(lambda e, buf=buf, ca=ca: e.reduce_sum(
                ca[:, :], buf[:, :], axis=AX.X),
                waits=[tk_x, mth_of.get(tc - 2), prev_pair_tr])
            tk_sq2 = b.v(lambda e, h0=h0, cb=cb: e.reduce_sum(
                cb[:, :], h0[:, :], axis=AX.X),
                waits=[tk_sqd, mth_of.get(tc - 2)])
            def mth(e, ca=ca, cb=cb, cd=cd, ce=ce, cm=cm, cv=cv, sd=sd):
                e.mul(cm[:, :], ca[:, 0:1], 1.0 / 2048.0)
                e.mul(cd[:, :], cb[:, 0:1], 1.0 / 2048.0)
                e.activation(ce[:, :], cm[:, 0:1], AF.Square)
                e.activation(cv[:, :], ce[:, 0:1], AF.Identity,
                             scale=-1.0, bias=cd[:, 0:1])
                return e.activation(sd[:, :], cv[:, 0:1], AF.Sqrt,
                                    bias=s_kc0[:, 0:1])
            tk_sd = b.s(mth, waits=[tk_sum, tk_sq2, h0tok.get(tc - 2), tk_kc])
            mth_of[tc] = tk_sd
            tk_rs = b.v(lambda e, rs=rs, sd=sd: e.reciprocal(rs[:, :],
                                                             sd[:, :]),
                        waits=[tk_sd])
            h0tok[tc] = b.v(lambda e, buf=buf, h0=h0, cm=cm, rs=rs:
                            e.tensor_scalar(
                h0[:, :], buf[:, :], cm[:, 0:1], rs[:, :],
                ALU.subtract, ALU.mult), waits=[tk_rs])
        tca = pair * 2
        for kk in range(NKC):
            bank = pt[kk % 2]
            def tr2(e, kk=kk, bank=bank):
                e.transpose(bank[:, 0:128], s_h0a[:, kk * 128:(kk + 1) * 128],
                            s_idb[:, :])
                return e.transpose(bank[:, 128:256],
                                   s_h0b[:, kk * 128:(kk + 1) * 128], s_idb[:, :])
            tk_tr = b.t(tr2, waits=[h0tok[tca], h0tok[tca + 1], tk_const["idb"],
                                    tk_ev_bank.get(kk % 2)])
            tk_ev = b.s(lambda e, kk=kk, bank=bank, tca=tca: e.activation(
                s_h1T[:, kk * TLOC + tca * 128:kk * TLOC + tca * 128 + 256],
                bank[:, 0:256], AF.Copy), waits=[tk_tr])
            tk_ev_bank[kk % 2] = tk_ev
            tok_h1T_plain = tk_ev
            prev_pair_tr = tk_tr
    bar_P0 = b.bar()

    b.gate = bar_P0
    tk_modmm = []
    for nt in range(3):
        def mmod0(e, nt=nt):
            last = None
            for kl in range(8):
                last = e.matmul(pb[nt][0:1, :], s_silc[:, kl:kl + 1],
                                s_wmod[:, kl * 1536 + nt * 512:
                                       kl * 1536 + (nt + 1) * 512],
                                start=(kl == 0), stop=False)
            return last
        tk_modmm.append(b.t(mmod0, waits=[tk_silc, tk_wm0]))
    tk_wm1 = b.dg(lambda e: ld_wm(e, 1), waits=[tk_modmm[-1]])
    for nt in range(3):
        def mmod1(e, nt=nt):
            last = None
            for kl in range(8):
                last = e.matmul(pb[nt][0:1, :], s_silc[:, 8 + kl:9 + kl],
                                s_wmod[:, kl * 1536 + nt * 512:
                                       kl * 1536 + (nt + 1) * 512],
                                start=False, stop=(kl == 7))
            return last
        tk_modmm[nt] = b.t(mmod1, waits=[tk_wm1])
    tk_gmask = b.ds(lambda e: e.dma_start(s_gmask[:, :], gmask[:, :]))
    tk_modev = []
    for nt in range(3):
        def mev(e, nt=nt):
            ps = pb[nt][0:1, :]
            mh = s_modH[0:1, nt * 512:(nt + 1) * 512]
            mh2 = s_modH2[0:1, nt * 512:(nt + 1) * 512]
            mk = s_gmask[0:1, nt * 512:(nt + 1) * 512]
            e.tensor_scalar(s_mcA[:, :], ps, 0.0, None, ALU.is_ge)
            e.tensor_scalar(s_mcB[:, :], s_mcA[:, :], 2.0, -1.0,
                            ALU.mult, ALU.add)
            e.tensor_scalar(s_mcA[:, :], ps, 0.0, None, ALU.max)
            e.scalar_tensor_tensor(s_mcC[:, :], ps, -1.0, s_mcA[:, :],
                                   ALU.mult, ALU.max)
            e.tensor_scalar(s_mcA[:, :], s_mcC[:, :], 0.25, None, ALU.max)
            e.tensor_tensor(s_mcC[:, :], s_mcB[:, :], s_mcA[:, :], ALU.mult)
            e.tensor_copy(mh, ps)
            e.tensor_tensor(s_mcA[:, :], s_mcC[:, :], mh, ALU.subtract)
            e.tensor_tensor(s_mcB[:, :], s_mcA[:, :], mk, ALU.mult)
            return e.tensor_tensor(mh2, mh, s_mcB[:, :], ALU.add)
        tk_modev.append(b.v(mev, waits=[tk_modmm[nt], tk_gmask]))
    tk_modsave = b.dg(lambda e: e.dma_start(agA_in[0:1, 0:1536], s_modH2[0:1, :]),
                      waits=tk_modev)

    _gate_save = b.gate
    b.gate = []
    tk_agA = b.cc(lambda e: e.collective_compute(
        "AllGather", ALU.bypass, replica_groups=[list(range(NCORES))],
        ins=[agA_in.ap().opt()], outs=[agA_out.ap().opt()]),
        waits=[tk_modsave, tk_wqsave] + tk_wssave)

    tk_mn = [b.ds(lambda e, r=r: e.dma_start(
        AP(s_mnat, r * 12 * 128, [[128, 12], [1, 128]]),
        AP(agA_out, r * 1792, [[128, 12], [1, 128]])), waits=[tk_agA])
        for r in range(NCORES)]
    tk_modT_t = b.t(lambda e: e.transpose(pb[4][:, 0:96],
                                          AP(s_mnat, 0, [[128, 96], [1, 128]]),
                                          s_idf[0:96, 0:96]),
                    waits=tk_mn + [tk_const["idf"], tk_modmm[2]])
    tk_modT = b.v(lambda e: e.tensor_copy(s_modT[:, :], pb[4][:, 0:96]),
                  waits=[tk_modT_t])
    tk_ca1 = b.v(lambda e: e.tensor_scalar(s_ca1[:, :], s_modT[:, 16:32], 1.0,
                                           None, ALU.add), waits=[tk_modT])

    tk_h1m = []
    for k in range(NKC):
        tk_h1m.append(b.v(lambda e, k=k: e.tensor_scalar(
            s_h1T[:, k * TLOC:(k + 1) * TLOC], s_h1T[:, k * TLOC:(k + 1) * TLOC],
            s_ca1[:, k:k + 1], s_modT[:, k:k + 1], ALU.mult, ALU.add),
            waits=[tk_ca1, tok_h1T_plain]))
    tok_h1T = tk_h1m[-1]
    b.gate = _gate_save
    tap("modT", s_modT)
    tap("ca1", s_ca1)
    tap("h1T", s_h1T)



    def qkv_panel(g, which, panel_tok, panelbuf, ncols, col0, prev,
                  rot_gp=True):
        nheads = ncols // 128
        h0l = col0 // 128
        dstT = s_qT if which == "q" else s_kT
        hist = {}
        rot_hist = {}
        trq_hist = {}
        prev = prev or {"ev": None, "hist": {}, "rot": {}, "trq": {}}
        ev_cell = [prev["ev"]]
        pend = [None]

        def histw(tc):
            return hist[tc] if tc >= 0 else prev["hist"].get(tc + 8)

        def rotw(tc):
            return rot_hist[tc] if tc >= 0 else prev["rot"].get(tc + 8, ())

        def trqw(tc):
            return trq_hist[tc] if tc >= 0 else prev["trq"].get(tc + 8)

        def emit_trq(tc, qr, rot_done):
            def trq(e, qr=qr, nheads=nheads):
                last = None
                for hh in range(nheads):
                    last = e.transpose(pt[1][:, hh * 128:(hh + 1) * 128],
                                       qr[:, hh * 128:(hh + 1) * 128],
                                       s_idb[:, :])
                return last
            tk_trq = b.t(trq, waits=list(rot_done) + [ev_cell[0]])
            trq_hist[tc] = tk_trq
            def evq(e, tc=tc, h0l=h0l, nheads=nheads, dstT=dstT):
                dst = AP(dstT, h0l * TLOC + tc * 128,
                         [[GH * TLOC, 128], [TLOC, nheads], [1, 128]])
                return e.tensor_copy(dst, AP(pt[1], 0, [[1024, 128],
                                                        [128, nheads],
                                                        [1, 128]]))
            ev_cell[0] = b.v(evq, waits=[tk_trq])

        for tc in range(NCH):
            bank = pb[tc % 2]
            def mmf(e, tc=tc, panelbuf=panelbuf, ncols=ncols, bank=bank):
                last = None
                for k in range(NKC):
                    last = e.matmul(
                        bank[:, 0:ncols],
                        s_h1T[:, k * TLOC + tc * 128:k * TLOC + tc * 128 + 128],
                        panelbuf[:, k * ncols:(k + 1) * ncols],
                        start=(k == 0), stop=(k == NKC - 1))
                return last
            tk_mm = b.t(mmf, waits=[panel_tok, tok_h1T, histw(tc - 2)])
            sil = s_sil0 if tc % 2 == 0 else s_sil1
            qr = s_qr0 if tc % 2 == 0 else s_qr1
            if which in ("q", "k"):
                tk_sil = b.s(lambda e, bank=bank, sil=sil, ncols=ncols:
                             e.activation(sil[:, 0:ncols], bank[:, 0:ncols],
                                          AF.Silu),
                             waits=[tk_mm] + list(rotw(tc - 2)))
                hist[tc] = tk_sil
                ca = AP(s_cos, tc * 64, [[NCH * 64, 128], [0, nheads], [1, 64]])
                sa = AP(s_sin, tc * 64, [[NCH * 64, 128], [0, nheads], [1, 64]])
                x1 = AP(sil, 0, [[512, 128], [128, nheads], [1, 64]])
                x2 = AP(sil, 64, [[512, 128], [128, nheads], [1, 64]])
                r1 = AP(qr, 0, [[512, 128], [128, nheads], [1, 64]])
                r2 = AP(qr, 64, [[512, 128], [128, nheads], [1, 64]])
                ta = AP(s_rt0 if tc % 2 == 0 else s_rt1, 0,
                        [[256, 128], [64, nheads], [1, 64]])
                tb = AP(s_rt2 if tc % 2 == 0 else s_rt3, 0,
                        [[256, 128], [64, nheads], [1, 64]])
                tk_r1 = b.v(lambda e, x1=x1, x2=x2, ta=ta, r1=r1, ca=ca, sa=sa: [
                    e.tensor_tensor(ta, x2, sa, ALU.mult),
                    e.tensor_tensor(r1, x1, ca, ALU.mult)][-1],
                    waits=[tk_sil, tk_const["cos"], tk_const["sin"],
                           trqw(tc - 2)])
                tk_r1b = b.v(lambda e, r1=r1, ta=ta: e.tensor_tensor(
                    r1, r1, ta, ALU.subtract), waits=[tk_r1])
                eng2 = b.g if rot_gp else b.v
                tk_r2 = eng2(lambda e, x1=x1, tb=tb, sa=sa: e.tensor_tensor(
                    tb, x1, sa, ALU.mult),
                    waits=[tk_sil, trqw(tc - 2)])
                tk_r2b = eng2(lambda e, r2=r2, x2=x2, ca=ca, tb=tb: [
                    e.tensor_tensor(r2, x2, ca, ALU.mult),
                    e.tensor_tensor(r2, r2, tb, ALU.add)][-1], waits=[tk_r2])
                rot_done = (tk_r1b, tk_r2b)
                rot_hist[tc] = rot_done
                if which == "k":
                    dkap = AP(s_dk, g * GH + h0l, [[H, 128], [1, nheads], [0, 128]])
                    qrap = AP(qr, 0, [[512, 128], [128, nheads], [1, 128]])
                    k2ap = AP(s_k2, tc * 1024 + h0l * 128,
                              [[NCH * GH * 128, 128], [128, nheads], [1, 128]])
                    b.v(lambda e, k2ap=k2ap, qrap=qrap, dkap=dkap: e.tensor_tensor(
                        k2ap, qrap, dkap, ALU.mult),
                        waits=list(rot_done) + [tk_const["dk"]])
                if pend[0] is not None:
                    emit_trq(*pend[0])
                pend[0] = (tc, qr, rot_done)
            else:
                def evv(e, bank=bank, tc=tc, h0l=h0l, ncols=ncols):
                    return e.copy(
                        s_vt[:, tc * 1024 + h0l * 128:tc * 1024 + h0l * 128 + ncols],
                        bank[:, 0:ncols])
                hist[tc] = b.s(evv, waits=[tk_mm])
                ev_cell[0] = hist[tc]
        if pend[0] is not None:
            emit_trq(*pend[0])
        return {"ev": ev_cell[0], "hist": hist, "rot": rot_hist,
                "trq": trq_hist}

    def emit_group(g, mode, bufs=None):
        ncols = 512 if mode == "a" else 256
        if bufs is None:
            bufs = (s_wh0, s_wh1) if mode == "a" else (s_wp0, s_wp1)
        npan = 1024 // ncols
        order = [("q", wq), ("k", wk), ("v", wv)]
        panlist = [(which, wsrc, p) for which, wsrc in order for p in range(npan)]
        panel_res = {}
        prev = None
        for i, (which, wsrc, p) in enumerate(panlist):
            buf = bufs[i % 2]
            def ldp(e, wsrc=wsrc, p=p, ncols=ncols, buf=buf):
                src = AP(wsrc, g * 1024 + p * ncols,
                         [[D, 128], [D * 128, NKC], [1, ncols]])
                dst = AP(buf, 0, [[NKC * ncols, 128], [ncols, NKC], [1, ncols]])
                return e.dma_start(dst, src)
            pv = panel_res.get(i - 2)
            tk_ld = b.dg(ldp, waits=[pv["hist"].get(NCH - 1)] if pv else [])
            panel_res[i] = qkv_panel(g, which, tk_ld, buf, ncols, p * ncols,
                                     prev)
            prev = panel_res[i]
        last = panel_res[len(panlist) - 1]
        return last["ev"] or last["hist"][NCH - 1]

    def emit_g1():
        tk_ld = b.dg(lambda e: e.dma_start(
            AP(s_wh0, 0, [[NKC * 512, 128], [128, NKC], [1, 128]]),
            AP(wg1, 0, [[128, 128], [128 * 128, NKC], [1, 128]])))
        tk_ld2 = b.dg(lambda e: e.dma_start(s_wg2[:, :], wg2_i[:, :]))
        last = None
        for tt in range(2):
            def mmf(e, tt=tt):
                lastx = None
                for k in range(NKC):
                    lastx = e.matmul(
                        pb[4][:, :], s_wh0[:, k * 128:(k + 1) * 128],
                        s_h1T[:, k * TLOC + tt * 512:k * TLOC + (tt + 1) * 512],
                        start=(k == 0), stop=(k == NKC - 1))
                return lastx
            tk_mm = b.t(mmf, waits=[tk_ld, tok_h1T, last])
            last = b.v(lambda e, tt=tt: e.tensor_copy(
                s_g1T[:, tt * 512:(tt + 1) * 512], pb[4][:, :]), waits=[tk_mm])
        return last, tk_ld2

    def emit_L(g):
        tk_z = b.v(lambda e: e.memset(s_S[:, :], 0.0))
        last_stt = {0: tk_z, 1: tk_z}
        for c in range(NCH):
            for hb in range(2):
                bank = pb[2 + hb]
                def umm(e, c=c, hb=hb, bank=bank):
                    last = None
                    for q in range(4):
                        o = c * 1024 + (hb * 4 + q) * 128
                        last = e.matmul(bank[:, q * 128:(q + 1) * 128],
                                        s_k2[:, o:o + 128], s_vt[:, o:o + 128],
                                        **MMK)
                    return last
                tk_u = b.t(umm, waits=[last_stt[hb]])
                def sttf(e, c=c, hb=hb, bank=bank, g=g):
                    last = None
                    for q in range(4):
                        hl = hb * 4 + q
                        last = e.scalar_tensor_tensor(
                            s_S[:, hl * 128:(hl + 1) * 128],
                            s_S[:, hl * 128:(hl + 1) * 128], LAMC[g * GH + hl],
                            bank[:, q * 128:(q + 1) * 128], ALU.mult, ALU.add)
                    return last
                last_stt[hb] = b.v(sttf, waits=[tk_u])
        tk_save = b.dg(lambda e, g=g: e.dma_start(agL_in[g][:, :], s_S[:, :]),
                       waits=[last_stt[0], last_stt[1]])
        tk_ag = b.cc(lambda e, g=g: e.collective_compute(
            "AllGather", ALU.bypass, replica_groups=[list(range(NCORES))],
            ins=[agL_in[g].ap().opt()], outs=[agL_out[g].ap().opt()]),
            waits=[tk_save])
        return tk_ag

    def emit_L_comb(g, tk_ag, extra):
        last = tk_ag
        for j in range(7):
            tk_ld = b.ds(lambda e, j=j, g=g: e.dma_start(
                s_lgj[:, :], AP(agL_out[g], j * 128 * (GH * 128),
                                [[GH * 128, 128], [1, GH * 128]])),
                waits=[last] + (extra if j == 0 else []))
            def comb(e, j=j, g=g):
                lastx = None
                for hl in range(GH):
                    col = j * H + g * GH + hl
                    if j == 0:
                        lastx = e.tensor_scalar(
                            s_S[:, hl * 128:(hl + 1) * 128],
                            s_lgj[:, hl * 128:(hl + 1) * 128],
                            s_coef[:, col:col + 1], None, ALU.mult)
                    else:
                        lastx = e.scalar_tensor_tensor(
                            s_S[:, hl * 128:(hl + 1) * 128],
                            s_lgj[:, hl * 128:(hl + 1) * 128],
                            s_coef[:, col:col + 1],
                            s_S[:, hl * 128:(hl + 1) * 128], ALU.mult, ALU.add)
                return lastx
            last = b.v(comb, waits=[tk_ld, tk_const["coef"]])
        tk_sb = b.v(lambda e: e.tensor_copy(s_Sb[:, :], s_S[:, :]),
                    waits=[last])
        return tk_sb

    def emit_scan(g, tok_Sb, onT, sgg, tok_g1, tok_wg2):
        tk_gate = {}
        prev = None
        for c in range(NCH):
            for half in range(2):
                bank = pb[4 + ((c * 2 + half) % 2)]
                def gmm(e, c=c, half=half, bank=bank):
                    return e.matmul(
                        bank[:, :], s_g1T[:, c * 128:(c + 1) * 128],
                        s_wg2[:, g * 1024 + half * 512:g * 1024 + (half + 1) * 512],
                        **MMK)
                tk_mm = b.t(gmm, waits=[tok_g1, tok_wg2, prev])
                prev = b.s(lambda e, c=c, half=half, bank=bank, sgg=sgg:
                           e.activation(
                               sgg[:, c * 1024 + half * 512:
                                   c * 1024 + (half + 1) * 512],
                               bank[:, :], AF.Sigmoid), waits=[tk_mm])
                tk_gate[(c, half)] = prev
        prev_sb = tok_Sb
        prev_onT = None
        prev_pt = {0: None, 1: None}
        prev_oev = {0: None, 1: None}
        prev_stt = {0: tk_gate[(NCH - 1, 1)], 1: tk_gate[(NCH - 1, 1)]}
        prev_omm = {0: None, 1: None}
        for c in range(NCH):
            oev = []
            stts = []
            for hb in range(2):
                psc = pb[hb]
                po = pb[2 + hb]
                pu = pb[4 + hb]
                hg0 = g * GH + hb * 4
                def smm(e, c=c, hb=hb, psc=psc):
                    last = None
                    for q in range(4):
                        o = (hb * 4 + q) * TLOC + c * 128
                        last = e.matmul(psc[:, q * 128:(q + 1) * 128],
                                        s_kT[:, o:o + 128], s_qT[:, o:o + 128],
                                        **MMK)
                    return last
                tk_sc = b.t(smm, waits=[prev_pt[hb]])
                pts = s_pt0 if hb == 0 else s_pt1
                qts = s_qt0 if hb == 0 else s_qt1
                tk_pt = b.v(lambda e, psc=psc, pts=pts, hg0=hg0: e.tensor_tensor(
                    pts[:, :], psc[:, :], s_dt[:, hg0 * 128:(hg0 + 4) * 128],
                    ALU.mult), waits=[tk_sc, tk_const["dt"]])
                prev_pt[hb] = tk_pt
                tk_qt = b.g(lambda e, c=c, hb=hb, qts=qts, hg0=hg0:
                            e.tensor_tensor(
                                qts[:, :],
                                AP(s_qT, (hb * 4) * TLOC + c * 128,
                                   [[GH * TLOC, 128], [TLOC, 4], [1, 128]]),
                                AP(s_dq, hg0 * 128,
                                   [[H * 128, 128], [128, 4], [1, 128]]),
                                ALU.mult),
                            waits=[tk_const["dq"], prev_omm[hb]])
                def omm(e, c=c, hb=hb, po=po, pts=pts, qts=qts):
                    last = None
                    for q in range(4):
                        hl = hb * 4 + q
                        o = c * 1024 + hl * 128
                        e.matmul(po[:, q * 128:(q + 1) * 128],
                                 pts[:, q * 128:(q + 1) * 128],
                                 s_vt[:, o:o + 128],
                                 start=True, stop=False, skip_group_check=True)
                        last = e.matmul(po[:, q * 128:(q + 1) * 128],
                                        qts[:, q * 128:(q + 1) * 128],
                                        s_Sb[:, hl * 128:(hl + 1) * 128],
                                        start=False, stop=True,
                                        skip_group_check=True)
                    return last
                tk_o = b.t(omm, waits=[tk_pt, tk_qt, prev_sb, prev_oev[hb]])
                prev_omm[hb] = tk_o
                def umm(e, c=c, hb=hb, pu=pu):
                    last = None
                    for q in range(4):
                        o = c * 1024 + (hb * 4 + q) * 128
                        last = e.matmul(pu[:, q * 128:(q + 1) * 128],
                                        s_k2[:, o:o + 128], s_vt[:, o:o + 128],
                                        **MMK)
                    return last
                tk_u = b.t(umm, waits=[prev_stt[hb]])
                oh = s_oh0 if c % 2 == 0 else s_oh1
                oev.append(b.s(lambda e, po=po, hb=hb, oh=oh: e.copy(
                    oh[:, hb * 512:(hb + 1) * 512], po[:, :]), waits=[tk_o]))
                prev_oev[hb] = oev[-1]
                def sttf(e, c=c, hb=hb, pu=pu, g=g):
                    last = None
                    for q in range(4):
                        hl = hb * 4 + q
                        last = e.scalar_tensor_tensor(
                            s_S[:, hl * 128:(hl + 1) * 128],
                            s_S[:, hl * 128:(hl + 1) * 128], LAMC[g * GH + hl],
                            pu[:, q * 128:(q + 1) * 128], ALU.mult, ALU.add)
                    return last
                stts.append(b.v(sttf, waits=[tk_u, tk_o]))
                prev_stt[hb] = stts[-1]
            prev_sb = b.s(lambda e: e.activation(s_Sb[:, :], s_S[:, :], AF.Copy),
                          waits=stts)
            oh = s_oh0 if c % 2 == 0 else s_oh1
            tk_gg = b.v(lambda e, oh=oh, c=c, sgg=sgg: e.tensor_tensor(
                s_gg[:, :], oh[:, :], sgg[:, c * 1024:(c + 1) * 1024], ALU.mult),
                waits=oev + [tk_gate[(c, 1)]])
            tk_sq = b.g(lambda e: e.tensor_tensor(s_sq[:, :], s_gg[:, :],
                                                  s_gg[:, :], ALU.mult),
                        waits=[tk_gg])
            tk_ms = b.v(lambda e: e.reduce_sum(
                AP(s_ms, 0, [[GH, 128], [1, GH]]),
                AP(s_sq, 0, [[1024, 128], [128, GH], [1, 128]]), axis=AX.X),
                waits=[tk_sq])
            tk_sdg = b.s(lambda e: e.activation(s_rsg[:, :], s_ms[:, :], AF.Sqrt,
                                                bias=s_kc1[:, 0:1],
                                                scale=s_kc2[:, 0:1]),
                         waits=[tk_ms])
            tk_rsg = b.v(lambda e: e.reciprocal(s_rsg2[:, :], s_rsg[:, :]),
                         waits=[tk_sdg])
            tk_on = b.v(lambda e: e.tensor_tensor(
                s_on[:, :], AP(s_gg, 0, [[1024, 128], [128, GH], [1, 128]]),
                AP(s_rsg2, 0, [[GH, 128], [1, GH], [0, 128]]), ALU.mult),
                waits=[tk_rsg])
            if c == 0:
                tap(f"oh{g}", s_oh0)
                tap(f"gg{g}", s_gg)
                tap(f"rsg{g}", s_rsg2)
                tap(f"pt{g}", s_pt0)
                tap(f"on{g}", s_on)
            def tro(e):
                last = None
                for kl in range(GH):
                    last = e.transpose(pt[0][:, kl * 128:(kl + 1) * 128],
                                       s_on[:, kl * 128:(kl + 1) * 128],
                                       s_idb[:, :])
                return last
            tk_tro = b.t(tro, waits=[tk_on, prev_onT])
            prev_onT = b.s(lambda e, c=c, onT=onT: e.copy(
                onT[:, c * 1024:(c + 1) * 1024], pt[0][:, :]), waits=[tk_tro])
        return prev_onT

    tap("wqTs", s_wqTs)

    def pld(g, wsrc, p, buf, waits):
        def ldp(e, wsrc=wsrc, p=p, buf=buf, g=g):
            src = AP(wsrc, g * 1024 + p * 512,
                     [[D, 128], [D * 128, NKC], [1, 512]])
            dst = AP(buf, 0, [[NKC * 512, 128], [512, NKC], [1, 512]])
            return e.dma_start(dst, src)
        return b.dg(ldp, waits=waits)

    def emit_grp(g, bufA, bufB):
        b.gate = b.bar()
        rK0 = qkv_panel(g, "k", pld(g, wk, 0, bufA, []), bufA, 512, 0, None)
        rK1 = qkv_panel(g, "k", pld(g, wk, 1, bufB, []), bufB, 512, 512, rK0)
        rV0 = qkv_panel(g, "v", pld(g, wv, 0, bufA, [rK0["hist"][NCH - 1]]),
                        bufA, 512, 0, rK1)
        rV1 = qkv_panel(g, "v", pld(g, wv, 1, bufB, [rK1["hist"][NCH - 1]]),
                        bufB, 512, 512, rV0)
        qld0 = pld(g, wq, 0, bufA, [rV0["hist"][NCH - 1]])
        qld1 = pld(g, wq, 1, bufB, [rV1["hist"][NCH - 1]])
        b.gate = b.bar()
        tk_ag = emit_L(g)
        rQ0 = qkv_panel(g, "q", qld0, bufA, 512, 0, rV1, rot_gp=False)
        rQ1 = qkv_panel(g, "q", qld1, bufB, 512, 512, rQ0, rot_gp=False)
        tok_S = emit_L_comb(g, tk_ag, [rQ0["hist"][NCH - 1],
                                       rQ1["hist"][NCH - 1]])
        return tok_S, rQ1

    tok_S0, _ = emit_grp(0, s_wh0, s_wh1)
    tk_agB = b.cc(lambda e: e.collective_compute(
        "AllGather", ALU.bypass, replica_groups=[list(range(NCORES))],
        ins=[agB_in.ap().opt()], outs=[agB_out.ap().opt()]), waits=[tk_wqsave])
    b.gate = b.bar()
    tok_g1T, tok_wg2 = emit_g1()
    tap("qT0", s_qT)
    tap("kT0", s_kT)
    tap("k20", s_k2)
    tap("vt0", s_vt)
    tap("g1T", s_g1T)
    tap("S0", s_S)
    b.gate = b.bar()
    tok_onT1 = emit_scan(0, tok_S0, s_onT1, s_sgg1, tok_g1T, tok_wg2)
    bar_scan0 = [t for t in b.bar() if t[0] in ("T", "V", "S", "G")]

    tap("onT1", s_onT1)
    tap("sgg1", s_sgg1)

    tok_S1, _ = emit_grp(1, s_whA, s_wh1)

    b.gate = bar_scan0

    bc_mm_prev = [None]
    bc_post_prev = {0: None, 1: None}

    def bc_build(kind, post):
        for nt in range(4):
            if kind == "ws":
                r0 = nt * 2
                tk_ra = b.da(lambda e, r0=r0: e.dma_start(
                    s_row[0:1, 0:256],
                    AP(agA_out, r0 * 1792 + 1536, [[256, 1], [1, 256]])),
                    waits=[tk_agA, bc_mm_prev[0]])
                tk_r = b.da(lambda e, r0=r0: e.dma_start(
                    s_row[0:1, 256:512],
                    AP(agA_out, (r0 + 1) * 1792 + 1536, [[256, 1], [1, 256]])),
                    waits=[tk_ra])
            elif kind == "fb":
                tk_r = b.da(lambda e, nt=nt: e.dma_start(
                    s_row[0:1, :], fb[0:1, nt * 512:(nt + 1) * 512]),
                    waits=[bc_mm_prev[0]])
            else:
                m = kind + nt * 512
                r, o = m // 1536, m % 1536
                tk_r = b.da(lambda e, r=r, o=o: e.dma_start(
                    s_row[0:1, :], AP(agA_out, r * 1792 + o, [[512, 1], [1, 512]])),
                    waits=[tk_agA, bc_mm_prev[0]])
            tk_m = b.t(lambda e, nt=nt: e.matmul(
                pb[4 + (nt % 2)][:, :], s_ones[0:1, :], s_row[0:1, :], **MMK),
                waits=[tk_r, tk_ones, bc_post_prev[nt % 2]])
            bc_mm_prev[0] = tk_m
            bc_post_prev[nt % 2] = post(nt, tk_m)
        return bc_post_prev[(4 - 1) % 2]

    def copy_post(dst):
        def post(nt, tk_m):
            return b.s(lambda e, nt=nt: e.copy(
                dst[:, nt * 512:(nt + 1) * 512], pb[4 + (nt % 2)][:, :]),
                waits=[tk_m])
        return post

    tk_gab = bc_build(2 * D, copy_post(s_gab))
    tk_sbb = bc_build(3 * D, copy_post(s_sbb))
    tk_gb1 = bc_build(5 * D, copy_post(s_wsgb))
    tk_gb2 = bc_build(5 * D, copy_post(s_bgbb))
    tk_cb1b = bc_build(4 * D, lambda nt, tk_m: b.v(
        lambda e, nt=nt: e.tensor_scalar(s_cb1b[:, nt * 512:(nt + 1) * 512],
                                         pb[4 + (nt % 2)][:, :], 1.0, None,
                                         ALU.add), waits=[tk_m]))
    tk_ws_bc = bc_build("ws", lambda nt, tk_m: b.v(
        lambda e, nt=nt: e.tensor_tensor(s_wsgb[:, nt * 512:(nt + 1) * 512],
                                         pb[4 + (nt % 2)][:, :],
                                         s_wsgb[:, nt * 512:(nt + 1) * 512],
                                         ALU.mult), waits=[tk_m, tk_gb1]))
    tk_fb_bc = bc_build("fb", lambda nt, tk_m: b.v(
        lambda e, nt=nt: e.tensor_tensor(s_bgbb[:, nt * 512:(nt + 1) * 512],
                                         pb[4 + (nt % 2)][:, :],
                                         s_bgbb[:, nt * 512:(nt + 1) * 512],
                                         ALU.mult), waits=[tk_m, tk_gb2]))
    tap("gab", s_gab)
    tap("cb1b", s_cb1b)
    tap("sbb", s_sbb)
    tap("wsgb", s_wsgb)
    tap("bgbb", s_bgbb)

    b.gate = b.bar()
    tok_onT2 = emit_scan(1, tok_S1, s_onT2, s_sgg2, tok_g1T, tok_wg2)
    tap("onT2", s_onT2)
    tap("S1", s_S)

    b.gate = b.bar()
    tk_wo_ld = []
    for hh in range(4):
        def ldwo(e, hh=hh):
            src = AP(wo, hh * 512 * D, [[D, 128], [D * 128, 4], [1, D]])
            dst = AP(s_woP, hh * 4 * D, [[NKC * D, 128], [D, 4], [1, D]])
            return e.dma_start(dst, src)
        tk_wo_ld.append(b.dg(ldwo))
    tk_wo_sc = []
    for k in range(NKC):
        tk_wo_sc.append(b.v(lambda e, k=k: e.tensor_scalar(
            s_woP[:, k * D:(k + 1) * D], s_woP[:, k * D:(k + 1) * D],
            s_nw[:, k:k + 1], None, ALU.mult),
            waits=[tk_wo_ld[k // 4], tk_const["nw"]]))
    tok_woP = tk_wo_sc[-1]

    tk_warm3 = b.s(lambda e: e.activation(s_wrm[:, :], s_kc0[:, 0:1], AF.Sqrt))
    tk_stage = {}
    tk_xqT = {}
    tk_bf_of = {}
    tk_wo_mm = {}
    tk_xs_of = {}
    prev_x = None
    prev_t1 = {}
    prev_bf = None
    prev_trx = {0: None, 1: None}
    prev_sd = None
    prev_rs_read = None
    prev_mth3 = None

    def emit_trx(c):
        bfd_c = s_bfd if c % 2 == 0 else s_bfd1
        for half in range(2):
            def trx(e, half=half, bfd_c=bfd_c):
                last = None
                for kl in range(8):
                    last = e.transpose(
                        pt[half][:, kl * 128:(kl + 1) * 128],
                        bfd_c[:, (half * 8 + kl) * 128:
                              (half * 8 + kl + 1) * 128],
                        s_idb[:, :])
                return last
            tk_trx = b.t(trx, waits=[tk_bf_of[c], tk_xqT.get((c - 1, half))])
            prev_trx[half] = tk_trx
            dstq = s_xqTa if half == 0 else s_xqTb
            tk_xqT[(c, half)] = b.s(lambda e, c=c, dstq=dstq, half=half:
                                    e.copy(
                                        dstq[:, c * 1024:(c + 1) * 1024],
                                        pt[half][:, :]), waits=[tk_trx])

    def emit_ln(c, xb2, tk_res):
        nonlocal prev_mth3, prev_sd, prev_rs_read, prev_bf
        tk_sqd = b.s(lambda e, xb2=xb2: e.activation(
            s_h2d[:, :], xb2[:, :], AF.Square),
            waits=tk_res + [prev_mth3, tk_warm3, prev_bf])
        tk_sum = b.v(lambda e, xb2=xb2: e.reduce_sum(
            s_ca_[:, :], xb2[:, :], axis=AX.X), waits=tk_res + [prev_mth3])
        tk_sq2 = b.v(lambda e: e.reduce_sum(
            s_cb_[:, :], s_h2d[:, :], axis=AX.X), waits=[tk_sqd])
        def mth3(e):
            e.mul(s_cm_[:, :], s_ca_[:, 0:1], 1.0 / 2048.0)
            e.mul(s_cd_[:, :], s_cb_[:, 0:1], 1.0 / 2048.0)
            e.activation(s_ce_[:, :], s_cm_[:, 0:1], AF.Square)
            e.activation(s_cv_[:, :], s_ce_[:, 0:1], AF.Identity,
                         scale=-1.0, bias=s_cd_[:, 0:1])
            return e.activation(s_sd[:, :], s_cv_[:, 0:1], AF.Sqrt,
                                bias=s_kc0[:, 0:1])
        tk_sd = b.s(mth3, waits=[tk_sum, tk_sq2, prev_sd, prev_rs_read, tk_kc])
        prev_mth3 = tk_sd
        prev_sd = tk_sd
        tk_rs = b.v(lambda e: e.reciprocal(s_rs[:, :], s_sd[:, :]), waits=[tk_sd])
        tk_h20 = b.v(lambda e, xb2=xb2: e.tensor_scalar(
            s_h2d[:, :], xb2[:, :], s_cm_[:, 0:1], s_rs[:, :],
            ALU.subtract, ALU.mult), waits=[tk_rs, prev_bf])
        prev_rs_read = tk_h20
        tk_m1 = b.g(lambda e: e.tensor_tensor(s_h2d[:, :], s_h2d[:, :],
                                              s_cb1b[:, :], ALU.mult),
                    waits=[tk_h20, tk_cb1b])
        tk_m2 = b.g(lambda e: e.tensor_tensor(s_h2d[:, :], s_h2d[:, :],
                                              s_sbb[:, :], ALU.add),
                    waits=[tk_m1, tk_sbb])
        tk_amax = b.v(lambda e: e.reduce_max(s_am[:, 0:1], s_h2d[:, :],
                                             axis=AX.X,
                                             apply_absolute_value=True),
                      waits=[tk_m2])
        tk_x1 = b.s(lambda e: e.activation(s_rws[:, :], s_am[:, 0:1], AF.Relu,
                                           bias=s_kcx1[:, 0:1],
                                           scale=1.0 / 127.0),
                    waits=[tk_amax])
        tk_xs = b.s(lambda e, c=c: e.activation(
            s_xsal[:, c:c + 1], s_rws[:, :], AF.Identity,
            bias=s_kcx2[:, 0:1]), waits=[tk_x1])
        tk_xs_of[c] = tk_xs
        tk_rxs = b.v(lambda e, c=c: e.reciprocal(s_rs[:, :],
                                                 s_xsal[:, c:c + 1]),
                     waits=[tk_xs])
        tk_qf = b.v(lambda e: e.tensor_scalar(s_h2d[:, :], s_h2d[:, :],
                                              s_rs[:, :], None, ALU.mult),
                    waits=[tk_rxs])
        tk_i8 = b.v(lambda e: e.tensor_copy(s_i8d[:, :], s_h2d[:, :]),
                    waits=[tk_qf, prev_bf])
        bfd_c = s_bfd if c % 2 == 0 else s_bfd1
        tk_bf = b.s(lambda e, bfd_c=bfd_c: e.activation(
            bfd_c[:, :], s_i8d[:, :], AF.Copy),
            waits=[tk_i8, prev_trx[0], prev_trx[1]])
        prev_bf = tk_bf
        tk_bf_of[c] = tk_bf
        tk_xp = b.g(lambda e, xb2=xb2: e.tensor_tensor(xb2[:, :], xb2[:, :],
                                              s_bgbb[:, :], ALU.add),
                    waits=[tk_m2, tk_fb_bc, tk_amax])
        tk_stage[c] = b.da(lambda e, c=c, xb2=xb2: e.dma_start(
            AP(out_s, c * 128 * D, [[D, 128], [1, D]]), xb2[:, :]),
            waits=[tk_xp])

    pend_ln = {}
    for c in range(NCH):
        if c >= 3:
            emit_trx(c - 3)
        xbuf = s_x0 if c % 2 == 0 else s_x1
        xb2 = (s_x2, s_x2b, s_x2c)[c % 3]
        tk_x = b.ds(lambda e, c=c, xbuf=xbuf: e.dma_start(
            xbuf[:, :], AP(x_s, c * 128 * D, [[D, 128], [1, D]])),
            waits=[prev_x])
        tk_res = []
        for oc in range(4):
            bank = pb[oc % 2]
            def womm(e, c=c, oc=oc, bank=bank):
                last = None
                for k in range(NKC):
                    src = s_onT1 if k < 8 else s_onT2
                    kl = k % 8
                    last = e.matmul(
                        bank[:, :],
                        src[:, c * 1024 + kl * 128:c * 1024 + kl * 128 + 128],
                        s_woP[:, k * D + oc * 512:k * D + (oc + 1) * 512],
                        start=(k == 0), stop=(k == NKC - 1))
                return last
            w2 = [tok_woP, tok_onT1, tok_onT2,
                  prev_t1.get((c, oc - 2)), prev_t1.get((c - 1, oc + 2))]
            tk_mm = b.t(womm, waits=w2)
            tk_wo_mm[(c, oc)] = tk_mm
            tk_t1 = b.v(lambda e, oc=oc, bank=bank, xb2=xb2: e.tensor_tensor(
                xb2[:, oc * 512:(oc + 1) * 512], bank[:, :],
                s_gab[:, oc * 512:(oc + 1) * 512], ALU.mult),
                waits=[tk_mm, tk_gab, tk_stage.get(c - 3)])
            prev_t1[(c, oc)] = tk_t1
            tk_res.append(b.g(lambda e, oc=oc, xbuf=xbuf, xb2=xb2:
                              e.tensor_tensor(
                xb2[:, oc * 512:(oc + 1) * 512],
                xb2[:, oc * 512:(oc + 1) * 512],
                xbuf[:, oc * 512:(oc + 1) * 512], ALU.add),
                waits=[tk_t1, tk_x]))
        prev_x = tk_res[-1]
        pend_ln[c] = (xb2, tk_res)
        if c >= 1:
            emit_ln(c - 1, *pend_ln.pop(c - 1))
    emit_ln(NCH - 1, *pend_ln.pop(NCH - 1))
    emit_trx(NCH - 3)
    emit_trx(NCH - 2)
    emit_trx(NCH - 1)

    tap("xqTa", s_xqTa)
    tap("xqTb", s_xqTb)
    tap("xsal", s_xsal)
    tap("end_modT", s_modT)
    tap("end_gab", s_gab)
    tap("end_wsgb", s_wsgb)
    tap("end_bgbb", s_bgbb)
    tap("end_cb1b", s_cb1b)
    tap("end_sbb", s_sbb)
    tap("end_xsal", s_xsal)

    SKIP_P3B = "stop_p3a" in DEBUG_TAPS
    p3a_tail = [prev_bf, prev_trx[0], prev_trx[1], prev_x]
    tk_wq_ld = {}
    for r in (range(NCORES) if not SKIP_P3B else []):
        def ldwq(e, r=r):
            src = AP(agB_out, r * D * 256, [[256, 128], [256 * 128, NKC], [1, 256]])
            dst = AP(s_wqT, r * 256, [[NKC * D, 128], [D, NKC], [1, 256]])
            return e.dma_start(dst, src)
        tk_wq_ld[r] = [b.ds, b.dg][r % 2](
            ldwq, waits=[tk_agB, tk_wo_mm.get((NCH - 1, r // 2))])
    prev_rb = None
    prev_fd = {}
    prev_et1 = {}
    final_dma = []
    for c in (range(NCH) if not SKIP_P3B else []):
        rbuf = s_rb0 if c % 2 == 0 else s_rb1
        tk_rb = b.ds(lambda e, c=c, rbuf=rbuf: e.dma_start(
            rbuf[:, :], AP(out_s, c * 128 * D, [[D, 128], [1, D]])),
            waits=[tk_stage[c], prev_rb] + ([prev_x] if c < 2 else []))
        obuf = s_ob0 if c % 2 == 0 else s_ob1
        tk_oc = []
        for oc in range(4):
            bank = pb[oc % 2]
            def fmm(e, c=c, oc=oc, bank=bank):
                last = None
                for k in range(NKC):
                    src = s_xqTa if k < 8 else s_xqTb
                    kl = k % 8
                    last = e.matmul(
                        bank[:, :],
                        src[:, c * 1024 + kl * 128:c * 1024 + kl * 128 + 128],
                        s_wqT[:, k * D + oc * 512:k * D + (oc + 1) * 512],
                        start=(k == 0), stop=(k == NKC - 1))
                return last
            wmm = [tk_wq_ld[2 * oc], tk_wq_ld[2 * oc + 1],
                   tk_xqT[(c, 0)], tk_xqT[(c, 1)],
                   prev_et1.get((c, oc - 2)), prev_et1.get((c - 1, oc + 2))]
            if c == 0 and oc < 2:
                wmm.append(prev_t1.get((NCH - 1, oc + 2)))
            tk_mm = b.t(fmm, waits=wmm)
            tk_et1 = b.v(lambda e, c=c, bank=bank, oc=oc: e.scalar_tensor_tensor(
                s_t1[:, :], bank[:, :], s_xsal[:, c:c + 1],
                s_wsgb[:, oc * 512:(oc + 1) * 512], ALU.mult, ALU.mult),
                waits=[tk_mm, tk_ws_bc, tk_xs_of[c]] +
                ([tk_stage[NCH - 1]] if c == 0 else []))
            prev_et1[(c, oc)] = tk_et1
            tk_oc.append(b.v(lambda e, oc=oc, obuf=obuf, rbuf=rbuf:
                             e.tensor_tensor(
                                 obuf[:, oc * 512:(oc + 1) * 512], s_t1[:, :],
                                 rbuf[:, oc * 512:(oc + 1) * 512], ALU.add),
                             waits=[tk_et1, tk_rb, prev_fd.get(c - 2)] +
                             (p3a_tail if c < 2 else [])))
        prev_rb = tk_oc[-1]
        final_dma.append(b.da(lambda e, c=c, obuf=obuf: e.dma_start(
            AP(out_s, c * 128 * D, [[D, 128], [1, D]]), obuf[:, :]),
            waits=tk_oc))
        prev_fd[c] = final_dma[-1]

    for nm, t_sb in end_taps:
        shp = [int(x) for x in t_sb.shape]
        dr = nc.dram_tensor("tap_" + nm, shp, t_sb.dtype, kind="ExternalOutput")
        b.ds(lambda e, dr=dr, t_sb=t_sb: e.dma_start(dr[:, :], t_sb[:, :]),
             waits=b.bar())

    b.s(lambda e: None,
        waits=[final_dma[-1] if final_dma else tk_stage[NCH - 1]])

    with nc.Block() as block:
        @block.tensor
        def _(e):
            for f in b.prog["tensor"]:
                f(e)

        @block.vector
        def _(e):
            for f in b.prog["vector"]:
                f(e)

        @block.scalar
        def _(e):
            for f in b.prog["scalar"]:
                f(e)

        @block.gpsimd
        def _(e):
            for f in b.prog["gpsimd"]:
                f(e)

        @block.sync
        def _(e):
            for f in b.prog["sync"]:
                f(e)

    return nc


def _host_constants(core):
    j = np.arange(64, dtype=np.float64)
    theta = BASEF ** (-2.0 * j / 128.0)
    T0 = core * TLOC
    cosx = np.zeros((128, NCH * 64), dtype=np.float32)
    sinx = np.zeros((128, NCH * 64), dtype=np.float32)
    for c in range(NCH):
        n = (T0 + c * 128 + np.arange(128, dtype=np.float64))[:, None]
        ang = n * theta[None, :]
        cosx[:, c * 64:(c + 1) * 64] = np.cos(ang)
        sinx[:, c * 64:(c + 1) * 64] = np.sin(ang)
    i = np.arange(128, dtype=np.float64)
    dt64 = np.zeros((128, H * 128), dtype=np.float64)
    dq64 = np.zeros((128, H * 128), dtype=np.float64)
    dk_c = np.zeros((128, H), dtype=np.float32)
    for h in range(H):
        ll = LOG_LAM[h]
        diff = i[None, :] - i[:, None]
        dt64[:, h * 128:(h + 1) * 128] = np.where(diff >= 0, np.exp(ll * diff), 0.0)
        dq64[:, h * 128:(h + 1) * 128] = np.exp(ll * (i[None, :] + 1.0))
        dk_c[:, h] = np.exp(ll * (128.0 - 1.0 - i)).astype(np.float32)
    coef = np.zeros((128, 7 * H), dtype=np.float32)
    for jj in range(7):
        for h in range(H):
            if jj < core:
                coef[:, jj * H + h] = np.float32(
                    np.exp(np.float64(LOG_LAM[h]) * 128.0 * 8.0 * (core - 1 - jj)))
    return (cosx, sinx,
            dt64.astype(np.float32).astype(ml_dtypes.bfloat16),
            dq64.astype(np.float32).astype(ml_dtypes.bfloat16),
            dk_c, coef)


def kernel(L0_attn_x, L0_attn_Wq, L0_attn_Wk, L0_attn_Wv, L0_attn_Wo,
           L0_attn_Wg1, L0_attn_Wg2, L0_attn_norm_weight, L0_ffn_weight,
           L0_ffn_bias, L0_c, L0_wmod):
    x = np.ascontiguousarray(np.asarray(L0_attn_x, dtype=np.float32).reshape(NTOK, D))
    wq = np.ascontiguousarray(np.asarray(L0_attn_Wq, dtype=np.float32))
    wk = np.ascontiguousarray(np.asarray(L0_attn_Wk, dtype=np.float32))
    wv = np.ascontiguousarray(np.asarray(L0_attn_Wv, dtype=np.float32))
    wo = np.ascontiguousarray(np.asarray(L0_attn_Wo, dtype=np.float32))
    wg1 = np.ascontiguousarray(np.asarray(L0_attn_Wg1, dtype=np.float32))
    wg2 = np.ascontiguousarray(np.asarray(L0_attn_Wg2, dtype=np.float32))
    nw = np.asarray(L0_attn_norm_weight, dtype=np.float32)
    fw = np.ascontiguousarray(np.asarray(L0_ffn_weight, dtype=np.float32))
    fbi = np.asarray(L0_ffn_bias, dtype=np.float32)
    cc = np.asarray(L0_c, dtype=np.float32)
    wm = np.ascontiguousarray(np.asarray(L0_wmod, dtype=np.float32))

    nw_t = np.ascontiguousarray(nw.reshape(16, 128).T)
    c_t = np.ascontiguousarray(cc.reshape(16, 128).T)
    fb = np.ascontiguousarray(fbi.reshape(1, D))
    idb = np.eye(128, dtype=np.float32).astype(ml_dtypes.bfloat16)
    idf = np.eye(128, dtype=np.float32)

    in_maps = []
    for core in range(NCORES):
        cosx, sinx, dt_c, dq_c, dk_c, coef = _host_constants(core)
        gm = np.zeros((1, 1536), np.float32)
        for o in range(1536):
            gidx = core * 1536 + o
            if 2 * D <= gidx < 3 * D or 5 * D <= gidx < 6 * D:
                gm[0, o] = 1.0
        in_maps.append({
            "x_s": np.ascontiguousarray(x[core * TLOC:(core + 1) * TLOC]),
            "wq": wq, "wk": wk, "wv": wv, "wo": wo,
            "wg1": wg1, "wg2_i": wg2,
            "nw_t": nw_t, "fb": fb, "c_t": c_t,
            "wmod_s": np.ascontiguousarray(wm[:, core * 1536:(core + 1) * 1536]),
            "ffnw_s": np.ascontiguousarray(fw[core * 256:(core + 1) * 256]),
            "cosx": cosx, "sinx": sinx, "dt_c": dt_c, "dq_c": dq_c,
            "dk_c": dk_c, "coef": coef, "idb": idb, "idf": idf,
            "gmask": gm,
        })

    nc = build_program()
    res = run_bass_kernel_spmd(nc, in_maps, core_ids=list(range(NCORES)))
    global _LAST_RES
    _LAST_RES = res.results
    out = np.concatenate([res.results[i]["out_s"] for i in range(NCORES)], axis=0)
    return out.reshape(1, NTOK, D).astype(np.float32)
